# version 03
# baseline (speedup 1.0000x reference)
import contextlib
import numpy as np
import ml_dtypes
import concourse.bass as bass
import concourse.mybir as mybir
from concourse.bass_utils import run_bass_kernel_spmd
from concourse.bass_core_rust import IndirectOffsetOnAxis

F32 = mybir.dt.float32
BF16 = mybir.dt.bfloat16
I16 = mybir.dt.int16
U16 = mybir.dt.uint16
AF = mybir.ActivationFunctionType
ALU = mybir.AluOpType
AX = mybir.AxisListType

NCORES = 8
B, L, H = 2, 8192, 1024
DI, NS = 512, 256
S = 2048
NT = S // 128
E, CAP, TOPK, G, TG = 64, 512, 8, 8, 4
MOE_SCALE = 2.5
EPC = E // NCORES
CH = 1024
NCH = L // CH
LNEPS = 1e-5

bfdt = ml_dtypes.bfloat16


class Sem:
    def __init__(self, h):
        self.h = h
        self.n = 0


class Prog:
    ENGINES = ("sync", "scalar", "vector", "tensor", "gpsimd")

    AUTO = ("scalar", "vector")

    def __init__(self):
        self.ops = {e: [] for e in self.ENGINES}
        self.last_inc = {e: None for e in self.ENGINES}
        self.chain = {}

    def do(self, eng, fn, auto=True):
        if auto and eng in self.AUTO and eng in self.chain:
            ch = self.chain[eng]
            prev = ch.n
            ch.n += 1

            def f(e, fn=fn, ch=ch, prev=prev):
                if prev > 0:
                    e.wait_ge(ch.h, prev)
                return fn(e).then_inc(ch.h, 1)
            f._wrapped = True
            self.ops[eng].append(f)
            self.last_inc[eng] = (ch, ch.n)
        else:
            self.ops[eng].append(fn)

    def dma(self, eng, out, in_, sem, slow=False):
        sem.n += 16

        def f(e, out=out, in_=in_, s=sem, slow=slow):
            if slow:
                e.dma_start(out=out, in_=in_, allow_slow_non_contiguous=True).then_inc(s.h, 16)
            else:
                e.dma_start(out=out, in_=in_).then_inc(s.h, 16)
        self.ops[eng].append(f)

    def cc(self, kind, op, ins, outs, sem):
        sem.n += 1

        def f(e, k=kind, op=op, ins=ins, outs=outs, s=sem):
            e.collective_compute(k, op, [list(range(NCORES))], ins, outs).then_inc(s.h, 1)
        self.ops["gpsimd"].append(f)

    def inc(self, eng, sem, n=1):
        ops = self.ops[eng]
        if eng in self.AUTO and eng in self.chain:
            if sem is self.chain[eng]:
                return
            ch = self.chain[eng]
            v = ch.n
            sem.n += n

            def f0(e, ch=ch, v=v, s=sem, n=n):
                if v > 0:
                    e.wait_ge(ch.h, v)
                return e.sem_inc(s.h, n)
            ops.append(f0)
            return
        sem.n += n
        if not ops:
            def fz(e, s=sem, n=n):
                return e.sem_inc(s.h, n)
            ops.append(fz)
            return
        last = ops[-1]
        if getattr(last, "_wrapped", False):
            li = self.last_inc.get(eng)
            lv = li[1] if li else 0
            lsem = li[0] if li else None

            def f1(e, s=sem, n=n, lsem=lsem, lv=lv):
                if lsem is not None and lv > 0:
                    e.wait_ge(lsem.h, lv)
                return e.sem_inc(s.h, n)
            ops.append(f1)
            return

        def f(e, last=last, s=sem, n=n):
            r = last(e)
            try:
                return r.then_inc(s.h, n)
            except Exception:
                return e.sem_inc(s.h, n)
        f._wrapped = True
        ops[-1] = f
        self.last_inc[eng] = (sem, sem.n)

    def wait(self, eng, sem, val=None):
        v = sem.n if val is None else val

        def f(e, s=sem, v=v):
            e.wait_ge(s.h, v)
        self.ops[eng].append(f)

    def fullbar(self, sem):
        for e in self.ENGINES:
            ops = self.ops[e]
            if ops and not getattr(ops[-1], "_wrapped", False):
                self.inc(e, sem)
            else:
                li = self.last_inc[e]
                if li is not None:
                    self.wait(e, li[0], li[1])
                sem.n += 1

                def f0(e2, s=sem):
                    return e2.sem_inc(s.h, 1)
                ops.append(f0)
        for e in self.ENGINES:
            self.wait(e, sem)



def _bctt(e, out, in0, in1, op):
    a, b = bass.broadcast_tensor_aps(in0, in1)
    return e.tensor_tensor(out=out, in0=a, in1=b, op=op)


def build_nc(core, taps=(), upto="all"):
    rank = core
    batch = core // 4
    dblk = core % 4
    batch_start = (core % 4 == 0)

    nc = bass.Bass("TRN2", target_bir_lowering=False, detect_race_conditions=False)

    def din(name, shape, dt=F32):
        return nc.dram_tensor(name, shape, dt, kind="ExternalInput").ap()

    def dout(name, shape, dt=F32):
        return nc.dram_tensor(name, shape, dt, kind="ExternalOutput").ap()

    def dint(name, shape, dt=F32):
        return nc.dram_tensor(name, shape, dt, kind="Internal").ap()

    x_in = din("x_slice", [S, H])
    x_halo = din("x_halo", [2, H])
    ln_g = din("ln_gamma", [1, H])
    ln_b = din("ln_beta", [1, H])
    w_up = din("w_up", [H, H])
    b_up = din("b_up", [1, H])
    conv_w = din("conv_w", [DI, 3])
    conv_b = din("conv_b", [DI, 1])
    w_delta = din("w_delta", [DI, DI])
    b_delta = din("b_delta", [DI, 1])
    w_B = din("w_B", [DI, NS])
    w_C = din("w_C", [DI, NS])
    a_log = din("a_log_blk", [128, NS])
    d_vec = din("d_vec", [DI, 1])
    w_down = din("w_down", [DI, H])
    b_down = din("b_down", [1, H])
    router_w = din("router_w", [H, E])
    gate_w = din("gate_w_blk", [EPC * H, DI])
    up_w = din("up_w_blk", [EPC * H, DI])
    down_w = din("down_w_blk", [EPC * DI, H])
    sh_g = din("sh_gate", [H, DI])
    sh_u = din("sh_up", [H, DI])
    sh_d = din("sh_down", [DI, H])
    halom = din("halo_mask", [128, 1])
    rankmask_in = din("rank_mask", [NCORES, 1])
    tok0_in = din("tok0", [128, 1])
    I32 = mybir.dt.int32
    gidx_f32 = din("gidx_f32", [128, 8], I32)
    gidx_bf = din("gidx_bf", [128, 16], I32)
    gidx_ys = din("gidx_ys", [128, 4], I32)
    iota128 = din("iota128", [128, 1])
    ebaset_in = din("ebase", [128, 8])
    ident = din("c_ident", [128, 128], BF16)
    strip = din("c_strip", [128, 255], BF16)
    tri = din("c_tri", [128, 128], BF16)
    onesb = din("c_ones", [1, 128], F32)
    iotae = din("c_iotae", [128, E])
    bmask = din("c_bmask", [128, 16], BF16)

    out = dout("out", [S, H])

    pre_f32 = dint("pre_f32", [2 * DI, S])
    ag_f32 = dint("ag_f32", [NCORES * 2 * DI, S])
    pre_bf = dint("pre_bf", [2 * NS, S], BF16)
    ag_bf = dint("ag_bf", [NCORES * 2 * NS, S], BF16)
    uT_scr = dint("uT_scr", [DI, S])
    srT_scr = dint("srT_scr", [DI, S])
    ysT_loc = dint("ysT_loc", [128, L])
    ag_ys = dint("ag_ys", [NCORES * 128, L])
    x1_scr = dint("x1_scr", [S, H])
    t1_loc = dint("t1_loc", [S, H], BF16)
    ag_t1 = dint("ag_t1", [B * L, H], BF16)
    cnt_loc = dint("cnt_loc", [1, E])
    ag_cnt = dint("ag_cnt", [NCORES, E])
    rec_islot = dint("rec_islot", [1, S * TOPK])
    rec_slotc = dint("rec_slotc", [1, S * TOPK])
    rec_wv = dint("rec_wv", [1, S * TOPK])
    rec_pairs = dint("rec_pairs", [S * TOPK, 2])
    T_loc = dint("T_loc", [E * CAP + 256, 2])
    T_glob = dint("T_glob", [E * CAP + 256, 2])
    T_all = dint("T_all", [NCORES * (E * CAP + 256), 2])
    eo_loc = dint("eo_loc", [EPC * CAP, H], BF16)
    ag_eo = dint("ag_eo", [E * CAP, H], BF16)

    tap_aps = {}
    for tname, tshape, tdt in taps:
        tap_aps[tname] = dout("tap_" + tname, tshape, tdt)

    P = Prog()
    stack = contextlib.ExitStack()
    _cms = {}

    def sbuf(name, shape, dt=F32):
        cm = nc.sbuf_tensor(name, shape, dt)
        t = cm.__enter__()
        _cms[name] = cm
        return t

    def sbfree(*names):
        for n in names:
            _cms.pop(n).__exit__(None, None, None)

    def psum(name, shape, dt=F32):
        return stack.enter_context(nc.psum_tensor(name, shape, dt))

    def mksem(name):
        return Sem(stack.enter_context(nc.semaphore(name)))

    tkA = sbuf("tkA", [128, H])
    tkB = sbuf("tkB", [128, H])
    tkC = sbuf("tkC", [128, H])
    sc1 = sbuf("sc1", [128, 1])
    sc2 = sbuf("sc2", [128, 1])
    sc3 = sbuf("sc3", [128, 1])
    sc4 = sbuf("sc4", [128, 1])
    sqsc = sbuf("sqsc", [128, H])
    big0 = sbuf("big0", [128, 4 * S])
    big1 = sbuf("big1", [128, 4 * S])
    wdl = sbuf("wdl", [128, 4 * DI], BF16)
    wBc = sbuf("wBc", [128, 4 * NS], BF16)
    wCc = sbuf("wCc", [128, 4 * NS], BF16)
    convw = sbuf("convw", [128, 12])
    convbc = sbuf("convbc", [128, 4])
    bupc = sbuf("bupc", [128, 8])
    bdlc = sbuf("bdlc", [128, 4])
    bdnc = sbuf("bdnc", [128, 8])
    dvecc = sbuf("dvecc", [128, 4])
    identt = sbuf("identt", [128, 128], BF16)
    identf = sbuf("identf", [128, 128])
    stript = sbuf("stript", [128, 255], BF16)
    trit = sbuf("trit", [128, 128], BF16)
    onest = sbuf("onest", [1, 128])
    iotaet = sbuf("iotaet", [128, E])
    bmaskt = sbuf("bmaskt", [128, 16], BF16)
    halomt = sbuf("halomt", [128, 1])
    epst = sbuf("epst", [128, 1])
    onec = sbuf("onec", [128, 1])
    gixf = sbuf("gixf", [128, 8], I32)
    gixb = sbuf("gixb", [128, 16], I32)
    gixy = sbuf("gixy", [128, 4], I32)
    iot128 = sbuf("iot128", [128, 1])
    ebaset = sbuf("ebaset", [128, 8])
    halT = sbuf("halT", [128, 16], BF16)

    PS0 = psum("PS0", [128, 1024])
    PS1 = psum("PS1", [128, 1024])
    PS2 = psum("PS2", [128, 1024])
    PS3 = psum("PS3", [128, 1024])

    sLD = mksem("sLD")
    sW = mksem("sW")
    sV = mksem("sV")
    sA = mksem("sA")
    sT = mksem("sT")
    sG = mksem("sG")
    sD = mksem("sD")
    sCC = mksem("sCC")
    sOUT = mksem("sOUT")
    sGA = mksem("sGA")
    sFIN = mksem("sFIN")
    P.chain = {"vector": sV, "scalar": sA}

    def ln_stats(src_ap):
        P.inc("vector", sV)
        P.wait("scalar", sV)
        P.do("vector", lambda e, a=src_ap: e.tensor_reduce(
            out=sc3[:, 0:1], in_=a, axis=AX.X, op=ALU.add))
        P.do("scalar", lambda e, a=src_ap: e.activation(
            out=sqsc[:], in_=a, func=AF.Square))
        P.inc("scalar", sA)
        P.wait("vector", sA)
        P.do("vector", lambda e: e.tensor_reduce(
            out=sc4[:, 0:1], in_=sqsc[:], axis=AX.X, op=ALU.add))
        P.do("vector", lambda e: e.tensor_scalar(
            out=sc3[:, 0:1], in0=sc3[:, 0:1], scalar1=1.0 / H, scalar2=None, op0=ALU.mult))
        P.do("vector", lambda e: e.tensor_scalar(
            out=sc4[:, 0:1], in0=sc4[:, 0:1], scalar1=1.0 / H, scalar2=None, op0=ALU.mult))
        P.do("vector", lambda e: e.tensor_tensor(
            out=sc1[:, 0:1], in0=sc3[:, 0:1], in1=sc3[:, 0:1], op=ALU.mult))
        P.do("vector", lambda e: e.tensor_tensor(
            out=sc4[:, 0:1], in0=sc4[:, 0:1], in1=sc1[:, 0:1], op=ALU.subtract))
        P.inc("vector", sV)
        P.wait("scalar", sV)
        P.do("scalar", lambda e: e.activation(
            out=sc1[:, 0:1], in_=sc4[:, 0:1], func=AF.Sqrt, bias=epst[:, 0:1]))
        P.inc("scalar", sA)
        P.wait("vector", sA)
        P.do("vector", lambda e: e.reciprocal(out=sc2[:, 0:1], in_=sc1[:, 0:1]))
        P.inc("vector", sV)
        P.wait("vector", sV)

    def ln_apply(src_ap, dst_ap):
        P.do("vector", lambda e, a=src_ap, d=dst_ap: e.tensor_scalar(
            out=d, in0=a, scalar1=sc3[:, 0:1], scalar2=sc2[:, 0:1],
            op0=ALU.subtract, op1=ALU.mult))

    P.dma("sync", identt[:], ident, sW)
    P.dma("sync", stript[:], strip, sW)
    P.dma("sync", trit[:], tri, sW)
    P.dma("sync", onest[:], onesb, sW)
    P.dma("sync", iotaet[:], iotae, sW)
    P.dma("sync", bmaskt[:], bmask, sW)
    P.dma("sync", halomt[:], halom, sW)
    P.dma("sync", gixf[:], gidx_f32, sW)
    P.dma("sync", gixb[:], gidx_bf, sW)
    P.dma("sync", gixy[:], gidx_ys, sW)
    P.dma("sync", iot128[:], iota128, sW)
    P.dma("sync", ebaset[:], ebaset_in, sW)
    P.dma("sync", convw[:].rearrange("p (c k) -> p c k", k=3), conv_w.rearrange("(c p) k -> p c k", p=128), sW, slow=True)
    P.dma("sync", convbc[:].rearrange("p (c o) -> p c o", o=1), conv_b.rearrange("(c p) o -> p c o", p=128), sW, slow=True)
    P.dma("sync", bupc[:].rearrange("p (o c) -> p o c", o=1), b_up.rearrange("o (c p) -> p o c", p=128), sW, slow=True)
    P.dma("sync", bdlc[:].rearrange("p (c o) -> p c o", o=1), b_delta.rearrange("(c p) o -> p c o", p=128), sW, slow=True)
    P.dma("sync", bdnc[:].rearrange("p (o c) -> p o c", o=1), b_down.rearrange("o (c p) -> p o c", p=128), sW, slow=True)
    P.dma("sync", dvecc[:].rearrange("p (c o) -> p c o", o=1), d_vec.rearrange("(c p) o -> p c o", p=128), sW, slow=True)
    def wconv(dst_tile, dst_off, src_ap, cols):
        P.wait("sync", sA)
        P.wait("sync", sV)
        P.dma("sync", tkA[:, 0:cols], src_ap, sLD)
        P.wait("scalar", sLD)
        P.wait("scalar", sT)
        P.do("scalar", lambda e, d=dst_tile, o=dst_off, c=cols: e.copy(
            out=d[:, o:o + c], in_=tkA[:, 0:c]))
        P.inc("scalar", sA)

    for c4 in range(4):
        wconv(wdl, c4 * DI, w_delta[c4 * 128:(c4 + 1) * 128, :], DI)
        wconv(wBc, c4 * NS, w_B[c4 * 128:(c4 + 1) * 128, :], NS)
        wconv(wCc, c4 * NS, w_C[c4 * 128:(c4 + 1) * 128, :], NS)

    wupT = sbuf("wupT", [128, 8 * H], BF16)
    xaT = sbuf("xaT", [128, 8 * S], BF16)
    gamt = sbuf("gamt", [128, H])
    bett = sbuf("bett", [128, H])
    P.dma("sync", gamt[0:1, :], ln_g, sW)
    P.dma("sync", bett[0:1, :], ln_b, sW)
    P.wait("scalar", sW)
    P.do("scalar", lambda e: e.copy(out=identf[:], in_=identt[:]))
    P.inc("scalar", sA)
    P.do("vector", lambda e: e.memset(epst[:], LNEPS))
    P.do("vector", lambda e: e.memset(onec[:], 1.0))
    P.inc("vector", sV)
    for c4 in range(8):
        wconv(wupT, c4 * H, w_up[c4 * 128:(c4 + 1) * 128, :], H)
    P.wait("tensor", sW)
    for half in range(2):
        P.do("tensor", lambda e, h=half: e.matmul(
            PS0[:, h * 512:(h + 1) * 512], onest[0:1, :],
            gamt[0:1, h * 512:(h + 1) * 512], start=True, stop=True))
        P.do("tensor", lambda e, h=half: e.matmul(
            PS1[:, h * 512:(h + 1) * 512], onest[0:1, :],
            bett[0:1, h * 512:(h + 1) * 512], start=True, stop=True))
    P.inc("tensor", sT)
    P.wait("scalar", sT)
    P.do("scalar", lambda e: e.copy(out=gamt[:], in_=PS0[:]))
    P.do("scalar", lambda e: e.copy(out=bett[:], in_=PS1[:]))
    P.inc("scalar", sA)
    P.wait("tensor", sA)
    P.wait("vector", sA)

    def tileproc(g, src_rows, halo=False):
        P.wait("sync", sV)
        P.wait("sync", sA)
        P.dma("sync", tkA[0:2, :] if halo else tkA[:], src_rows, sLD)
        P.wait("vector", sLD)
        P.wait("scalar", sLD)
        ln_stats(tkA[:])
        ln_apply(tkA[:], tkB[:])
        ln_stats(tkB[:])
        ln_apply(tkB[:], tkC[:])
        P.do("vector", lambda e: e.tensor_tensor(out=tkC[:], in0=tkC[:], in1=gamt[:], op=ALU.mult))
        P.do("vector", lambda e: e.tensor_tensor(out=tkC[:], in0=tkC[:], in1=bett[:], op=ALU.add))
        P.inc("vector", sV)
        if g == 0 and not halo and "t0tile" in tap_aps:
            P.wait("gpsimd", sV)
            P.dma("gpsimd", tap_aps["t0tile"], tkB[:], sD)
            P.dma("gpsimd", tap_aps["xatile"], tkC[:], sD)
            P.dma("gpsimd", tap_aps["gamtile"], gamt[:], sD)
            P.wait("tensor", sD)
        P.wait("tensor", sV)
        for hc in range(8):
            pslot = PS2 if hc % 2 == 0 else PS3
            P.do("tensor", lambda e, hc=hc, p=pslot: e.transpose(
                p[:, 0:128], tkC[:, hc * 128:(hc + 1) * 128], identf[:]))
            P.inc("tensor", sT)
            P.wait("scalar", sT)
            if halo:
                P.do("scalar", lambda e, hc=hc, p=pslot: e.copy(
                    out=halT[:, hc * 2:hc * 2 + 2], in_=p[:, 0:2]))
            else:
                P.do("scalar", lambda e, hc=hc, p=pslot, g=g: e.copy(
                    out=xaT[:, hc * S + g * 128: hc * S + (g + 1) * 128], in_=p[:, 0:128]))
            P.inc("scalar", sA)
            P.wait("tensor", sA)

    for g in range(NT):
        tileproc(g, x_in[g * 128:(g + 1) * 128, :])
    P.wait("vector", sA)
    P.do("vector", lambda e: e.memset(tkA[:], 0.0))
    P.inc("vector", sV)
    tileproc(0, x_halo[0:2, :], halo=True)
    P.wait("vector", sA)
    P.do("vector", lambda e: e.tensor_scalar(
        out=halT[:], in0=halT[:], scalar1=halomt[:, 0:1], scalar2=None, op0=ALU.mult))
    P.inc("vector", sV)

    P.fullbar(sOUT)
    sbfree("bett", "gamt")
    leftT = sbuf("leftT", [128, 4 * (S + 8)])
    big3 = sbuf("big3", [128, 4 * S])
    P.do("vector", lambda e: e.memset(leftT[:, 0:2 * (S + 8)], 0.0))
    P.do("vector", lambda e: e.memset(leftT[:, 2 * (S + 8):], 0.0))
    P.inc("vector", sV)
    P.wait("tensor", sV)
    P.wait("tensor", sA)
    for oc in range(8):
        for th in range(2):
            pslot = PS0 if th == 0 else PS1
            for kc in range(8):
                for q in range(2):
                    P.do("tensor", lambda e, oc=oc, th=th, kc=kc, q=q, p=pslot: e.matmul(
                        p[:, q * 512:(q + 1) * 512],
                        wupT[:, kc * H + oc * 128: kc * H + (oc + 1) * 128],
                        xaT[:, kc * S + th * 1024 + q * 512: kc * S + th * 1024 + (q + 1) * 512],
                        start=(kc == 0), stop=(kc == 7)))
            P.inc("tensor", sT)
            P.wait("scalar", sT)
            if oc < 4:
                P.do("scalar", lambda e, oc=oc, th=th, p=pslot: e.activation(
                    out=leftT[:, oc * (S + 8) + 4 + th * 1024: oc * (S + 8) + 4 + (th + 1) * 1024],
                    in_=p[:], func=AF.Identity, bias=bupc[:, oc:oc + 1]))
            else:
                P.do("scalar", lambda e, oc=oc, th=th, p=pslot: e.activation(
                    out=big3[:, (oc - 4) * S + th * 1024: (oc - 4) * S + (th + 1) * 1024],
                    in_=p[:], func=AF.Silu, bias=bupc[:, oc:oc + 1]))
            P.inc("scalar", sA)
            P.wait("tensor", sA)
    if True:
        for oc in range(4):
            for kc in range(8):
                P.do("tensor", lambda e, oc=oc, kc=kc: e.matmul(
                    PS0[:, 0:2],
                    wupT[:, kc * H + oc * 128: kc * H + (oc + 1) * 128],
                    halT[:, kc * 2:(kc + 1) * 2],
                    start=(kc == 0), stop=(kc == 7)))
            P.inc("tensor", sT)
            P.wait("scalar", sT)
            P.do("scalar", lambda e, oc=oc: e.activation(
                out=leftT[:, oc * (S + 8) + 2: oc * (S + 8) + 4],
                in_=PS0[:, 0:2], func=AF.Identity, bias=bupc[:, oc:oc + 1]))
            P.inc("scalar", sA)
            P.wait("tensor", sA)
            P.wait("vector", sA)
            P.do("vector", lambda e, oc=oc: e.tensor_scalar(
                out=leftT[:, oc * (S + 8) + 2: oc * (S + 8) + 4],
                in0=leftT[:, oc * (S + 8) + 2: oc * (S + 8) + 4],
                scalar1=halomt[:, 0:1], scalar2=None, op0=ALU.mult))
            P.inc("vector", sV)
            P.wait("scalar", sV)
    P.wait("gpsimd", sA)
    P.dma("gpsimd", srT_scr.rearrange("(c p) t -> p c t", p=128), big3[:].rearrange("p (c t) -> p c t", t=S), sD)
    if "xaT" in tap_aps:
        P.dma("gpsimd", tap_aps["xaT"], xaT[:], sD)
    if "leftT" in tap_aps:
        P.dma("gpsimd", tap_aps["leftT"], leftT[:], sD)

    P.wait("vector", sA)
    for dc in range(4):
        b0 = dc * (S + 8)
        sl = slice(dc * S, (dc + 1) * S)
        P.do("vector", lambda e, b=b0, sl=sl, dc=dc: e.tensor_scalar(
            out=big0[:, sl], in0=leftT[:, b + 2:b + 2 + S],
            scalar1=convw[:, dc * 3 + 0: dc * 3 + 1], scalar2=None, op0=ALU.mult))
        P.do("vector", lambda e, b=b0, sl=sl, dc=dc: e.scalar_tensor_tensor(
            out=big0[:, sl], in0=leftT[:, b + 3:b + 3 + S],
            scalar=convw[:, dc * 3 + 1: dc * 3 + 2], in1=big0[:, sl],
            op0=ALU.mult, op1=ALU.add))
        P.do("vector", lambda e, b=b0, sl=sl, dc=dc: e.scalar_tensor_tensor(
            out=big0[:, sl], in0=leftT[:, b + 4:b + 4 + S],
            scalar=convw[:, dc * 3 + 2: dc * 3 + 3], in1=big0[:, sl],
            op0=ALU.mult, op1=ALU.add))
    P.inc("vector", sV)
    P.wait("scalar", sV)
    for dc in range(4):
        sl = slice(dc * S, (dc + 1) * S)
        P.do("scalar", lambda e, sl=sl, dc=dc: e.activation(
            out=big0[:, sl], in_=big0[:, sl], func=AF.Silu, bias=convbc[:, dc:dc + 1]))
    P.inc("scalar", sA)
    P.wait("gpsimd", sA)
    P.dma("gpsimd", uT_scr.rearrange("(c p) t -> p c t", p=128), big0[:].rearrange("p (c t) -> p c t", t=S), sD)

    P.fullbar(sOUT)
    sbfree("big3", "leftT", "xaT", "wupT")
    uTb = sbuf("uTb", [128, 4 * S], BF16)
    big2 = sbuf("big2", [128, 4 * S])
    bmstage = sbuf("bmstage", [128, 4 * S], BF16)
    P.wait("vector", sA)
    P.do("vector", lambda e: e.tensor_copy(out=uTb[:], in_=big0[:]))
    P.inc("vector", sV)
    P.wait("tensor", sV)
    for dc in range(4):
        for th in range(2):
            pslot = PS0 if th == 0 else PS1
            for kc in range(4):
                for q in range(2):
                    P.do("tensor", lambda e, dc=dc, th=th, kc=kc, q=q, p=pslot: e.matmul(
                        p[:, q * 512:(q + 1) * 512],
                        wdl[:, kc * DI + dc * 128: kc * DI + (dc + 1) * 128],
                        uTb[:, kc * S + th * 1024 + q * 512: kc * S + th * 1024 + (q + 1) * 512],
                        start=(kc == 0), stop=(kc == 3)))
            P.inc("tensor", sT)
            P.wait("scalar", sT)
            P.do("scalar", lambda e, dc=dc, th=th, p=pslot: e.activation(
                out=big1[:, dc * S + th * 1024: dc * S + (th + 1) * 1024],
                in_=p[:], func=AF.Identity, bias=bdlc[:, dc:dc + 1]))
            P.inc("scalar", sA)
            P.wait("tensor", sA)
    P.wait("vector", sA)
    for dc in range(4):
        sl = slice(dc * S, (dc + 1) * S)
        P.do("vector", lambda e, sl=sl: e.tensor_scalar(
            out=big2[:, sl], in0=big1[:, sl], scalar1=-1.0, scalar2=None, op0=ALU.mult))
        P.do("vector", lambda e, sl=sl: e.tensor_tensor(
            out=big2[:, sl], in0=big1[:, sl], in1=big2[:, sl], op=ALU.min))
        P.inc("vector", sV)
        P.wait("scalar", sV)
        P.do("scalar", lambda e, sl=sl: e.activation(
            out=big2[:, sl], in_=big2[:, sl], func=AF.Exp))
        P.do("scalar", lambda e, sl=sl: e.activation(
            out=big2[:, sl], in_=big2[:, sl], func=AF.Ln, bias=onec[:, 0:1]))
        P.inc("scalar", sA)
        P.wait("vector", sA)
        P.do("vector", lambda e, sl=sl: e.tensor_relu(out=big1[:, sl], in_=big1[:, sl]))
        P.do("vector", lambda e, sl=sl: e.tensor_tensor(
            out=big1[:, sl], in0=big1[:, sl], in1=big2[:, sl], op=ALU.add))
        P.do("vector", lambda e, sl=sl: e.tensor_tensor(
            out=big2[:, sl], in0=big1[:, sl], in1=big0[:, sl], op=ALU.mult))
    P.inc("vector", sV)
    P.wait("gpsimd", sV)
    P.dma("gpsimd", pre_f32[0:DI, :].rearrange("(c p) t -> p c t", p=128), big1[:].rearrange("p (c t) -> p c t", t=S), sD)
    P.dma("gpsimd", pre_f32[DI:2 * DI, :].rearrange("(c p) t -> p c t", p=128), big2[:].rearrange("p (c t) -> p c t", t=S), sD)
    P.wait("tensor", sV)
    for which in range(2):
        wtile = wBc if which == 0 else wCc
        for nch in range(2):
            for th in range(2):
                pslot = PS0 if th == 0 else PS1
                for kc in range(4):
                    for q in range(2):
                        P.do("tensor", lambda e, w=wtile, ncb=nch, th=th, kc=kc, q=q, p=pslot: e.matmul(
                            p[:, q * 512:(q + 1) * 512],
                            w[:, kc * NS + ncb * 128: kc * NS + (ncb + 1) * 128],
                            uTb[:, kc * S + th * 1024 + q * 512: kc * S + th * 1024 + (q + 1) * 512],
                            start=(kc == 0), stop=(kc == 3)))
                P.inc("tensor", sT)
                P.wait("scalar", sT)
                P.do("scalar", lambda e, w=which, ncb=nch, th=th, p=pslot: e.copy(
                    out=bmstage[:, (w * 2 + ncb) * S + th * 1024: (w * 2 + ncb) * S + (th + 1) * 1024],
                    in_=p[:]))
                P.inc("scalar", sA)
                P.wait("tensor", sA)
    P.wait("gpsimd", sA)
    P.dma("gpsimd", pre_bf.rearrange("(c p) t -> p c t", p=128), bmstage[:].rearrange("p (c t) -> p c t", t=S), sD)
    P.wait("gpsimd", sD)
    P.cc("AllGather", ALU.bypass, [pre_f32.opt()], [ag_f32.opt()], sCC)
    P.wait("gpsimd", sCC)
    P.cc("AllGather", ALU.bypass, [pre_bf.opt()], [ag_bf.opt()], sCC)
    P.wait("gpsimd", sCC)
    P.inc("gpsimd", sG)

    if "preT" in tap_aps:
        P.dma("gpsimd", tap_aps["preT"], pre_f32, sD)
    if "uT" in tap_aps:
        P.dma("gpsimd", tap_aps["uT"], uT_scr, sD)

    if upto == "pre":
        P.wait("gpsimd", sD)
        stack.close()
        _emit(nc, P)
        return nc

    P.fullbar(sOUT)
    sbfree("bmstage", "big2", "uTb")
    dltT = big0
    duT = big1
    bmT = sbuf("bmT", [128, 2 * L], BF16)
    cmT = sbuf("cmT", [128, 2 * L], BF16)
    a_nT = sbuf("a_nT", [128, 2 * 128])
    hbuf = sbuf("hbuf", [128, 2 * 1024])
    hcar = sbuf("hcar", [128, 2 * 128])
    dAt = sbuf("dAt", [128, 4 * 1024])
    trm = sbuf("trm", [128, 2 * 1024])
    ymt = sbuf("ymt", [128, 2 * 1024], BF16)
    ysst = sbuf("ysst", [128, 1024])
    onesf = sbuf("onesf", [128, 128])
    wbc = sbuf("wbc", [128, 2 * 128])

    def igather(eng, dst, table, idxcol, sem):
        sem.n += 16

        def gf(e, dst=dst, table=table, idxcol=idxcol, s=sem):
            return e.indirect_dma_start(
                out=dst, out_offset=None, in_=table,
                in_offset=IndirectOffsetOnAxis(ap=idxcol, axis=0),
            ).then_inc(s.h, 16)
        P.do(eng, gf)

    gi = 0
    for rr in range(4):
        for arr, dst in ((0, dltT[:, rr * S:(rr + 1) * S]), (1, duT[:, rr * S:(rr + 1) * S])):
            igather("gpsimd", dst, ag_f32, gixf[:, gi:gi + 1], sGA)
            gi += 1
    gi = 0
    for rr in range(4):
        for tau in range(2):
            for arr, dst in ((0, bmT[:, tau * L + rr * S: tau * L + (rr + 1) * S]),
                             (1, cmT[:, tau * L + rr * S: tau * L + (rr + 1) * S])):
                igather("gpsimd", dst, ag_bf, gixb[:, gi:gi + 1], sGA)
                gi += 1
    P.do("vector", lambda e: e.memset(onesf[:], 1.0))
    P.inc("vector", sV)
    P.wait("scalar", sV)
    P.dma("sync", tkA[:, 0:NS], a_log, sLD)
    P.wait("scalar", sLD)
    P.do("scalar", lambda e: e.activation(out=tkA[:, 0:NS], in_=tkA[:, 0:NS], func=AF.Exp))
    P.inc("scalar", sA)
    P.wait("vector", sA)
    P.do("vector", lambda e: e.tensor_scalar(
        out=tkA[:, 0:NS], in0=tkA[:, 0:NS], scalar1=-1.0, scalar2=None, op0=ALU.mult))
    P.do("vector", lambda e: e.memset(hcar[:], 0.0))
    P.inc("vector", sV)
    P.wait("tensor", sV)
    for tau in range(2):
        P.do("tensor", lambda e, tau=tau: e.transpose(
            PS2[:, 0:128], tkA[:, tau * 128:(tau + 1) * 128], identf[:]))
        P.inc("tensor", sT)
        P.wait("scalar", sT)
        P.do("scalar", lambda e, tau=tau: e.copy(
            out=a_nT[:, tau * 128:(tau + 1) * 128], in_=PS2[:, 0:128]))
        P.inc("scalar", sA)
        P.wait("tensor", sA)
    P.wait("vector", sA)
    P.wait("vector", sLD)
    P.wait("vector", sGA)
    P.wait("tensor", sLD)
    P.wait("tensor", sGA)
    P.wait("scalar", sLD)
    P.wait("scalar", sGA)

    redmark = [0]
    for c in range(NCH):
        for i in range(128):
            first = (c == 0 and i == 0)
            par = i % 2
            P.do("scalar", lambda e, i=i, par=par: e.activation(
                out=wbc[:, par * 128:(par + 1) * 128], in_=onesf[:],
                func=AF.Copy, scale=identf[:, i:i + 1]))
            P.inc("scalar", sA)
            if not first:
                P.wait("tensor", sV)
            P.wait("tensor", sA)
            for q in range(2):
                P.do("tensor", lambda e, i=i, c=c, q=q, par=par: e.matmul(
                    PS0[:, q * 512:(q + 1) * 512], wbc[:, par * 128:(par + 1) * 128],
                    dltT[:, c * CH + q * 512: c * CH + (q + 1) * 512],
                    start=True, stop=True))
            for q in range(2):
                P.do("tensor", lambda e, i=i, c=c, q=q, par=par: e.matmul(
                    PS1[:, q * 512:(q + 1) * 512], wbc[:, par * 128:(par + 1) * 128],
                    duT[:, c * CH + q * 512: c * CH + (q + 1) * 512],
                    start=True, stop=True))
            P.inc("tensor", sT)
            P.wait("scalar", sT)
            for tau in range(2):
                P.do("scalar", lambda e, tau=tau, i=i, par=par: e.activation(
                    out=dAt[:, (par * 2 + tau) * 1024:(par * 2 + tau + 1) * 1024], in_=PS0[:],
                    func=AF.Exp, scale=a_nT[:, tau * 128 + i: tau * 128 + i + 1]))
            P.inc("scalar", sA)
            P.wait("vector", sT)
            for tau in range(2):
                P.do("vector", lambda e, tau=tau, c=c: e.tensor_tensor(
                    out=trm[:, tau * 1024:(tau + 1) * 1024],
                    in0=bmT[:, tau * L + c * CH: tau * L + (c + 1) * CH],
                    in1=PS1[:], op=ALU.mult))
            P.inc("vector", sV)
            P.wait("vector", sA)
            if not first:
                P.wait("vector", sT, redmark[0])
            for tau in range(2):
                P.do("vector", lambda e, tau=tau, i=i, par=par: e.tensor_tensor_scan(
                    out=hbuf[:, tau * 1024:(tau + 1) * 1024],
                    data0=dAt[:, (par * 2 + tau) * 1024:(par * 2 + tau + 1) * 1024],
                    data1=trm[:, tau * 1024:(tau + 1) * 1024],
                    initial=hcar[:, tau * 128 + i: tau * 128 + i + 1],
                    op0=ALU.mult, op1=ALU.add))
                P.do("vector", lambda e, tau=tau, i=i: e.tensor_copy(
                    out=hcar[:, tau * 128 + i: tau * 128 + i + 1],
                    in_=hbuf[:, tau * 1024 + 1023: tau * 1024 + 1024]))
                P.do("vector", lambda e, tau=tau, c=c: e.tensor_tensor(
                    out=ymt[:, tau * 1024:(tau + 1) * 1024],
                    in0=hbuf[:, tau * 1024:(tau + 1) * 1024],
                    in1=cmT[:, tau * L + c * CH: tau * L + (c + 1) * CH], op=ALU.mult))
                P.inc("vector", sV)
                P.wait("tensor", sV)
                for q in range(2):
                    P.do("tensor", lambda e, i=i, tau=tau, q=q: e.matmul(
                        PS2[:, q * 512:(q + 1) * 512],
                        stript[:, 127 - i:255 - i],
                        ymt[:, tau * 1024 + q * 512: tau * 1024 + (q + 1) * 512],
                        start=(i == 0 and tau == 0), stop=(i == 127 and tau == 1)))
            P.inc("tensor", sT)
            redmark[0] = sT.n
        P.wait("scalar", sT)
        P.do("scalar", lambda e: e.copy(out=ysst[:], in_=PS2[:]))
        P.inc("scalar", sA)
        P.wait("gpsimd", sA)
        P.dma("gpsimd", ysT_loc[:, c * CH:(c + 1) * CH], ysst[:], sD)
        P.wait("gpsimd", sD)
        P.inc("gpsimd", sG)
        P.wait("tensor", sA)

    P.cc("AllGather", ALU.bypass, [ysT_loc.opt()], [ag_ys.opt()], sCC)
    P.wait("gpsimd", sCC)
    if "ysT" in tap_aps:
        P.dma("gpsimd", tap_aps["ysT"], ysT_loc, sD)

    if upto == "scan":
        P.wait("gpsimd", sD)
        stack.close()
        _emit(nc, P)
        return nc

    P.fullbar(sOUT)
    sbfree("wbc", "onesf", "ysst", "ymt", "trm", "dAt", "hcar", "hbuf",
           "a_nT", "cmT", "bmT")
    big2s = sbuf("big2s", [128, 4 * S])
    wdn = sbuf("wdn", [128, 4 * H], BF16)
    bdnT = sbuf("bdnT", [128, H])
    gtb = sbuf("gtb", [128, 4 * S], BF16)
    t1Tg = sbuf("t1Tg", [128, H])
    rwf = sbuf("rwf", [128, 8 * E])
    posb = sbuf("posb", [128, 128])
    ixb = sbuf("ixb", [128, 128])
    wvb = sbuf("wvb", [128, 128])
    ohb = sbuf("ohb", [128, E], BF16)
    crow = sbuf("crow", [1, E])
    brow = sbuf("brow", [128, E])
    agcS = sbuf("agcS", [8, E])
    rkmS = sbuf("rkmS", [8, 1])
    ixt = sbuf("ixt", [128, 8], U16)
    t1bt = sbuf("t1bt", [128, H], BF16)
    ptile = sbuf("ptile", [128, 16])
    tokcol = sbuf("tokcol", [128, 1])
    tok0S = sbuf("tok0S", [128, 1])

    for rr in range(4):
        igather("gpsimd", big0[:, rr * S:(rr + 1) * S],
                ag_ys.rearrange("a (c t) -> (a c) t", t=S), gixy[:, rr:rr + 1], sGA)
    P.dma("sync", big1[:].rearrange("p (c t) -> p c t", t=S),
          uT_scr.rearrange("(c p) t -> p c t", p=128), sLD)
    P.dma("sync", big2s[:].rearrange("p (c t) -> p c t", t=S),
          srT_scr.rearrange("(c p) t -> p c t", p=128), sLD)
    P.dma("sync", rwf[:].rearrange("p (c e) -> p c e", e=E),
          router_w.rearrange("(c p) e -> p c e", p=128), sLD)
    P.dma("sync", rkmS[:], rankmask_in, sLD)
    P.dma("sync", tok0S[:], tok0_in, sLD)
    P.dma("sync", bdnT[0:1, :], b_down, sLD)
    for c4 in range(4):
        wconv(wdn, c4 * H, w_down[c4 * 128:(c4 + 1) * 128, :], H)
    P.wait("tensor", sLD)
    P.wait("tensor", sA)
    for half in range(2):
        P.do("tensor", lambda e, h=half: e.matmul(
            PS0[:, h * 512:(h + 1) * 512], onest[0:1, :],
            bdnT[0:1, h * 512:(h + 1) * 512], start=True, stop=True))
    P.inc("tensor", sT)
    P.wait("scalar", sT)
    P.do("scalar", lambda e: e.copy(out=bdnT[:], in_=PS0[:]))
    P.inc("scalar", sA)
    P.wait("tensor", sA)

    P.wait("vector", sLD)
    P.do("vector", lambda e: e.memset(crow[:], 0.0))
    P.inc("vector", sV)
    P.wait("vector", sGA)
    P.wait("vector", sLD)
    for dc in range(4):
        sl = slice(dc * S, (dc + 1) * S)
        P.do("vector", lambda e, sl=sl, dc=dc: e.scalar_tensor_tensor(
            out=big1[:, sl], in0=big1[:, sl], scalar=dvecc[:, dc:dc + 1],
            in1=big0[:, sl], op0=ALU.mult, op1=ALU.add))
        P.do("vector", lambda e, sl=sl: e.tensor_tensor(
            out=big0[:, sl], in0=big1[:, sl], in1=big2s[:, sl], op=ALU.mult))
        P.do("vector", lambda e, sl=sl: e.tensor_copy(out=gtb[:, sl], in_=big0[:, sl]))
    P.inc("vector", sV)

    if "gatedT" in tap_aps:
        P.wait("gpsimd", sV)
        P.dma("gpsimd", tap_aps["gatedT"], big0[:], sD)
        P.dma("gpsimd", tap_aps["ysl"], big1[:], sD)
    P.wait("tensor", sV)
    for g in range(NT):
        P.wait("sync", sV)
        P.wait("sync", sA)
        P.dma("sync", tkA[:], x_in[g * 128:(g + 1) * 128, :], sLD)
        if g > 0:
            P.wait("tensor", sV)
            P.wait("tensor", sA)
        for kc in range(4):
            for q in range(2):
                P.do("tensor", lambda e, g=g, kc=kc, q=q: e.matmul(
                    PS0[:, q * 512:(q + 1) * 512],
                    gtb[:, kc * S + g * 128: kc * S + (g + 1) * 128],
                    wdn[:, kc * H + q * 512: kc * H + (q + 1) * 512],
                    start=(kc == 0), stop=(kc == 3)))
        P.inc("tensor", sT)
        P.wait("vector", sT)
        P.do("vector", lambda e: e.scalar_tensor_tensor(
            out=tkC[:], in0=PS0[:], scalar=1.0, in1=bdnT[:], op0=ALU.mult, op1=ALU.add))
        P.wait("vector", sLD)
        P.wait("scalar", sLD)
        ln_stats(tkA[:])
        ln_apply(tkA[:], tkB[:])
        P.do("vector", lambda e: e.tensor_tensor(out=tkC[:], in0=tkC[:], in1=tkB[:], op=ALU.add))
        ln_stats(tkC[:])
        ln_apply(tkC[:], tkB[:])
        P.do("vector", lambda e: e.tensor_tensor(out=tkA[:], in0=tkA[:], in1=tkB[:], op=ALU.add))
        P.inc("vector", sV)
        P.wait("gpsimd", sV)
        P.dma("gpsimd", x1_scr[g * 128:(g + 1) * 128, :], tkA[:], sD)
        ln_stats(tkA[:])
        ln_apply(tkA[:], tkB[:])
        P.inc("vector", sV)
        P.wait("scalar", sV)
        P.do("scalar", lambda e: e.copy(out=t1bt[:], in_=tkB[:]))
        P.inc("scalar", sA)
        P.wait("gpsimd", sA)
        P.dma("gpsimd", t1_loc[g * 128:(g + 1) * 128, :], t1bt[:], sD)
        P.wait("tensor", sV)
        for hc in range(8):
            pslot = PS2 if hc % 2 == 0 else PS3
            P.do("tensor", lambda e, hc=hc, p=pslot: e.transpose(
                p[:, 0:128], tkB[:, hc * 128:(hc + 1) * 128], identf[:]))
            P.inc("tensor", sT)
            P.wait("scalar", sT)
            P.do("scalar", lambda e, hc=hc, p=pslot: e.copy(
                out=t1Tg[:, hc * 128:(hc + 1) * 128], in_=p[:, 0:128]))
            P.inc("scalar", sA)
            P.wait("tensor", sA)
        for hc in range(8):
            P.do("tensor", lambda e, hc=hc: e.matmul(
                PS1[:, 0:64], t1Tg[:, hc * 128:(hc + 1) * 128],
                rwf[:, hc * E:(hc + 1) * E], start=(hc == 0), stop=(hc == 7)))
        P.inc("tensor", sT)
        C = tkC
        P.wait("scalar", sT)
        P.do("scalar", lambda e: e.activation(out=C[:, 0:64], in_=PS1[:, 0:64], func=AF.Sigmoid))
        P.inc("scalar", sA)
        P.wait("vector", sA)
        P.do("vector", lambda e: e.tensor_reduce(
            out=C[:, 64:72], in_=C[:, 0:64].rearrange("p (g e) -> p g e", e=8),
            axis=AX.X, op=ALU.max))
        P.do("vector", lambda e: e.max(out=C[:, 72:80], in_=C[:, 64:72]))
        P.do("vector", lambda e: e.tensor_scalar(
            out=C[:, 80:88], in0=C[:, 64:72], scalar1=C[:, 75:76], scalar2=None, op0=ALU.is_ge))
        P.do("vector", lambda e: _bctt(e,
            C[:, 88:152].rearrange("p (g e) -> p g e", e=8),
            C[:, 0:64].rearrange("p (g e) -> p g e", e=8),
            C[:, 80:88].rearrange("p (g o) -> p g o", o=1),
            ALU.mult))
        P.do("vector", lambda e: e.max(out=C[:, 152:160], in_=C[:, 88:152]))
        P.do("vector", lambda e: e.max_index(out=ixt[:], in_max=C[:, 152:160], in_values=C[:, 88:152]))
        P.do("vector", lambda e: e.tensor_copy(out=C[:, 170:178], in_=ixt[:]))
        P.do("vector", lambda e: e.tensor_reduce(out=C[:, 160:161], in_=C[:, 152:160], axis=AX.X, op=ALU.add))
        P.do("vector", lambda e: e.tensor_scalar(
            out=C[:, 160:161], in0=C[:, 160:161], scalar1=1e-9, scalar2=None, op0=ALU.max))
        P.do("vector", lambda e: e.reciprocal(out=C[:, 161:162], in_=C[:, 160:161]))
        P.do("vector", lambda e: e.tensor_scalar(
            out=C[:, 162:170], in0=C[:, 152:160], scalar1=C[:, 161:162], scalar2=MOE_SCALE,
            op0=ALU.mult, op1=ALU.mult))
        B_ = tkB
        P.do("vector", lambda e: _bctt(e,
            B_[:, 0:512].rearrange("p (x k) -> p x k", k=8),
            C[:, 170:178].rearrange("p (o k) -> p o k", o=1),
            iotaet[:].rearrange("p (x o) -> p x o", o=1),
            ALU.is_equal))
        P.do("vector", lambda e: e.tensor_reduce(
            out=B_[:, 512:576], in_=B_[:, 0:512].rearrange("p (x k) -> p x k", k=8),
            axis=AX.X, op=ALU.add))
        P.do("vector", lambda e: e.tensor_copy(out=ohb[:], in_=B_[:, 512:576]))
        P.do("vector", lambda e, g=g: e.tensor_copy(out=wvb[:, g * 8:(g + 1) * 8], in_=C[:, 162:170]))
        P.do("vector", lambda e, g=g: e.tensor_copy(out=ixb[:, g * 8:(g + 1) * 8], in_=C[:, 170:178]))
        P.inc("vector", sV)
        P.wait("tensor", sV)
        P.do("tensor", lambda e: e.matmul(PS0[:, 0:64], trit[:], ohb[:], start=True, stop=False))
        P.do("tensor", lambda e: e.matmul(PS0[:, 0:64], onest[0:1, :], crow[0:1, :], start=False, stop=True))
        P.do("tensor", lambda e: e.matmul(PS0[0:1, 64:128], stript[:, 127:128], ohb[:], start=True, stop=True))
        P.inc("tensor", sT)
        P.wait("vector", sT)
        P.do("vector", lambda e: _bctt(e,
            B_[:, 0:512].rearrange("p (k x) -> p k x", x=64),
            C[:, 170:178].rearrange("p (k o) -> p k o", o=1),
            iotaet[:].rearrange("p (o x) -> p o x", o=1),
            ALU.is_equal))
        P.do("vector", lambda e: e.tensor_copy(out=B_[:, 576:640], in_=PS0[:, 0:64]))
        P.do("vector", lambda e: _bctt(e,
            C[:, 256:768].rearrange("p (k x) -> p k x", x=64),
            B_[:, 0:512].rearrange("p (k x) -> p k x", x=64),
            B_[:, 576:640].rearrange("p (o x) -> p o x", o=1),
            ALU.mult))
        P.do("vector", lambda e, g=g: e.tensor_reduce(
            out=posb[:, g * 8:(g + 1) * 8],
            in_=C[:, 256:768].rearrange("p (k x) -> p k x", x=64),
            axis=AX.X, op=ALU.add))
        P.do("vector", lambda e: e.tensor_tensor(
            out=crow[0:1, :], in0=crow[0:1, :], in1=PS0[0:1, 64:128], op=ALU.add))
        P.inc("vector", sV)
        P.wait("tensor", sV)

    P.wait("gpsimd", sV)
    P.wait("gpsimd", sD)
    P.cc("AllGather", ALU.bypass, [t1_loc.opt()], [ag_t1.opt()], sCC)
    P.dma("gpsimd", cnt_loc, crow[0:1, :], sD)
    P.wait("gpsimd", sD)
    P.cc("AllGather", ALU.bypass, [cnt_loc.opt()], [ag_cnt.opt()], sCC)
    P.wait("sync", sCC)
    P.dma("sync", agcS[:], ag_cnt, sLD)
    P.wait("tensor", sLD)
    P.wait("tensor", sV)
    P.wait("tensor", sA)
    P.do("tensor", lambda e: e.matmul(PS0[0:1, 0:64], rkmS[:], agcS[:], start=True, stop=True))
    P.inc("tensor", sT)
    P.wait("scalar", sT)
    P.do("scalar", lambda e: e.copy(out=crow[0:1, :], in_=PS0[0:1, 0:64]))
    P.inc("scalar", sA)
    P.wait("tensor", sA)
    P.do("tensor", lambda e: e.matmul(PS1[:, 0:64], onest[0:1, :], crow[0:1, :], start=True, stop=True))
    P.inc("tensor", sT)
    P.wait("scalar", sT)
    P.do("scalar", lambda e: e.copy(out=brow[:, 0:64], in_=PS1[:, 0:64]))
    P.inc("scalar", sA)

    P.wait("vector", sA)
    for g in range(NT):
        B_ = tkB
        C = tkC
        gs8 = slice(g * 8, (g + 1) * 8)
        P.do("vector", lambda e, gs8=gs8: _bctt(e,
            B_[:, 0:512].rearrange("p (k x) -> p k x", x=64),
            ixb[:, gs8].rearrange("p (k o) -> p k o", o=1),
            iotaet[:].rearrange("p (o x) -> p o x", o=1),
            ALU.is_equal))
        P.do("vector", lambda e: _bctt(e,
            C[:, 0:512].rearrange("p (k x) -> p k x", x=64),
            B_[:, 0:512].rearrange("p (k x) -> p k x", x=64),
            brow[:, 0:64].rearrange("p (o x) -> p o x", o=1),
            ALU.mult))
        P.do("vector", lambda e: e.tensor_reduce(
            out=C[:, 512:520], in_=C[:, 0:512].rearrange("p (k x) -> p k x", x=64),
            axis=AX.X, op=ALU.add))
        P.do("vector", lambda e, gs8=gs8: e.tensor_tensor(
            out=C[:, 520:528], in0=posb[:, gs8], in1=C[:, 512:520], op=ALU.add))
        P.do("vector", lambda e: e.tensor_scalar(
            out=C[:, 528:536], in0=C[:, 520:528], scalar1=float(CAP), scalar2=None, op0=ALU.is_lt))
        P.do("vector", lambda e, gs8=gs8: e.tensor_tensor(
            out=C[:, 536:544], in0=wvb[:, gs8], in1=C[:, 528:536], op=ALU.mult))
        P.do("vector", lambda e: e.tensor_scalar(
            out=C[:, 544:552], in0=C[:, 520:528], scalar1=float(CAP - 1), scalar2=None, op0=ALU.min))
        P.do("vector", lambda e, gs8=gs8: e.scalar_tensor_tensor(
            out=C[:, 552:560], in0=ixb[:, gs8], scalar=float(CAP), in1=C[:, 544:552],
            op0=ALU.mult, op1=ALU.add))
        P.do("vector", lambda e: e.tensor_scalar(
            out=C[:, 560:568], in0=C[:, 552:560], scalar1=1.0, scalar2=None, op0=ALU.add))
        P.do("vector", lambda e: e.tensor_tensor(
            out=C[:, 560:568], in0=C[:, 560:568], in1=C[:, 528:536], op=ALU.mult))
        P.do("vector", lambda e: e.tensor_scalar(
            out=C[:, 560:568], in0=C[:, 560:568], scalar1=-1.0, scalar2=None, op0=ALU.add))
        P.do("vector", lambda e, g=g: e.tensor_scalar(
            out=tokcol[:, 0:1], in0=tok0S[:, 0:1], scalar1=float(g * 128 + 1), scalar2=None, op0=ALU.add))
        P.do("vector", lambda e: e.tensor_scalar(
            out=C[:, 568:576], in0=C[:, 552:560], scalar1=0.0, scalar2=tokcol[:, 0:1],
            op0=ALU.mult, op1=ALU.add))
        P.do("vector", lambda e: e.tensor_copy(
            out=ptile[:].rearrange("p (k c) -> p k c", c=2)[:, :, 0:1],
            in_=C[:, 568:576].rearrange("p (k o) -> p k o", o=1)))
        P.do("vector", lambda e: e.tensor_copy(
            out=ptile[:].rearrange("p (k c) -> p k c", c=2)[:, :, 1:2],
            in_=C[:, 536:544].rearrange("p (k o) -> p k o", o=1)))
        P.inc("vector", sV)
        P.wait("gpsimd", sV)
        gsl = slice(g * 1024, (g + 1) * 1024)
        P.dma("gpsimd", rec_islot[0, gsl].rearrange("(t k) -> t k", k=8), C[:, 560:568], sD)
        P.dma("gpsimd", rec_slotc[0, gsl].rearrange("(t k) -> t k", k=8), C[:, 552:560], sD)
        P.dma("gpsimd", rec_wv[0, gsl].rearrange("(t k) -> t k", k=8), C[:, 536:544], sD)
        P.dma("gpsimd", rec_pairs[gsl, :].rearrange("(t k) c -> t (k c)", k=128), ptile[:], sD)
        P.wait("vector", sD)

    if "recs" in tap_aps:
        P.wait("gpsimd", sD)
        P.dma("gpsimd", tap_aps["recs"], rec_islot, sD)
        P.dma("gpsimd", tap_aps["recw"], rec_wv, sD)
        P.dma("gpsimd", tap_aps["recc"], rec_slotc, sD)
    if "x1" in tap_aps:
        P.wait("gpsimd", sD)
        P.dma("gpsimd", tap_aps["x1"], x1_scr, sD)
    if "t1" in tap_aps:
        P.wait("gpsimd", sD)
        P.dma("gpsimd", tap_aps["t1"], t1_loc, sD)

    if upto == "router":
        P.wait("gpsimd", sD)
        stack.close()
        _emit(nc, P)
        return nc

    isrc = sbuf("isrc", [128, 128])
    idx32 = sbuf("idx32", [128, 128], mybir.dt.int32)
    pairsT = sbuf("pairsT", [128, 256])
    P.wait("vector", sA)
    P.do("vector", lambda e: e.memset(tkC[:, 0:520], 0.0))
    P.inc("vector", sV)
    P.wait("gpsimd", sV)
    P.wait("gpsimd", sD)
    P.dma("gpsimd", T_loc.rearrange("(p a) c -> p (a c)", p=128), tkC[:, 0:516], sD)
    P.wait("sync", sD)
    P.dma("sync", isrc[:], rec_islot[0, :].rearrange("(p j) -> p j", p=128), sLD)
    P.dma("sync", pairsT[:], rec_pairs[:, :].rearrange("(p j) c -> p (j c)", p=128), sLD)
    P.wait("vector", sLD)
    P.do("vector", lambda e: e.tensor_scalar(
        out=tkA[:, 0:128], in0=isrc[:], scalar1=0.0, scalar2=None, op0=ALU.is_lt))
    P.do("vector", lambda e: e.scalar_tensor_tensor(
        out=tkA[:, 0:128], in0=tkA[:, 0:128], scalar=float(E * CAP + 100 + 1), in1=isrc[:],
        op0=ALU.mult, op1=ALU.add))
    P.do("vector", lambda e: e.tensor_copy(out=idx32[:], in_=tkA[:, 0:128]))
    P.inc("vector", sV)
    P.wait("gpsimd", sV)
    P.wait("gpsimd", sD)
    for j in range(128):
        sGA.n += 16

        def fsc(e, j=j):
            return e.indirect_dma_start(
                out=T_loc, out_offset=IndirectOffsetOnAxis(ap=idx32[:, j:j + 1], axis=0),
                in_=pairsT[:, 2 * j:2 * j + 2], in_offset=None,
                compute_op=ALU.add,
            ).then_inc(sGA.h, 16)
        P.do("gpsimd", fsc)
    P.wait("gpsimd", sGA)
    P.cc("AllReduce", ALU.add, [T_loc.opt()], [T_glob.opt()], sCC)
    P.wait("gpsimd", sCC)
    if "Tglob" in tap_aps:
        P.dma("gpsimd", tap_aps["Tglob"], T_glob, sD)

    if upto == "table":
        P.wait("gpsimd", sD)
        stack.close()
        _emit(nc, P)
        return nc

    P.fullbar(sOUT)
    sbfree("pairsT", "idx32", "isrc", "tok0S", "tokcol", "ptile", "t1bt",
           "ixt", "rkmS", "agcS", "brow", "crow", "ohb", "wvb", "ixb",
           "posb", "rwf", "t1Tg", "gtb", "bdnT", "wdn", "big2s")
    wge = sbuf("wge", [128, 8 * DI], BF16)
    wue = sbuf("wue", [128, 8 * DI], BF16)
    wde = sbuf("wde", [128, 4 * H], BF16)
    dispT = sbuf("dispT", [128, 8 * CAP], BF16)
    gub = sbuf("gub", [128, 4 * CAP], BF16)
    twt = sbuf("twt", [128, 8])
    eob = sbuf("eob", [128, H], BF16)
    exidx = sbuf("exidx", [128, 1], mybir.dt.int32)
    exf = sbuf("exf", [128, 1])
    dtok = sbuf("dtok", [128, H], BF16)

    for e_i in range(EPC):
        for c4 in range(8):
            wconv(wge, c4 * DI, gate_w[e_i * H + c4 * 128: e_i * H + (c4 + 1) * 128, :], DI)
            wconv(wue, c4 * DI, up_w[e_i * H + c4 * 128: e_i * H + (c4 + 1) * 128, :], DI)
        for c4 in range(4):
            wconv(wde, c4 * H, down_w[e_i * DI + c4 * 128: e_i * DI + (c4 + 1) * 128, :], H)
        for sc in range(4):
            P.wait("vector", sGA)
            P.wait("vector", sD)
            P.do("vector", lambda e, e_i=e_i, sc=sc: e.tensor_scalar(
                out=exf[:, 0:1], in0=iot128[:, 0:1], scalar1=float(128 * sc), scalar2=None, op0=ALU.add))
            P.do("vector", lambda e, e_i=e_i: e.tensor_tensor(
                out=exf[:, 0:1], in0=exf[:, 0:1], in1=ebaset[:, e_i:e_i + 1], op=ALU.add))
            P.do("vector", lambda e: e.tensor_copy(out=exidx[:], in_=exf[:, 0:1]))
            P.inc("vector", sV)
            P.wait("gpsimd", sV)
            igather("gpsimd", twt[:, 2 * sc:2 * sc + 2], T_glob, exidx[:, 0:1], sGA)
            P.wait("vector", sGA)
            P.do("vector", lambda e, sc=sc: e.tensor_scalar(
                out=exf[:, 0:1], in0=twt[:, 2 * sc:2 * sc + 1], scalar1=-1.0, scalar2=0.0,
                op0=ALU.add, op1=ALU.max))
            P.do("vector", lambda e: e.tensor_copy(out=exidx[:], in_=exf[:, 0:1]))
            P.inc("vector", sV)
            P.wait("gpsimd", sV)
            P.wait("gpsimd", sT)
            igather("gpsimd", dtok[:], ag_t1, exidx[:, 0:1], sGA)
            P.wait("tensor", sGA)
            P.wait("tensor", sA)
            for hc in range(8):
                pslot = PS2 if hc % 2 == 0 else PS3
                P.do("tensor", lambda e, hc=hc, p=pslot: e.transpose(
                    p.ap().bitcast(BF16)[:, 0:128],
                    dtok[:, hc * 128:(hc + 1) * 128], identt[:]))
                P.inc("tensor", sT)
                P.wait("scalar", sT)
                P.do("scalar", lambda e, hc=hc, sc=sc, p=pslot: e.copy(
                    out=dispT[:, hc * CAP + sc * 128: hc * CAP + (sc + 1) * 128],
                    in_=p.ap().bitcast(BF16)[:, 0:128]))
                P.inc("scalar", sA)
                P.wait("tensor", sA)
        for ic in range(4):
            P.wait("tensor", sA)
            P.wait("tensor", sV)
            for hc in range(8):
                P.do("tensor", lambda e, ic=ic, hc=hc: e.matmul(
                    PS0[:, 0:512], wge[:, hc * DI + ic * 128: hc * DI + (ic + 1) * 128],
                    dispT[:, hc * CAP:(hc + 1) * CAP], start=(hc == 0), stop=(hc == 7)))
            for hc in range(8):
                P.do("tensor", lambda e, ic=ic, hc=hc: e.matmul(
                    PS1[:, 0:512], wue[:, hc * DI + ic * 128: hc * DI + (ic + 1) * 128],
                    dispT[:, hc * CAP:(hc + 1) * CAP], start=(hc == 0), stop=(hc == 7)))
            P.inc("tensor", sT)
            P.wait("scalar", sT)
            P.do("scalar", lambda e: e.activation(out=tkA[:, 0:512], in_=PS0[:, 0:512], func=AF.Silu))
            P.inc("scalar", sA)
            P.wait("vector", sA)
            P.do("vector", lambda e: e.tensor_tensor(out=tkB[:, 0:512], in0=tkA[:, 0:512], in1=PS1[:, 0:512], op=ALU.mult))
            P.do("vector", lambda e, ic=ic: e.tensor_copy(out=gub[:, ic * CAP:(ic + 1) * CAP], in_=tkB[:, 0:512]))
            P.inc("vector", sV)
            P.wait("tensor", sV)
        for sc in range(4):
            P.wait("tensor", sA)
            for ic in range(4):
                for q in range(2):
                    P.do("tensor", lambda e, sc=sc, ic=ic, q=q: e.matmul(
                        PS2[:, q * 512:(q + 1) * 512],
                        gub[:, ic * CAP + sc * 128: ic * CAP + (sc + 1) * 128],
                        wde[:, ic * H + q * 512: ic * H + (q + 1) * 512],
                        start=(ic == 0), stop=(ic == 3)))
            P.inc("tensor", sT)
            P.wait("scalar", sT)
            P.wait("scalar", sGA)
            P.do("scalar", lambda e: e.copy(out=eob[:], in_=PS2[:, 0:1024]))
            P.inc("scalar", sA)
            P.wait("gpsimd", sA)
            P.dma("gpsimd", eo_loc[e_i * CAP + sc * 128: e_i * CAP + (sc + 1) * 128, :], eob[:], sD)
            P.wait("tensor", sA)
        P.wait("sync", sA)
    P.wait("gpsimd", sD)
    P.cc("AllGather", ALU.bypass, [eo_loc.opt()], [ag_eo.opt()], sCC)
    P.wait("gpsimd", sCC)
    if "eo" in tap_aps:
        P.dma("gpsimd", tap_aps["eo"], eo_loc, sD)
    if upto == "expert":
        P.wait("gpsimd", sD)
        stack.close()
        _emit(nc, P)
        return nc

    P.fullbar(sOUT)
    sbfree("dtok", "exf", "exidx", "eob", "twt", "gub", "dispT", "wde", "wue", "wge")
    sct = sbuf("sct", [128, 128])
    sci = sbuf("sci", [128, 128], mybir.dt.int32)
    wvT = sbuf("wvT", [128, 128])
    ctr = sbuf("ctr", [128, H])
    cgt = sbuf("cgt", [128, H], BF16)
    wj = sbuf("wj", [128, 128], BF16)
    wshg = sbuf("wshg", [128, 8 * DI], BF16)
    wshu = sbuf("wshu", [128, 8 * DI], BF16)
    wsd = sbuf("wsd", [128, 4 * H], BF16)
    t1hb = sbuf("t1hb", [128, H], BF16)
    sgb = sbuf("sgb", [128, 4 * 128], BF16)

    for c4 in range(8):
        wconv(wshg, c4 * DI, sh_g[c4 * 128:(c4 + 1) * 128, :], DI)
        wconv(wshu, c4 * DI, sh_u[c4 * 128:(c4 + 1) * 128, :], DI)
    for c4 in range(4):
        wconv(wsd, c4 * H, sh_d[c4 * 128:(c4 + 1) * 128, :], H)
    P.wait("sync", sD)
    P.dma("sync", sct[:], rec_slotc[0, :].rearrange("(p j) -> p j", p=128), sLD)
    P.dma("sync", wvT[:], rec_wv[0, :].rearrange("(p j) -> p j", p=128), sLD)
    P.wait("tensor", sLD)
    P.wait("tensor", sA)
    P.do("tensor", lambda e: e.transpose(PS3[:, 0:128], sct[:], identf[:]))
    P.inc("tensor", sT)
    P.wait("scalar", sT)
    P.do("scalar", lambda e: e.copy(out=sct[:], in_=PS3[:, 0:128]))
    P.inc("scalar", sA)
    P.wait("tensor", sA)
    P.do("tensor", lambda e: e.transpose(PS3[:, 0:128], wvT[:], identf[:]))
    P.inc("tensor", sT)
    P.wait("scalar", sT)
    P.do("scalar", lambda e: e.copy(out=wvT[:], in_=PS3[:, 0:128]))
    P.inc("scalar", sA)
    P.wait("vector", sA)
    P.do("vector", lambda e: e.tensor_copy(out=sci[:], in_=sct[:]))
    P.inc("vector", sV)

    for g in range(NT):
        for jj in range(8):
            j = g * 8 + jj
            P.wait("gpsimd", sV)
            P.wait("gpsimd", sA)
            P.wait("gpsimd", sT)
            igather("gpsimd", cgt[:], ag_eo, sci[:, j:j + 1], sGA)
            P.wait("vector", sGA)
            P.do("vector", lambda e: e.memset(wj[:], 0.0))
            P.do("vector", lambda e, j=j, jj=jj: e.tensor_scalar(
                out=wj[:, 16 * jj:16 * (jj + 1)], in0=bmaskt[:],
                scalar1=wvT[:, j:j + 1], scalar2=None, op0=ALU.mult))
            P.inc("vector", sV)
            P.wait("tensor", sV)
            P.wait("tensor", sGA)
            if jj == 0:
                P.wait("tensor", sA)
            for q in range(2):
                P.do("tensor", lambda e, q=q, jj=jj: e.matmul(
                    PS0[:, q * 512:(q + 1) * 512], wj[:],
                    cgt[:, q * 512:(q + 1) * 512], start=(jj == 0), stop=(jj == 7)))
            P.inc("tensor", sT)
        P.wait("scalar", sT)
        P.do("scalar", lambda e: e.copy(out=ctr[:], in_=PS0[:]))
        P.inc("scalar", sA)
        P.wait("sync", sA)
        P.wait("sync", sV)
        P.wait("sync", sT)
        P.dma("sync", cgt[:], t1_loc[g * 128:(g + 1) * 128, :], sLD)
        P.wait("tensor", sLD)
        for hc in range(8):
            pslot = PS2 if hc % 2 == 0 else PS3
            P.do("tensor", lambda e, hc=hc, p=pslot: e.transpose(
                p.ap().bitcast(BF16)[:, 0:128], cgt[:, hc * 128:(hc + 1) * 128], identt[:]))
            P.inc("tensor", sT)
            P.wait("scalar", sT)
            P.do("scalar", lambda e, hc=hc, p=pslot: e.copy(
                out=t1hb[:, hc * 128:(hc + 1) * 128], in_=p.ap().bitcast(BF16)[:, 0:128]))
            P.inc("scalar", sA)
            P.wait("tensor", sA)
        for ic in range(4):
            for hc in range(8):
                P.do("tensor", lambda e, ic=ic, hc=hc: e.matmul(
                    PS2[:, 0:128], wshg[:, hc * DI + ic * 128: hc * DI + (ic + 1) * 128],
                    t1hb[:, hc * 128:(hc + 1) * 128], start=(hc == 0), stop=(hc == 7)))
            for hc in range(8):
                P.do("tensor", lambda e, ic=ic, hc=hc: e.matmul(
                    PS3[:, 0:128], wshu[:, hc * DI + ic * 128: hc * DI + (ic + 1) * 128],
                    t1hb[:, hc * 128:(hc + 1) * 128], start=(hc == 0), stop=(hc == 7)))
            P.inc("tensor", sT)
            P.wait("scalar", sT)
            P.do("scalar", lambda e: e.activation(out=tkA[:, 0:128], in_=PS2[:, 0:128], func=AF.Silu))
            P.inc("scalar", sA)
            P.wait("vector", sA)
            P.do("vector", lambda e: e.tensor_tensor(out=tkA[:, 128:256], in0=tkA[:, 0:128], in1=PS3[:, 0:128], op=ALU.mult))
            P.do("vector", lambda e, ic=ic: e.tensor_copy(out=sgb[:, ic * 128:(ic + 1) * 128], in_=tkA[:, 128:256]))
            P.inc("vector", sV)
            P.wait("tensor", sV)
        for ic in range(4):
            for q in range(2):
                P.do("tensor", lambda e, ic=ic, q=q: e.matmul(
                    PS1[:, q * 512:(q + 1) * 512], sgb[:, ic * 128:(ic + 1) * 128],
                    wsd[:, ic * H + q * 512: ic * H + (q + 1) * 512],
                    start=(ic == 0), stop=(ic == 3)))
        P.inc("tensor", sT)
        P.wait("sync", sV)
        P.wait("sync", sD)
        P.dma("sync", tkA[:], x1_scr[g * 128:(g + 1) * 128, :], sLD)
        P.wait("vector", sT)
        P.do("vector", lambda e: e.tensor_tensor(out=tkC[:], in0=ctr[:], in1=PS1[:], op=ALU.add))
        ln_stats(tkC[:])
        ln_apply(tkC[:], tkB[:])
        P.wait("vector", sLD)
        P.do("vector", lambda e: e.tensor_tensor(out=tkC[:], in0=tkA[:], in1=tkB[:], op=ALU.add))
        P.inc("vector", sV)
        P.wait("gpsimd", sV)
        P.dma("gpsimd", out[g * 128:(g + 1) * 128, :], tkC[:], sFIN)
        P.wait("tensor", sV)
        P.wait("scalar", sLD)

    P.wait("gpsimd", sFIN)
    stack.close()
    _emit(nc, P)
    return nc


def _emit(nc, P):
    with nc.Block() as block:
        @block.sync
        def _(eng):
            for f in P.ops["sync"]:
                f(eng)

        @block.scalar
        def _(eng):
            for f in P.ops["scalar"]:
                f(eng)

        @block.vector
        def _(eng):
            for f in P.ops["vector"]:
                f(eng)

        @block.tensor
        def _(eng):
            for f in P.ops["tensor"]:
                f(eng)

        @block.gpsimd
        def _(eng):
            for f in P.ops["gpsimd"]:
                f(eng)


def _consts():
    ident = np.eye(128, dtype=bfdt)
    strip = np.zeros((128, 255), dtype=bfdt)
    strip[:, 127] = 1
    tri = np.triu(np.ones((128, 128), np.float32), 1).astype(bfdt)
    ones = np.ones((1, 128), np.float32)
    iotae = np.tile(np.arange(E, dtype=np.float32)[None, :], (128, 1))
    bmask = np.zeros((128, 16), np.float32)
    for r in range(128):
        bmask[r, r // 8] = 1
    return ident, strip, tri, ones, iotae, bmask.astype(bfdt)


def make_in_maps(inputs):
    f32 = np.float32
    x = np.ascontiguousarray(np.asarray(inputs["L0_attn_x"], f32).reshape(B * L, H))
    ident, strip, tri, ones, iotae, bmask = _consts()
    al = np.asarray(inputs["L0_attn_A_log"], f32)
    in_maps = []
    for core in range(NCORES):
        t0c = core * S
        halo = np.zeros((2, H), f32)
        if core % 4 != 0:
            halo[:] = x[t0c - 2:t0c]
        dblk = core % 4
        m = {
            "x_slice": np.ascontiguousarray(x[t0c:t0c + S]),
            "x_halo": halo,
            "ln_gamma": np.asarray(inputs["L0_attn_ln_gamma"], f32).reshape(1, H),
            "ln_beta": np.asarray(inputs["L0_attn_ln_beta"], f32).reshape(1, H),
            "w_up": np.ascontiguousarray(np.asarray(inputs["L0_attn_W_up"], f32)),
            "b_up": np.asarray(inputs["L0_attn_b_up"], f32).reshape(1, H),
            "conv_w": np.ascontiguousarray(np.asarray(inputs["L0_attn_conv_w"], f32).reshape(DI, 3)),
            "conv_b": np.asarray(inputs["L0_attn_conv_b"], f32).reshape(DI, 1),
            "w_delta": np.ascontiguousarray(np.asarray(inputs["L0_attn_W_delta"], f32)),
            "b_delta": np.asarray(inputs["L0_attn_b_delta"], f32).reshape(DI, 1),
            "w_B": np.ascontiguousarray(np.asarray(inputs["L0_attn_W_B"], f32)),
            "w_C": np.ascontiguousarray(np.asarray(inputs["L0_attn_W_C"], f32)),
            "a_log_blk": np.ascontiguousarray(al[dblk * 128:(dblk + 1) * 128]),
            "d_vec": np.asarray(inputs["L0_attn_D"], f32).reshape(DI, 1),
            "w_down": np.ascontiguousarray(np.asarray(inputs["L0_attn_W_down"], f32)),
            "b_down": np.asarray(inputs["L0_attn_b_down"], f32).reshape(1, H),
            "router_w": np.ascontiguousarray(np.asarray(inputs["L0_ffn_router_w"], f32)),
            "gate_w_blk": np.ascontiguousarray(
                np.asarray(inputs["L0_ffn_gate_w"], f32)[core * EPC:(core + 1) * EPC].reshape(EPC * H, DI)),
            "up_w_blk": np.ascontiguousarray(
                np.asarray(inputs["L0_ffn_up_w"], f32)[core * EPC:(core + 1) * EPC].reshape(EPC * H, DI)),
            "down_w_blk": np.ascontiguousarray(
                np.asarray(inputs["L0_ffn_down_w"], f32)[core * EPC:(core + 1) * EPC].reshape(EPC * DI, H)),
            "sh_gate": np.ascontiguousarray(np.asarray(inputs["L0_ffn_shared_gate_w"], f32)),
            "sh_up": np.ascontiguousarray(np.asarray(inputs["L0_ffn_shared_up_w"], f32)),
            "sh_down": np.ascontiguousarray(np.asarray(inputs["L0_ffn_shared_down_w"], f32)),
            "c_ident": ident, "c_strip": strip, "c_tri": tri, "c_ones": ones,
            "c_iotae": iotae, "c_bmask": bmask,
        }
        m["halo_mask"] = np.full((128, 1), 0.0 if core % 4 == 0 else 1.0, f32)
        rm = np.zeros((NCORES, 1), f32)
        rm[:core] = 1.0
        m["rank_mask"] = rm
        m["tok0"] = (core * S + np.arange(128, dtype=f32))[:, None].copy()

        bat = core // 4
        gf = []
        for rr in range(4):
            for arr in range(2):
                base = (4 * bat + rr) * 2 * DI + arr * DI + dblk * 128
                gf.append(base + np.arange(128))
        m["gidx_f32"] = np.stack(gf, axis=1).astype(np.int32)
        gb = []
        for rr in range(4):
            for tau in range(2):
                for arr in range(2):
                    base = (4 * bat + rr) * 2 * NS + arr * NS + tau * 128
                    gb.append(base + np.arange(128))
        m["gidx_bf"] = np.stack(gb, axis=1).astype(np.int32)
        gy = []
        for rr in range(4):
            rows = ((4 * bat + rr) * 128 + np.arange(128)) * 4 + (core % 4)
            gy.append(rows)
        m["gidx_ys"] = np.stack(gy, axis=1).astype(np.int32)
        m["iota128"] = np.arange(128, dtype=f32)[:, None].copy()
        m["ebase"] = np.tile((512.0 * (core * EPC + np.arange(8))).astype(f32)[None, :], (128, 1))
        in_maps.append(m)
    return in_maps


def kernel(**inputs):
    in_maps = make_in_maps(inputs)
    nc = build_nc(0)
    res = run_bass_kernel_spmd(nc, in_maps, core_ids=list(range(NCORES)))
    outs = [res.results[c]["out"] for c in range(NCORES)]
    return np.concatenate(outs, 0).reshape(B, L, H)


def run_partial(inputs, upto, taps=()):
    in_maps = make_in_maps(inputs)
    nc = build_nc(0, taps=taps, upto=upto)
    res = run_bass_kernel_spmd(nc, in_maps, core_ids=list(range(NCORES)))
    return res


# version 04
# speedup vs baseline: 1.0034x; 1.0034x over previous
import contextlib
import numpy as np
import ml_dtypes
import concourse.bass as bass
import concourse.mybir as mybir
from concourse.bass_utils import run_bass_kernel_spmd
from concourse.bass_core_rust import IndirectOffsetOnAxis

F32 = mybir.dt.float32
BF16 = mybir.dt.bfloat16
I16 = mybir.dt.int16
U16 = mybir.dt.uint16
AF = mybir.ActivationFunctionType
ALU = mybir.AluOpType
AX = mybir.AxisListType

NCORES = 8
B, L, H = 2, 8192, 1024
DI, NS = 512, 256
S = 2048
NT = S // 128
E, CAP, TOPK, G, TG = 64, 512, 8, 8, 4
MOE_SCALE = 2.5
EPC = E // NCORES
CH = 1024
NCH = L // CH
LNEPS = 1e-5

bfdt = ml_dtypes.bfloat16


class Sem:
    def __init__(self, h):
        self.h = h
        self.n = 0


class Prog:
    ENGINES = ("sync", "scalar", "vector", "tensor", "gpsimd")

    AUTO = ("scalar", "vector")

    def __init__(self):
        self.ops = {e: [] for e in self.ENGINES}
        self.last_inc = {e: None for e in self.ENGINES}
        self.chain = {}

    def do(self, eng, fn, auto=True):
        if auto and eng in self.AUTO and eng in self.chain:
            ch = self.chain[eng]
            prev = ch.n
            ch.n += 1

            def f(e, fn=fn, ch=ch, prev=prev):
                if prev > 0:
                    e.wait_ge(ch.h, prev)
                return fn(e).then_inc(ch.h, 1)
            f._wrapped = True
            self.ops[eng].append(f)
            self.last_inc[eng] = (ch, ch.n)
        else:
            self.ops[eng].append(fn)

    def dma(self, eng, out, in_, sem, slow=False):
        sem.n += 16

        def f(e, out=out, in_=in_, s=sem, slow=slow):
            if slow:
                e.dma_start(out=out, in_=in_, allow_slow_non_contiguous=True).then_inc(s.h, 16)
            else:
                e.dma_start(out=out, in_=in_).then_inc(s.h, 16)
        self.ops[eng].append(f)

    def cc(self, kind, op, ins, outs, sem):
        sem.n += 1

        def f(e, k=kind, op=op, ins=ins, outs=outs, s=sem):
            e.collective_compute(k, op, [list(range(NCORES))], ins, outs).then_inc(s.h, 1)
        self.ops["gpsimd"].append(f)

    def inc(self, eng, sem, n=1):
        ops = self.ops[eng]
        if eng in self.AUTO and eng in self.chain:
            if sem is self.chain[eng]:
                return
            ch = self.chain[eng]
            v = ch.n
            sem.n += n

            def f0(e, ch=ch, v=v, s=sem, n=n):
                if v > 0:
                    e.wait_ge(ch.h, v)
                return e.sem_inc(s.h, n)
            ops.append(f0)
            return
        sem.n += n
        if not ops:
            def fz(e, s=sem, n=n):
                return e.sem_inc(s.h, n)
            ops.append(fz)
            return
        last = ops[-1]
        if getattr(last, "_wrapped", False):
            li = self.last_inc.get(eng)
            lv = li[1] if li else 0
            lsem = li[0] if li else None

            def f1(e, s=sem, n=n, lsem=lsem, lv=lv):
                if lsem is not None and lv > 0:
                    e.wait_ge(lsem.h, lv)
                return e.sem_inc(s.h, n)
            ops.append(f1)
            return

        def f(e, last=last, s=sem, n=n):
            r = last(e)
            try:
                return r.then_inc(s.h, n)
            except Exception:
                return e.sem_inc(s.h, n)
        f._wrapped = True
        ops[-1] = f
        self.last_inc[eng] = (sem, sem.n)

    def wait(self, eng, sem, val=None):
        v = sem.n if val is None else val

        def f(e, s=sem, v=v):
            e.wait_ge(s.h, v)
        self.ops[eng].append(f)

    def fullbar(self, sem):
        for e in self.ENGINES:
            ops = self.ops[e]
            if ops and not getattr(ops[-1], "_wrapped", False):
                self.inc(e, sem)
            else:
                li = self.last_inc[e]
                if li is not None:
                    self.wait(e, li[0], li[1])
                sem.n += 1

                def f0(e2, s=sem):
                    return e2.sem_inc(s.h, 1)
                ops.append(f0)
        for e in self.ENGINES:
            self.wait(e, sem)



def _bctt(e, out, in0, in1, op):
    a, b = bass.broadcast_tensor_aps(in0, in1)
    return e.tensor_tensor(out=out, in0=a, in1=b, op=op)


def build_nc(core, taps=(), upto="all"):
    rank = core
    batch = core // 4
    dblk = core % 4
    batch_start = (core % 4 == 0)

    nc = bass.Bass("TRN2", target_bir_lowering=False, detect_race_conditions=False)

    def din(name, shape, dt=F32):
        return nc.dram_tensor(name, shape, dt, kind="ExternalInput").ap()

    def dout(name, shape, dt=F32):
        return nc.dram_tensor(name, shape, dt, kind="ExternalOutput").ap()

    def dint(name, shape, dt=F32):
        return nc.dram_tensor(name, shape, dt, kind="Internal").ap()

    x_in = din("x_slice", [S, H])
    x_halo = din("x_halo", [2, H])
    ln_g = din("ln_gamma", [1, H])
    ln_b = din("ln_beta", [1, H])
    w_up = din("w_up", [H, H])
    b_up = din("b_up", [1, H])
    conv_w = din("conv_w", [DI, 3])
    conv_b = din("conv_b", [DI, 1])
    w_delta = din("w_delta", [DI, DI])
    b_delta = din("b_delta", [DI, 1])
    w_B = din("w_B", [DI, NS])
    w_C = din("w_C", [DI, NS])
    a_log = din("a_log_blk", [128, NS])
    d_vec = din("d_vec", [DI, 1])
    w_down = din("w_down", [DI, H])
    b_down = din("b_down", [1, H])
    router_w = din("router_w", [H, E])
    gate_w = din("gate_w_blk", [EPC * H, DI])
    up_w = din("up_w_blk", [EPC * H, DI])
    down_w = din("down_w_blk", [EPC * DI, H])
    sh_g = din("sh_gate", [H, DI])
    sh_u = din("sh_up", [H, DI])
    sh_d = din("sh_down", [DI, H])
    halom = din("halo_mask", [128, 1])
    rankmask_in = din("rank_mask", [NCORES, 1])
    tok0_in = din("tok0", [128, 1])
    I32 = mybir.dt.int32
    gidx_f32 = din("gidx_f32", [128, 8], I32)
    gidx_bf = din("gidx_bf", [128, 16], I32)
    gidx_ys = din("gidx_ys", [128, 4], I32)
    iota128 = din("iota128", [128, 1])
    ebaset_in = din("ebase", [128, 8])
    ident = din("c_ident", [128, 128], BF16)
    strip = din("c_strip", [128, 255], BF16)
    tri = din("c_tri", [128, 128], BF16)
    onesb = din("c_ones", [1, 128], F32)
    iotae = din("c_iotae", [128, E])
    bmask = din("c_bmask", [128, 16], BF16)

    out = dout("out", [S, H])

    pre_f32 = dint("pre_f32", [2 * DI, S])
    ag_f32 = dint("ag_f32", [NCORES * 2 * DI, S])
    pre_bf = dint("pre_bf", [2 * NS, S], BF16)
    ag_bf = dint("ag_bf", [NCORES * 2 * NS, S], BF16)
    uT_scr = dint("uT_scr", [DI, S])
    srT_scr = dint("srT_scr", [DI, S])
    ysT_loc = dint("ysT_loc", [128, L])
    ag_ys = dint("ag_ys", [NCORES * 128, L])
    x1_scr = dint("x1_scr", [S, H])
    t1_loc = dint("t1_loc", [S, H], BF16)
    ag_t1 = dint("ag_t1", [B * L, H], BF16)
    cnt_loc = dint("cnt_loc", [1, E])
    ag_cnt = dint("ag_cnt", [NCORES, E])
    rec_islot = dint("rec_islot", [1, S * TOPK])
    rec_slotc = dint("rec_slotc", [1, S * TOPK])
    rec_wv = dint("rec_wv", [1, S * TOPK])
    rec_pairs = dint("rec_pairs", [S * TOPK, 2])
    T_loc = dint("T_loc", [E * CAP + 256, 2])
    T_glob = dint("T_glob", [E * CAP + 256, 2])
    T_all = dint("T_all", [NCORES * (E * CAP + 256), 2])
    eo_loc = dint("eo_loc", [EPC * CAP, H], BF16)
    ag_eo = dint("ag_eo", [E * CAP, H], BF16)

    tap_aps = {}
    for tname, tshape, tdt in taps:
        tap_aps[tname] = dout("tap_" + tname, tshape, tdt)

    P = Prog()
    stack = contextlib.ExitStack()
    _cms = {}

    def sbuf(name, shape, dt=F32):
        cm = nc.sbuf_tensor(name, shape, dt)
        t = cm.__enter__()
        _cms[name] = cm
        return t

    def sbfree(*names):
        for n in names:
            _cms.pop(n).__exit__(None, None, None)

    def psum(name, shape, dt=F32):
        return stack.enter_context(nc.psum_tensor(name, shape, dt))

    def mksem(name):
        return Sem(stack.enter_context(nc.semaphore(name)))

    tkA = sbuf("tkA", [128, H])
    tkB = sbuf("tkB", [128, H])
    tkC = sbuf("tkC", [128, H])
    sc1 = sbuf("sc1", [128, 1])
    sc2 = sbuf("sc2", [128, 1])
    sc3 = sbuf("sc3", [128, 1])
    sc4 = sbuf("sc4", [128, 1])
    sqsc = sbuf("sqsc", [128, H])
    big0 = sbuf("big0", [128, 4 * S])
    big1 = sbuf("big1", [128, 4 * S])
    wdl = sbuf("wdl", [128, 4 * DI], BF16)
    wBc = sbuf("wBc", [128, 4 * NS], BF16)
    wCc = sbuf("wCc", [128, 4 * NS], BF16)
    convw = sbuf("convw", [128, 12])
    convbc = sbuf("convbc", [128, 4])
    bupc = sbuf("bupc", [128, 8])
    bdlc = sbuf("bdlc", [128, 4])
    bdnc = sbuf("bdnc", [128, 8])
    dvecc = sbuf("dvecc", [128, 4])
    identt = sbuf("identt", [128, 128], BF16)
    identf = sbuf("identf", [128, 128])
    stript = sbuf("stript", [128, 255], BF16)
    trit = sbuf("trit", [128, 128], BF16)
    onest = sbuf("onest", [1, 128])
    iotaet = sbuf("iotaet", [128, E])
    bmaskt = sbuf("bmaskt", [128, 16], BF16)
    halomt = sbuf("halomt", [128, 1])
    epst = sbuf("epst", [128, 1])
    onec = sbuf("onec", [128, 1])
    gixf = sbuf("gixf", [128, 8], I32)
    gixb = sbuf("gixb", [128, 16], I32)
    gixy = sbuf("gixy", [128, 4], I32)
    iot128 = sbuf("iot128", [128, 1])
    ebaset = sbuf("ebaset", [128, 8])
    halT = sbuf("halT", [128, 16], BF16)

    PS0 = psum("PS0", [128, 1024])
    PS1 = psum("PS1", [128, 1024])
    PS2 = psum("PS2", [128, 1024])
    PS3 = psum("PS3", [128, 1024])

    sLD = mksem("sLD")
    sW = mksem("sW")
    sV = mksem("sV")
    sA = mksem("sA")
    sT = mksem("sT")
    sG = mksem("sG")
    sD = mksem("sD")
    sCC = mksem("sCC")
    sOUT = mksem("sOUT")
    sGA = mksem("sGA")
    sFIN = mksem("sFIN")
    P.chain = {"vector": sV, "scalar": sA}

    def ln_stats(src_ap):
        P.inc("vector", sV)
        P.wait("scalar", sV)
        P.do("vector", lambda e, a=src_ap: e.tensor_reduce(
            out=sc3[:, 0:1], in_=a, axis=AX.X, op=ALU.add))
        P.do("scalar", lambda e, a=src_ap: e.activation(
            out=sqsc[:], in_=a, func=AF.Square))
        P.inc("scalar", sA)
        P.wait("vector", sA)
        P.do("vector", lambda e: e.tensor_reduce(
            out=sc4[:, 0:1], in_=sqsc[:], axis=AX.X, op=ALU.add))
        P.do("vector", lambda e: e.tensor_scalar(
            out=sc3[:, 0:1], in0=sc3[:, 0:1], scalar1=1.0 / H, scalar2=None, op0=ALU.mult))
        P.do("vector", lambda e: e.tensor_scalar(
            out=sc4[:, 0:1], in0=sc4[:, 0:1], scalar1=1.0 / H, scalar2=None, op0=ALU.mult))
        P.do("vector", lambda e: e.tensor_tensor(
            out=sc1[:, 0:1], in0=sc3[:, 0:1], in1=sc3[:, 0:1], op=ALU.mult))
        P.do("vector", lambda e: e.tensor_tensor(
            out=sc4[:, 0:1], in0=sc4[:, 0:1], in1=sc1[:, 0:1], op=ALU.subtract))
        P.inc("vector", sV)
        P.wait("scalar", sV)
        P.do("scalar", lambda e: e.activation(
            out=sc1[:, 0:1], in_=sc4[:, 0:1], func=AF.Sqrt, bias=epst[:, 0:1]))
        P.inc("scalar", sA)
        P.wait("vector", sA)
        P.do("vector", lambda e: e.reciprocal(out=sc2[:, 0:1], in_=sc1[:, 0:1]))
        P.inc("vector", sV)
        P.wait("vector", sV)

    def ln_apply(src_ap, dst_ap):
        P.do("vector", lambda e, a=src_ap, d=dst_ap: e.tensor_scalar(
            out=d, in0=a, scalar1=sc3[:, 0:1], scalar2=sc2[:, 0:1],
            op0=ALU.subtract, op1=ALU.mult))

    P.dma("sync", identt[:], ident, sW)
    P.dma("sync", stript[:], strip, sW)
    P.dma("sync", trit[:], tri, sW)
    P.dma("sync", onest[:], onesb, sW)
    P.dma("sync", iotaet[:], iotae, sW)
    P.dma("sync", bmaskt[:], bmask, sW)
    P.dma("sync", halomt[:], halom, sW)
    P.dma("sync", gixf[:], gidx_f32, sW)
    P.dma("sync", gixb[:], gidx_bf, sW)
    P.dma("sync", gixy[:], gidx_ys, sW)
    P.dma("sync", iot128[:], iota128, sW)
    P.dma("sync", ebaset[:], ebaset_in, sW)
    P.dma("sync", convw[:].rearrange("p (c k) -> p c k", k=3), conv_w.rearrange("(c p) k -> p c k", p=128), sW, slow=True)
    P.dma("sync", convbc[:].rearrange("p (c o) -> p c o", o=1), conv_b.rearrange("(c p) o -> p c o", p=128), sW, slow=True)
    P.dma("sync", bupc[:].rearrange("p (o c) -> p o c", o=1), b_up.rearrange("o (c p) -> p o c", p=128), sW, slow=True)
    P.dma("sync", bdlc[:].rearrange("p (c o) -> p c o", o=1), b_delta.rearrange("(c p) o -> p c o", p=128), sW, slow=True)
    P.dma("sync", bdnc[:].rearrange("p (o c) -> p o c", o=1), b_down.rearrange("o (c p) -> p o c", p=128), sW, slow=True)
    P.dma("sync", dvecc[:].rearrange("p (c o) -> p c o", o=1), d_vec.rearrange("(c p) o -> p c o", p=128), sW, slow=True)
    def wconv(dst_tile, dst_off, src_ap, cols):
        P.wait("sync", sA)
        P.wait("sync", sV)
        P.dma("sync", tkA[:, 0:cols], src_ap, sLD)
        P.wait("scalar", sLD)
        P.wait("scalar", sT)
        P.do("scalar", lambda e, d=dst_tile, o=dst_off, c=cols: e.copy(
            out=d[:, o:o + c], in_=tkA[:, 0:c]))
        P.inc("scalar", sA)

    for c4 in range(4):
        wconv(wdl, c4 * DI, w_delta[c4 * 128:(c4 + 1) * 128, :], DI)
        wconv(wBc, c4 * NS, w_B[c4 * 128:(c4 + 1) * 128, :], NS)
        wconv(wCc, c4 * NS, w_C[c4 * 128:(c4 + 1) * 128, :], NS)

    wupT = sbuf("wupT", [128, 8 * H], BF16)
    xaT = sbuf("xaT", [128, 8 * S], BF16)
    gamt = sbuf("gamt", [128, H])
    bett = sbuf("bett", [128, H])
    P.dma("sync", gamt[0:1, :], ln_g, sW)
    P.dma("sync", bett[0:1, :], ln_b, sW)
    P.wait("scalar", sW)
    P.do("scalar", lambda e: e.copy(out=identf[:], in_=identt[:]))
    P.inc("scalar", sA)
    P.do("vector", lambda e: e.memset(epst[:], LNEPS))
    P.do("vector", lambda e: e.memset(onec[:], 1.0))
    P.inc("vector", sV)
    for c4 in range(8):
        wconv(wupT, c4 * H, w_up[c4 * 128:(c4 + 1) * 128, :], H)
    P.wait("tensor", sW)
    for half in range(2):
        P.do("tensor", lambda e, h=half: e.matmul(
            PS0[:, h * 512:(h + 1) * 512], onest[0:1, :],
            gamt[0:1, h * 512:(h + 1) * 512], start=True, stop=True))
        P.do("tensor", lambda e, h=half: e.matmul(
            PS1[:, h * 512:(h + 1) * 512], onest[0:1, :],
            bett[0:1, h * 512:(h + 1) * 512], start=True, stop=True))
    P.inc("tensor", sT)
    P.wait("scalar", sT)
    P.do("scalar", lambda e: e.copy(out=gamt[:], in_=PS0[:]))
    P.do("scalar", lambda e: e.copy(out=bett[:], in_=PS1[:]))
    P.inc("scalar", sA)
    P.wait("tensor", sA)
    P.wait("vector", sA)

    def tileproc(g, src_rows, halo=False):
        P.wait("sync", sV)
        P.wait("sync", sA)
        P.dma("sync", tkA[0:2, :] if halo else tkA[:], src_rows, sLD)
        P.wait("vector", sLD)
        P.wait("scalar", sLD)
        ln_stats(tkA[:])
        ln_apply(tkA[:], tkB[:])
        ln_stats(tkB[:])
        ln_apply(tkB[:], tkC[:])
        P.do("vector", lambda e: e.tensor_tensor(out=tkC[:], in0=tkC[:], in1=gamt[:], op=ALU.mult))
        P.do("vector", lambda e: e.tensor_tensor(out=tkC[:], in0=tkC[:], in1=bett[:], op=ALU.add))
        P.inc("vector", sV)
        if g == 0 and not halo and "t0tile" in tap_aps:
            P.wait("gpsimd", sV)
            P.dma("gpsimd", tap_aps["t0tile"], tkB[:], sD)
            P.dma("gpsimd", tap_aps["xatile"], tkC[:], sD)
            P.dma("gpsimd", tap_aps["gamtile"], gamt[:], sD)
            P.wait("tensor", sD)
        P.wait("tensor", sV)
        for hc in range(8):
            pslot = PS2 if hc % 2 == 0 else PS3
            P.do("tensor", lambda e, hc=hc, p=pslot: e.transpose(
                p[:, 0:128], tkC[:, hc * 128:(hc + 1) * 128], identf[:]))
            P.inc("tensor", sT)
            P.wait("scalar", sT)
            if halo:
                P.do("scalar", lambda e, hc=hc, p=pslot: e.copy(
                    out=halT[:, hc * 2:hc * 2 + 2], in_=p[:, 0:2]))
            else:
                P.do("scalar", lambda e, hc=hc, p=pslot, g=g: e.copy(
                    out=xaT[:, hc * S + g * 128: hc * S + (g + 1) * 128], in_=p[:, 0:128]))
            P.inc("scalar", sA)
            P.wait("tensor", sA)

    for g in range(NT):
        tileproc(g, x_in[g * 128:(g + 1) * 128, :])
    P.wait("vector", sA)
    P.do("vector", lambda e: e.memset(tkA[:], 0.0))
    P.inc("vector", sV)
    tileproc(0, x_halo[0:2, :], halo=True)
    P.wait("vector", sA)
    P.do("vector", lambda e: e.tensor_scalar(
        out=halT[:], in0=halT[:], scalar1=halomt[:, 0:1], scalar2=None, op0=ALU.mult))
    P.inc("vector", sV)

    P.fullbar(sOUT)
    sbfree("bett", "gamt")
    leftT = sbuf("leftT", [128, 4 * (S + 8)])
    big3 = sbuf("big3", [128, 4 * S])
    P.do("vector", lambda e: e.memset(leftT[:, 0:2 * (S + 8)], 0.0))
    P.do("vector", lambda e: e.memset(leftT[:, 2 * (S + 8):], 0.0))
    P.inc("vector", sV)
    P.wait("tensor", sV)
    P.wait("tensor", sA)
    for oc in range(8):
        for th in range(2):
            pslot = PS0 if th == 0 else PS1
            for kc in range(8):
                for q in range(2):
                    P.do("tensor", lambda e, oc=oc, th=th, kc=kc, q=q, p=pslot: e.matmul(
                        p[:, q * 512:(q + 1) * 512],
                        wupT[:, kc * H + oc * 128: kc * H + (oc + 1) * 128],
                        xaT[:, kc * S + th * 1024 + q * 512: kc * S + th * 1024 + (q + 1) * 512],
                        start=(kc == 0), stop=(kc == 7)))
            P.inc("tensor", sT)
            P.wait("scalar", sT)
            if oc < 4:
                P.do("scalar", lambda e, oc=oc, th=th, p=pslot: e.activation(
                    out=leftT[:, oc * (S + 8) + 4 + th * 1024: oc * (S + 8) + 4 + (th + 1) * 1024],
                    in_=p[:], func=AF.Identity, bias=bupc[:, oc:oc + 1]))
            else:
                P.do("scalar", lambda e, oc=oc, th=th, p=pslot: e.activation(
                    out=big3[:, (oc - 4) * S + th * 1024: (oc - 4) * S + (th + 1) * 1024],
                    in_=p[:], func=AF.Silu, bias=bupc[:, oc:oc + 1]))
            P.inc("scalar", sA)
            P.wait("tensor", sA)
    if True:
        for oc in range(4):
            for kc in range(8):
                P.do("tensor", lambda e, oc=oc, kc=kc: e.matmul(
                    PS0[:, 0:2],
                    wupT[:, kc * H + oc * 128: kc * H + (oc + 1) * 128],
                    halT[:, kc * 2:(kc + 1) * 2],
                    start=(kc == 0), stop=(kc == 7)))
            P.inc("tensor", sT)
            P.wait("scalar", sT)
            P.do("scalar", lambda e, oc=oc: e.activation(
                out=leftT[:, oc * (S + 8) + 2: oc * (S + 8) + 4],
                in_=PS0[:, 0:2], func=AF.Identity, bias=bupc[:, oc:oc + 1]))
            P.inc("scalar", sA)
            P.wait("tensor", sA)
            P.wait("vector", sA)
            P.do("vector", lambda e, oc=oc: e.tensor_scalar(
                out=leftT[:, oc * (S + 8) + 2: oc * (S + 8) + 4],
                in0=leftT[:, oc * (S + 8) + 2: oc * (S + 8) + 4],
                scalar1=halomt[:, 0:1], scalar2=None, op0=ALU.mult))
            P.inc("vector", sV)
            P.wait("scalar", sV)
    P.wait("gpsimd", sA)
    P.dma("gpsimd", srT_scr.rearrange("(c p) t -> p c t", p=128), big3[:].rearrange("p (c t) -> p c t", t=S), sD)
    if "xaT" in tap_aps:
        P.dma("gpsimd", tap_aps["xaT"], xaT[:], sD)
    if "leftT" in tap_aps:
        P.dma("gpsimd", tap_aps["leftT"], leftT[:], sD)

    P.wait("vector", sA)
    for dc in range(4):
        b0 = dc * (S + 8)
        sl = slice(dc * S, (dc + 1) * S)
        P.do("vector", lambda e, b=b0, sl=sl, dc=dc: e.tensor_scalar(
            out=big0[:, sl], in0=leftT[:, b + 2:b + 2 + S],
            scalar1=convw[:, dc * 3 + 0: dc * 3 + 1], scalar2=None, op0=ALU.mult))
        P.do("vector", lambda e, b=b0, sl=sl, dc=dc: e.scalar_tensor_tensor(
            out=big0[:, sl], in0=leftT[:, b + 3:b + 3 + S],
            scalar=convw[:, dc * 3 + 1: dc * 3 + 2], in1=big0[:, sl],
            op0=ALU.mult, op1=ALU.add))
        P.do("vector", lambda e, b=b0, sl=sl, dc=dc: e.scalar_tensor_tensor(
            out=big0[:, sl], in0=leftT[:, b + 4:b + 4 + S],
            scalar=convw[:, dc * 3 + 2: dc * 3 + 3], in1=big0[:, sl],
            op0=ALU.mult, op1=ALU.add))
    P.inc("vector", sV)
    P.wait("scalar", sV)
    for dc in range(4):
        sl = slice(dc * S, (dc + 1) * S)
        P.do("scalar", lambda e, sl=sl, dc=dc: e.activation(
            out=big0[:, sl], in_=big0[:, sl], func=AF.Silu, bias=convbc[:, dc:dc + 1]))
    P.inc("scalar", sA)
    P.wait("gpsimd", sA)
    P.dma("gpsimd", uT_scr.rearrange("(c p) t -> p c t", p=128), big0[:].rearrange("p (c t) -> p c t", t=S), sD)

    P.fullbar(sOUT)
    sbfree("big3", "leftT", "xaT", "wupT")
    uTb = sbuf("uTb", [128, 4 * S], BF16)
    big2 = sbuf("big2", [128, 4 * S])
    bmstage = sbuf("bmstage", [128, 4 * S], BF16)
    P.wait("vector", sA)
    P.do("vector", lambda e: e.tensor_copy(out=uTb[:], in_=big0[:]))
    P.inc("vector", sV)
    P.wait("tensor", sV)
    for dc in range(4):
        for th in range(2):
            pslot = PS0 if th == 0 else PS1
            for kc in range(4):
                for q in range(2):
                    P.do("tensor", lambda e, dc=dc, th=th, kc=kc, q=q, p=pslot: e.matmul(
                        p[:, q * 512:(q + 1) * 512],
                        wdl[:, kc * DI + dc * 128: kc * DI + (dc + 1) * 128],
                        uTb[:, kc * S + th * 1024 + q * 512: kc * S + th * 1024 + (q + 1) * 512],
                        start=(kc == 0), stop=(kc == 3)))
            P.inc("tensor", sT)
            P.wait("scalar", sT)
            P.do("scalar", lambda e, dc=dc, th=th, p=pslot: e.activation(
                out=big1[:, dc * S + th * 1024: dc * S + (th + 1) * 1024],
                in_=p[:], func=AF.Identity, bias=bdlc[:, dc:dc + 1]))
            P.inc("scalar", sA)
            P.wait("tensor", sA)
    P.wait("vector", sA)
    for dc in range(4):
        sl = slice(dc * S, (dc + 1) * S)
        P.do("vector", lambda e, sl=sl: e.tensor_scalar(
            out=big2[:, sl], in0=big1[:, sl], scalar1=-1.0, scalar2=None, op0=ALU.mult))
        P.do("vector", lambda e, sl=sl: e.tensor_tensor(
            out=big2[:, sl], in0=big1[:, sl], in1=big2[:, sl], op=ALU.min))
        P.inc("vector", sV)
        P.wait("scalar", sV)
        P.do("scalar", lambda e, sl=sl: e.activation(
            out=big2[:, sl], in_=big2[:, sl], func=AF.Exp))
        P.do("scalar", lambda e, sl=sl: e.activation(
            out=big2[:, sl], in_=big2[:, sl], func=AF.Ln, bias=onec[:, 0:1]))
        P.inc("scalar", sA)
        P.wait("vector", sA)
        P.do("vector", lambda e, sl=sl: e.tensor_relu(out=big1[:, sl], in_=big1[:, sl]))
        P.do("vector", lambda e, sl=sl: e.tensor_tensor(
            out=big1[:, sl], in0=big1[:, sl], in1=big2[:, sl], op=ALU.add))
        P.do("vector", lambda e, sl=sl: e.tensor_tensor(
            out=big2[:, sl], in0=big1[:, sl], in1=big0[:, sl], op=ALU.mult))
    P.inc("vector", sV)
    P.wait("gpsimd", sV)
    P.dma("gpsimd", pre_f32[0:DI, :].rearrange("(c p) t -> p c t", p=128), big1[:].rearrange("p (c t) -> p c t", t=S), sD)
    P.dma("gpsimd", pre_f32[DI:2 * DI, :].rearrange("(c p) t -> p c t", p=128), big2[:].rearrange("p (c t) -> p c t", t=S), sD)
    P.wait("tensor", sV)
    for which in range(2):
        wtile = wBc if which == 0 else wCc
        for nch in range(2):
            for th in range(2):
                pslot = PS0 if th == 0 else PS1
                for kc in range(4):
                    for q in range(2):
                        P.do("tensor", lambda e, w=wtile, ncb=nch, th=th, kc=kc, q=q, p=pslot: e.matmul(
                            p[:, q * 512:(q + 1) * 512],
                            w[:, kc * NS + ncb * 128: kc * NS + (ncb + 1) * 128],
                            uTb[:, kc * S + th * 1024 + q * 512: kc * S + th * 1024 + (q + 1) * 512],
                            start=(kc == 0), stop=(kc == 3)))
                P.inc("tensor", sT)
                P.wait("scalar", sT)
                P.do("scalar", lambda e, w=which, ncb=nch, th=th, p=pslot: e.copy(
                    out=bmstage[:, (w * 2 + ncb) * S + th * 1024: (w * 2 + ncb) * S + (th + 1) * 1024],
                    in_=p[:]))
                P.inc("scalar", sA)
                P.wait("tensor", sA)
    P.wait("gpsimd", sA)
    P.dma("gpsimd", pre_bf.rearrange("(c p) t -> p c t", p=128), bmstage[:].rearrange("p (c t) -> p c t", t=S), sD)
    P.wait("gpsimd", sD)
    P.cc("AllGather", ALU.bypass, [pre_f32.opt()], [ag_f32.opt()], sCC)
    P.wait("gpsimd", sCC)
    P.cc("AllGather", ALU.bypass, [pre_bf.opt()], [ag_bf.opt()], sCC)
    P.wait("gpsimd", sCC)
    P.inc("gpsimd", sG)

    if "preT" in tap_aps:
        P.dma("gpsimd", tap_aps["preT"], pre_f32, sD)
    if "uT" in tap_aps:
        P.dma("gpsimd", tap_aps["uT"], uT_scr, sD)

    if upto == "pre":
        P.wait("gpsimd", sD)
        stack.close()
        _emit(nc, P)
        return nc

    P.fullbar(sOUT)
    sbfree("bmstage", "big2", "uTb")
    dltT = big0
    duT = big1
    bmT = sbuf("bmT", [128, 2 * L], BF16)
    cmT = sbuf("cmT", [128, 2 * L], BF16)
    a_nT = sbuf("a_nT", [128, 2 * 128])
    hbuf = sbuf("hbuf", [128, 2 * 1024])
    hcar = sbuf("hcar", [128, 2 * 128])
    dAt = sbuf("dAt", [128, 4 * 1024])
    trm = sbuf("trm", [128, 2 * 1024])
    ymt = sbuf("ymt", [128, 2 * 1024], BF16)
    ysst = sbuf("ysst", [128, 1024])
    onesf = sbuf("onesf", [128, 128])
    wbc = sbuf("wbc", [128, 2 * 128])

    def igather(eng, dst, table, idxcol, sem):
        sem.n += 16

        def gf(e, dst=dst, table=table, idxcol=idxcol, s=sem):
            return e.indirect_dma_start(
                out=dst, out_offset=None, in_=table,
                in_offset=IndirectOffsetOnAxis(ap=idxcol, axis=0),
            ).then_inc(s.h, 16)
        P.do(eng, gf)

    gi = 0
    for rr in range(4):
        for arr, dst in ((0, dltT[:, rr * S:(rr + 1) * S]), (1, duT[:, rr * S:(rr + 1) * S])):
            igather("gpsimd", dst, ag_f32, gixf[:, gi:gi + 1], sGA)
            gi += 1
    gi = 0
    for rr in range(4):
        for tau in range(2):
            for arr, dst in ((0, bmT[:, tau * L + rr * S: tau * L + (rr + 1) * S]),
                             (1, cmT[:, tau * L + rr * S: tau * L + (rr + 1) * S])):
                igather("gpsimd", dst, ag_bf, gixb[:, gi:gi + 1], sGA)
                gi += 1
    P.do("vector", lambda e: e.memset(onesf[:], 1.0))
    P.inc("vector", sV)
    P.wait("scalar", sV)
    P.dma("sync", tkA[:, 0:NS], a_log, sLD)
    P.wait("scalar", sLD)
    P.do("scalar", lambda e: e.activation(out=tkA[:, 0:NS], in_=tkA[:, 0:NS], func=AF.Exp))
    P.inc("scalar", sA)
    P.wait("vector", sA)
    P.do("vector", lambda e: e.tensor_scalar(
        out=tkA[:, 0:NS], in0=tkA[:, 0:NS], scalar1=-1.0, scalar2=None, op0=ALU.mult))
    P.do("vector", lambda e: e.memset(hcar[:], 0.0))
    P.inc("vector", sV)
    P.wait("tensor", sV)
    for tau in range(2):
        P.do("tensor", lambda e, tau=tau: e.transpose(
            PS2[:, 0:128], tkA[:, tau * 128:(tau + 1) * 128], identf[:]))
        P.inc("tensor", sT)
        P.wait("scalar", sT)
        P.do("scalar", lambda e, tau=tau: e.copy(
            out=a_nT[:, tau * 128:(tau + 1) * 128], in_=PS2[:, 0:128]))
        P.inc("scalar", sA)
        P.wait("tensor", sA)
    P.wait("vector", sA)
    P.wait("vector", sLD)
    P.wait("vector", sGA)
    P.wait("tensor", sLD)
    P.wait("tensor", sGA)
    P.wait("scalar", sLD)
    P.wait("scalar", sGA)

    redmark = [0]
    for c in range(NCH):
        for i in range(128):
            first = (c == 0 and i == 0)
            par = i % 2
            P.do("scalar", lambda e, i=i, par=par: e.activation(
                out=wbc[:, par * 128:(par + 1) * 128], in_=onesf[:],
                func=AF.Copy, scale=identf[:, i:i + 1]))
            P.inc("scalar", sA)
            if not first:
                P.wait("tensor", sV)
            P.wait("tensor", sA)
            for q in range(2):
                P.do("tensor", lambda e, i=i, c=c, q=q, par=par: e.matmul(
                    PS0[:, q * 512:(q + 1) * 512], wbc[:, par * 128:(par + 1) * 128],
                    dltT[:, c * CH + q * 512: c * CH + (q + 1) * 512],
                    start=True, stop=True))
            for q in range(2):
                P.do("tensor", lambda e, i=i, c=c, q=q, par=par: e.matmul(
                    PS1[:, q * 512:(q + 1) * 512], wbc[:, par * 128:(par + 1) * 128],
                    duT[:, c * CH + q * 512: c * CH + (q + 1) * 512],
                    start=True, stop=True))
            P.inc("tensor", sT)
            P.wait("scalar", sT)
            for tau in range(2):
                P.do("scalar", lambda e, tau=tau, i=i, par=par: e.activation(
                    out=dAt[:, (par * 2 + tau) * 1024:(par * 2 + tau + 1) * 1024], in_=PS0[:],
                    func=AF.Exp, scale=a_nT[:, tau * 128 + i: tau * 128 + i + 1]))
            P.inc("scalar", sA)
            P.wait("vector", sT)
            for tau in range(2):
                P.do("vector", lambda e, tau=tau, c=c: e.tensor_tensor(
                    out=trm[:, tau * 1024:(tau + 1) * 1024],
                    in0=bmT[:, tau * L + c * CH: tau * L + (c + 1) * CH],
                    in1=PS1[:], op=ALU.mult))
            P.inc("vector", sV)
            P.wait("vector", sA)
            if not first:
                P.wait("vector", sT, redmark[0])
            for tau in range(2):
                P.do("vector", lambda e, tau=tau, i=i, par=par: e.tensor_tensor_scan(
                    out=hbuf[:, tau * 1024:(tau + 1) * 1024],
                    data0=dAt[:, (par * 2 + tau) * 1024:(par * 2 + tau + 1) * 1024],
                    data1=trm[:, tau * 1024:(tau + 1) * 1024],
                    initial=hcar[:, tau * 128 + i: tau * 128 + i + 1],
                    op0=ALU.mult, op1=ALU.add))
                P.do("vector", lambda e, tau=tau, i=i: e.tensor_copy(
                    out=hcar[:, tau * 128 + i: tau * 128 + i + 1],
                    in_=hbuf[:, tau * 1024 + 1023: tau * 1024 + 1024]))
                P.do("vector", lambda e, tau=tau, c=c: e.tensor_tensor(
                    out=ymt[:, tau * 1024:(tau + 1) * 1024],
                    in0=hbuf[:, tau * 1024:(tau + 1) * 1024],
                    in1=cmT[:, tau * L + c * CH: tau * L + (c + 1) * CH], op=ALU.mult))
                P.inc("vector", sV)
                P.wait("tensor", sV)
                for q in range(2):
                    P.do("tensor", lambda e, i=i, tau=tau, q=q: e.matmul(
                        PS2[:, q * 512:(q + 1) * 512],
                        stript[:, 127 - i:255 - i],
                        ymt[:, tau * 1024 + q * 512: tau * 1024 + (q + 1) * 512],
                        start=(i == 0 and tau == 0), stop=(i == 127 and tau == 1)))
            P.inc("tensor", sT)
            redmark[0] = sT.n
        P.wait("scalar", sT)
        P.do("scalar", lambda e: e.copy(out=ysst[:], in_=PS2[:]))
        P.inc("scalar", sA)
        P.wait("gpsimd", sA)
        P.dma("gpsimd", ysT_loc[:, c * CH:(c + 1) * CH], ysst[:], sD)
        P.wait("gpsimd", sD)
        P.inc("gpsimd", sG)
        P.wait("tensor", sA)

    P.cc("AllGather", ALU.bypass, [ysT_loc.opt()], [ag_ys.opt()], sCC)
    P.wait("gpsimd", sCC)
    if "ysT" in tap_aps:
        P.dma("gpsimd", tap_aps["ysT"], ysT_loc, sD)

    if upto == "scan":
        P.wait("gpsimd", sD)
        stack.close()
        _emit(nc, P)
        return nc

    P.fullbar(sOUT)
    sbfree("wbc", "onesf", "ysst", "ymt", "trm", "dAt", "hcar", "hbuf",
           "a_nT", "cmT", "bmT")
    big2s = sbuf("big2s", [128, 4 * S])
    wdn = sbuf("wdn", [128, 4 * H], BF16)
    bdnT = sbuf("bdnT", [128, H])
    gtb = sbuf("gtb", [128, 4 * S], BF16)
    t1Tg = sbuf("t1Tg", [128, H])
    rwf = sbuf("rwf", [128, 8 * E])
    posb = sbuf("posb", [128, 128])
    ixb = sbuf("ixb", [128, 128])
    wvb = sbuf("wvb", [128, 128])
    ohb = sbuf("ohb", [128, E], BF16)
    crow = sbuf("crow", [1, E])
    brow = sbuf("brow", [128, E])
    agcS = sbuf("agcS", [8, E])
    rkmS = sbuf("rkmS", [8, 1])
    ixt = sbuf("ixt", [128, 8], U16)
    t1bt = sbuf("t1bt", [128, H], BF16)
    ptile = sbuf("ptile", [128, 16])
    tokcol = sbuf("tokcol", [128, 1])
    tok0S = sbuf("tok0S", [128, 1])

    for rr in range(4):
        igather("gpsimd", big0[:, rr * S:(rr + 1) * S],
                ag_ys.rearrange("a (c t) -> (a c) t", t=S), gixy[:, rr:rr + 1], sGA)
    P.dma("sync", big1[:].rearrange("p (c t) -> p c t", t=S),
          uT_scr.rearrange("(c p) t -> p c t", p=128), sLD)
    P.dma("sync", big2s[:].rearrange("p (c t) -> p c t", t=S),
          srT_scr.rearrange("(c p) t -> p c t", p=128), sLD)
    P.dma("sync", rwf[:].rearrange("p (c e) -> p c e", e=E),
          router_w.rearrange("(c p) e -> p c e", p=128), sLD)
    P.dma("sync", rkmS[:], rankmask_in, sLD)
    P.dma("sync", tok0S[:], tok0_in, sLD)
    P.dma("sync", bdnT[0:1, :], b_down, sLD)
    for c4 in range(4):
        wconv(wdn, c4 * H, w_down[c4 * 128:(c4 + 1) * 128, :], H)
    P.wait("tensor", sLD)
    P.wait("tensor", sA)
    for half in range(2):
        P.do("tensor", lambda e, h=half: e.matmul(
            PS0[:, h * 512:(h + 1) * 512], onest[0:1, :],
            bdnT[0:1, h * 512:(h + 1) * 512], start=True, stop=True))
    P.inc("tensor", sT)
    P.wait("scalar", sT)
    P.do("scalar", lambda e: e.copy(out=bdnT[:], in_=PS0[:]))
    P.inc("scalar", sA)
    P.wait("tensor", sA)

    P.wait("vector", sLD)
    P.do("vector", lambda e: e.memset(crow[:], 0.0))
    P.inc("vector", sV)
    P.wait("vector", sGA)
    P.wait("vector", sLD)
    for dc in range(4):
        sl = slice(dc * S, (dc + 1) * S)
        P.do("vector", lambda e, sl=sl, dc=dc: e.scalar_tensor_tensor(
            out=big1[:, sl], in0=big1[:, sl], scalar=dvecc[:, dc:dc + 1],
            in1=big0[:, sl], op0=ALU.mult, op1=ALU.add))
        P.do("vector", lambda e, sl=sl: e.tensor_tensor(
            out=big0[:, sl], in0=big1[:, sl], in1=big2s[:, sl], op=ALU.mult))
        P.do("vector", lambda e, sl=sl: e.tensor_copy(out=gtb[:, sl], in_=big0[:, sl]))
    P.inc("vector", sV)

    if "gatedT" in tap_aps:
        P.wait("gpsimd", sV)
        P.dma("gpsimd", tap_aps["gatedT"], big0[:], sD)
        P.dma("gpsimd", tap_aps["ysl"], big1[:], sD)
    P.wait("tensor", sV)
    for g in range(NT):
        P.wait("sync", sV)
        P.wait("sync", sA)
        P.dma("sync", tkA[:], x_in[g * 128:(g + 1) * 128, :], sLD)
        if g > 0:
            P.wait("tensor", sV)
            P.wait("tensor", sA)
        for kc in range(4):
            for q in range(2):
                P.do("tensor", lambda e, g=g, kc=kc, q=q: e.matmul(
                    PS0[:, q * 512:(q + 1) * 512],
                    gtb[:, kc * S + g * 128: kc * S + (g + 1) * 128],
                    wdn[:, kc * H + q * 512: kc * H + (q + 1) * 512],
                    start=(kc == 0), stop=(kc == 3)))
        P.inc("tensor", sT)
        P.wait("vector", sT)
        P.do("vector", lambda e: e.scalar_tensor_tensor(
            out=tkC[:], in0=PS0[:], scalar=1.0, in1=bdnT[:], op0=ALU.mult, op1=ALU.add))
        P.wait("vector", sLD)
        P.wait("scalar", sLD)
        ln_stats(tkA[:])
        ln_apply(tkA[:], tkB[:])
        P.do("vector", lambda e: e.tensor_tensor(out=tkC[:], in0=tkC[:], in1=tkB[:], op=ALU.add))
        ln_stats(tkC[:])
        ln_apply(tkC[:], tkB[:])
        P.do("vector", lambda e: e.tensor_tensor(out=tkA[:], in0=tkA[:], in1=tkB[:], op=ALU.add))
        P.inc("vector", sV)
        P.wait("gpsimd", sV)
        P.dma("gpsimd", x1_scr[g * 128:(g + 1) * 128, :], tkA[:], sD)
        ln_stats(tkA[:])
        ln_apply(tkA[:], tkB[:])
        P.inc("vector", sV)
        P.wait("scalar", sV)
        P.do("scalar", lambda e: e.copy(out=t1bt[:], in_=tkB[:]))
        P.inc("scalar", sA)
        P.wait("gpsimd", sA)
        P.dma("gpsimd", t1_loc[g * 128:(g + 1) * 128, :], t1bt[:], sD)
        P.wait("tensor", sV)
        for hc in range(8):
            pslot = PS2 if hc % 2 == 0 else PS3
            P.do("tensor", lambda e, hc=hc, p=pslot: e.transpose(
                p[:, 0:128], tkB[:, hc * 128:(hc + 1) * 128], identf[:]))
            P.inc("tensor", sT)
            P.wait("scalar", sT)
            P.do("scalar", lambda e, hc=hc, p=pslot: e.copy(
                out=t1Tg[:, hc * 128:(hc + 1) * 128], in_=p[:, 0:128]))
            P.inc("scalar", sA)
            P.wait("tensor", sA)
        for hc in range(8):
            P.do("tensor", lambda e, hc=hc: e.matmul(
                PS1[:, 0:64], t1Tg[:, hc * 128:(hc + 1) * 128],
                rwf[:, hc * E:(hc + 1) * E], start=(hc == 0), stop=(hc == 7)))
        P.inc("tensor", sT)
        C = tkC
        P.wait("scalar", sT)
        P.do("scalar", lambda e: e.activation(out=C[:, 0:64], in_=PS1[:, 0:64], func=AF.Sigmoid))
        P.inc("scalar", sA)
        P.wait("vector", sA)
        P.do("vector", lambda e: e.tensor_reduce(
            out=C[:, 64:72], in_=C[:, 0:64].rearrange("p (g e) -> p g e", e=8),
            axis=AX.X, op=ALU.max))
        P.do("vector", lambda e: e.max(out=C[:, 72:80], in_=C[:, 64:72]))
        P.do("vector", lambda e: e.tensor_scalar(
            out=C[:, 80:88], in0=C[:, 64:72], scalar1=C[:, 75:76], scalar2=None, op0=ALU.is_ge))
        P.do("vector", lambda e: _bctt(e,
            C[:, 88:152].rearrange("p (g e) -> p g e", e=8),
            C[:, 0:64].rearrange("p (g e) -> p g e", e=8),
            C[:, 80:88].rearrange("p (g o) -> p g o", o=1),
            ALU.mult))
        P.do("vector", lambda e: e.max(out=C[:, 152:160], in_=C[:, 88:152]))
        P.do("vector", lambda e: e.max_index(out=ixt[:], in_max=C[:, 152:160], in_values=C[:, 88:152]))
        P.do("vector", lambda e: e.tensor_copy(out=C[:, 170:178], in_=ixt[:]))
        P.do("vector", lambda e: e.tensor_reduce(out=C[:, 160:161], in_=C[:, 152:160], axis=AX.X, op=ALU.add))
        P.do("vector", lambda e: e.tensor_scalar(
            out=C[:, 160:161], in0=C[:, 160:161], scalar1=1e-9, scalar2=None, op0=ALU.max))
        P.do("vector", lambda e: e.reciprocal(out=C[:, 161:162], in_=C[:, 160:161]))
        P.do("vector", lambda e: e.tensor_scalar(
            out=C[:, 162:170], in0=C[:, 152:160], scalar1=C[:, 161:162], scalar2=MOE_SCALE,
            op0=ALU.mult, op1=ALU.mult))
        B_ = tkB
        P.do("vector", lambda e: _bctt(e,
            B_[:, 0:512].rearrange("p (x k) -> p x k", k=8),
            C[:, 170:178].rearrange("p (o k) -> p o k", o=1),
            iotaet[:].rearrange("p (x o) -> p x o", o=1),
            ALU.is_equal))
        P.do("vector", lambda e: e.tensor_reduce(
            out=B_[:, 512:576], in_=B_[:, 0:512].rearrange("p (x k) -> p x k", k=8),
            axis=AX.X, op=ALU.add))
        P.do("vector", lambda e: e.tensor_copy(out=ohb[:], in_=B_[:, 512:576]))
        P.do("vector", lambda e, g=g: e.tensor_copy(out=wvb[:, g * 8:(g + 1) * 8], in_=C[:, 162:170]))
        P.do("vector", lambda e, g=g: e.tensor_copy(out=ixb[:, g * 8:(g + 1) * 8], in_=C[:, 170:178]))
        P.inc("vector", sV)
        P.wait("tensor", sV)
        P.do("tensor", lambda e: e.matmul(PS0[:, 0:64], trit[:], ohb[:], start=True, stop=False))
        P.do("tensor", lambda e: e.matmul(PS0[:, 0:64], onest[0:1, :], crow[0:1, :], start=False, stop=True))
        P.do("tensor", lambda e: e.matmul(PS0[0:1, 64:128], stript[:, 127:128], ohb[:], start=True, stop=True))
        P.inc("tensor", sT)
        P.wait("vector", sT)
        P.do("vector", lambda e: _bctt(e,
            B_[:, 0:512].rearrange("p (k x) -> p k x", x=64),
            C[:, 170:178].rearrange("p (k o) -> p k o", o=1),
            iotaet[:].rearrange("p (o x) -> p o x", o=1),
            ALU.is_equal))
        P.do("vector", lambda e: e.tensor_copy(out=B_[:, 576:640], in_=PS0[:, 0:64]))
        P.do("vector", lambda e: _bctt(e,
            C[:, 256:768].rearrange("p (k x) -> p k x", x=64),
            B_[:, 0:512].rearrange("p (k x) -> p k x", x=64),
            B_[:, 576:640].rearrange("p (o x) -> p o x", o=1),
            ALU.mult))
        P.do("vector", lambda e, g=g: e.tensor_reduce(
            out=posb[:, g * 8:(g + 1) * 8],
            in_=C[:, 256:768].rearrange("p (k x) -> p k x", x=64),
            axis=AX.X, op=ALU.add))
        P.do("vector", lambda e: e.tensor_tensor(
            out=crow[0:1, :], in0=crow[0:1, :], in1=PS0[0:1, 64:128], op=ALU.add))
        P.inc("vector", sV)
        P.wait("tensor", sV)

    P.wait("gpsimd", sV)
    P.wait("gpsimd", sD)
    P.cc("AllGather", ALU.bypass, [t1_loc.opt()], [ag_t1.opt()], sCC)
    P.dma("gpsimd", cnt_loc, crow[0:1, :], sD)
    P.wait("gpsimd", sD)
    P.cc("AllGather", ALU.bypass, [cnt_loc.opt()], [ag_cnt.opt()], sCC)
    P.wait("sync", sCC)
    P.dma("sync", agcS[:], ag_cnt, sLD)
    P.wait("tensor", sLD)
    P.wait("tensor", sV)
    P.wait("tensor", sA)
    P.do("tensor", lambda e: e.matmul(PS0[0:1, 0:64], rkmS[:], agcS[:], start=True, stop=True))
    P.inc("tensor", sT)
    P.wait("scalar", sT)
    P.do("scalar", lambda e: e.copy(out=crow[0:1, :], in_=PS0[0:1, 0:64]))
    P.inc("scalar", sA)
    P.wait("tensor", sA)
    P.do("tensor", lambda e: e.matmul(PS1[:, 0:64], onest[0:1, :], crow[0:1, :], start=True, stop=True))
    P.inc("tensor", sT)
    P.wait("scalar", sT)
    P.do("scalar", lambda e: e.copy(out=brow[:, 0:64], in_=PS1[:, 0:64]))
    P.inc("scalar", sA)

    P.wait("vector", sA)
    for g in range(NT):
        B_ = tkB
        C = tkC
        gs8 = slice(g * 8, (g + 1) * 8)
        P.do("vector", lambda e, gs8=gs8: _bctt(e,
            B_[:, 0:512].rearrange("p (k x) -> p k x", x=64),
            ixb[:, gs8].rearrange("p (k o) -> p k o", o=1),
            iotaet[:].rearrange("p (o x) -> p o x", o=1),
            ALU.is_equal))
        P.do("vector", lambda e: _bctt(e,
            C[:, 0:512].rearrange("p (k x) -> p k x", x=64),
            B_[:, 0:512].rearrange("p (k x) -> p k x", x=64),
            brow[:, 0:64].rearrange("p (o x) -> p o x", o=1),
            ALU.mult))
        P.do("vector", lambda e: e.tensor_reduce(
            out=C[:, 512:520], in_=C[:, 0:512].rearrange("p (k x) -> p k x", x=64),
            axis=AX.X, op=ALU.add))
        P.do("vector", lambda e, gs8=gs8: e.tensor_tensor(
            out=C[:, 520:528], in0=posb[:, gs8], in1=C[:, 512:520], op=ALU.add))
        P.do("vector", lambda e: e.tensor_scalar(
            out=C[:, 528:536], in0=C[:, 520:528], scalar1=float(CAP), scalar2=None, op0=ALU.is_lt))
        P.do("vector", lambda e, gs8=gs8: e.tensor_tensor(
            out=C[:, 536:544], in0=wvb[:, gs8], in1=C[:, 528:536], op=ALU.mult))
        P.do("vector", lambda e: e.tensor_scalar(
            out=C[:, 544:552], in0=C[:, 520:528], scalar1=float(CAP - 1), scalar2=None, op0=ALU.min))
        P.do("vector", lambda e, gs8=gs8: e.scalar_tensor_tensor(
            out=C[:, 552:560], in0=ixb[:, gs8], scalar=float(CAP), in1=C[:, 544:552],
            op0=ALU.mult, op1=ALU.add))
        P.do("vector", lambda e: e.tensor_scalar(
            out=C[:, 560:568], in0=C[:, 552:560], scalar1=1.0, scalar2=None, op0=ALU.add))
        P.do("vector", lambda e: e.tensor_tensor(
            out=C[:, 560:568], in0=C[:, 560:568], in1=C[:, 528:536], op=ALU.mult))
        P.do("vector", lambda e: e.tensor_scalar(
            out=C[:, 560:568], in0=C[:, 560:568], scalar1=-1.0, scalar2=None, op0=ALU.add))
        P.do("vector", lambda e, g=g: e.tensor_scalar(
            out=tokcol[:, 0:1], in0=tok0S[:, 0:1], scalar1=float(g * 128 + 1), scalar2=None, op0=ALU.add))
        P.do("vector", lambda e: e.tensor_scalar(
            out=C[:, 568:576], in0=C[:, 552:560], scalar1=0.0, scalar2=tokcol[:, 0:1],
            op0=ALU.mult, op1=ALU.add))
        P.do("vector", lambda e: e.tensor_copy(
            out=ptile[:].rearrange("p (k c) -> p k c", c=2)[:, :, 0:1],
            in_=C[:, 568:576].rearrange("p (k o) -> p k o", o=1)))
        P.do("vector", lambda e: e.tensor_copy(
            out=ptile[:].rearrange("p (k c) -> p k c", c=2)[:, :, 1:2],
            in_=C[:, 536:544].rearrange("p (k o) -> p k o", o=1)))
        P.inc("vector", sV)
        P.wait("gpsimd", sV)
        gsl = slice(g * 1024, (g + 1) * 1024)
        P.dma("gpsimd", rec_islot[0, gsl].rearrange("(t k) -> t k", k=8), C[:, 560:568], sD)
        P.dma("gpsimd", rec_slotc[0, gsl].rearrange("(t k) -> t k", k=8), C[:, 552:560], sD)
        P.dma("gpsimd", rec_wv[0, gsl].rearrange("(t k) -> t k", k=8), C[:, 536:544], sD)
        P.dma("gpsimd", rec_pairs[gsl, :].rearrange("(t k) c -> t (k c)", k=128), ptile[:], sD)
        P.wait("vector", sD)

    if "recs" in tap_aps:
        P.wait("gpsimd", sD)
        P.dma("gpsimd", tap_aps["recs"], rec_islot, sD)
        P.dma("gpsimd", tap_aps["recw"], rec_wv, sD)
        P.dma("gpsimd", tap_aps["recc"], rec_slotc, sD)
    if "x1" in tap_aps:
        P.wait("gpsimd", sD)
        P.dma("gpsimd", tap_aps["x1"], x1_scr, sD)
    if "t1" in tap_aps:
        P.wait("gpsimd", sD)
        P.dma("gpsimd", tap_aps["t1"], t1_loc, sD)

    if upto == "router":
        P.wait("gpsimd", sD)
        stack.close()
        _emit(nc, P)
        return nc

    P.fullbar(sOUT)
    sbfree("tok0S", "tokcol", "ptile", "t1bt",
           "ixt", "rkmS", "agcS", "brow", "crow", "ohb", "wvb", "ixb",
           "posb", "rwf", "t1Tg", "gtb", "bdnT", "wdn", "big2s")
    isrc = sbuf("isrc", [128, 128])
    idx32 = sbuf("idx32", [128, 128], mybir.dt.int32)
    pairsT = sbuf("pairsT", [128, 256])
    P.wait("vector", sA)
    P.do("vector", lambda e: e.memset(tkC[:, 0:520], 0.0))
    P.inc("vector", sV)
    P.wait("gpsimd", sV)
    P.wait("gpsimd", sD)
    P.dma("gpsimd", T_loc.rearrange("(p a) c -> p (a c)", p=128), tkC[:, 0:516], sD)
    P.wait("sync", sD)
    P.dma("sync", isrc[:], rec_islot[0, :].rearrange("(p j) -> p j", p=128), sLD)
    P.dma("sync", pairsT[:], rec_pairs[:, :].rearrange("(p j) c -> p (j c)", p=128), sLD)
    P.wait("vector", sLD)
    P.do("vector", lambda e: e.tensor_scalar(
        out=tkA[:, 0:128], in0=isrc[:], scalar1=0.0, scalar2=None, op0=ALU.is_lt))
    P.do("vector", lambda e: e.scalar_tensor_tensor(
        out=tkA[:, 0:128], in0=tkA[:, 0:128], scalar=float(E * CAP + 100 + 1), in1=isrc[:],
        op0=ALU.mult, op1=ALU.add))
    P.do("vector", lambda e: e.tensor_copy(out=idx32[:], in_=tkA[:, 0:128]))
    P.inc("vector", sV)
    P.wait("gpsimd", sV)
    P.wait("gpsimd", sD)
    for j in range(128):
        sGA.n += 16

        def fsc(e, j=j):
            return e.indirect_dma_start(
                out=T_loc, out_offset=IndirectOffsetOnAxis(ap=idx32[:, j:j + 1], axis=0),
                in_=pairsT[:, 2 * j:2 * j + 2], in_offset=None,
                compute_op=ALU.add,
            ).then_inc(sGA.h, 16)
        P.do("gpsimd", fsc)
    P.wait("gpsimd", sGA)
    P.cc("AllReduce", ALU.add, [T_loc.opt()], [T_glob.opt()], sCC)
    P.wait("gpsimd", sCC)
    if "Tglob" in tap_aps:
        P.dma("gpsimd", tap_aps["Tglob"], T_glob, sD)

    if upto == "table":
        P.wait("gpsimd", sD)
        stack.close()
        _emit(nc, P)
        return nc

    wge = sbuf("wge", [128, 8 * DI], BF16)
    wue = sbuf("wue", [128, 8 * DI], BF16)
    wde = sbuf("wde", [128, 4 * H], BF16)
    dispT = sbuf("dispT", [128, 8 * CAP], BF16)
    gub = sbuf("gub", [128, 4 * CAP], BF16)
    twt = sbuf("twt", [128, 8])
    eob = sbuf("eob", [128, H], BF16)
    exidx = sbuf("exidx", [128, 1], mybir.dt.int32)
    exf = sbuf("exf", [128, 1])
    dtok = sbuf("dtok", [128, H], BF16)

    for e_i in range(EPC):
        for c4 in range(8):
            wconv(wge, c4 * DI, gate_w[e_i * H + c4 * 128: e_i * H + (c4 + 1) * 128, :], DI)
            wconv(wue, c4 * DI, up_w[e_i * H + c4 * 128: e_i * H + (c4 + 1) * 128, :], DI)
        for c4 in range(4):
            wconv(wde, c4 * H, down_w[e_i * DI + c4 * 128: e_i * DI + (c4 + 1) * 128, :], H)
        for sc in range(4):
            P.wait("vector", sGA)
            P.wait("vector", sD)
            P.do("vector", lambda e, e_i=e_i, sc=sc: e.tensor_scalar(
                out=exf[:, 0:1], in0=iot128[:, 0:1], scalar1=float(128 * sc), scalar2=None, op0=ALU.add))
            P.do("vector", lambda e, e_i=e_i: e.tensor_tensor(
                out=exf[:, 0:1], in0=exf[:, 0:1], in1=ebaset[:, e_i:e_i + 1], op=ALU.add))
            P.do("vector", lambda e: e.tensor_copy(out=exidx[:], in_=exf[:, 0:1]))
            P.inc("vector", sV)
            P.wait("gpsimd", sV)
            igather("gpsimd", twt[:, 2 * sc:2 * sc + 2], T_glob, exidx[:, 0:1], sGA)
            P.wait("vector", sGA)
            P.do("vector", lambda e, sc=sc: e.tensor_scalar(
                out=exf[:, 0:1], in0=twt[:, 2 * sc:2 * sc + 1], scalar1=-1.0, scalar2=0.0,
                op0=ALU.add, op1=ALU.max))
            P.do("vector", lambda e: e.tensor_copy(out=exidx[:], in_=exf[:, 0:1]))
            P.inc("vector", sV)
            P.wait("gpsimd", sV)
            P.wait("gpsimd", sT)
            igather("gpsimd", dtok[:], ag_t1, exidx[:, 0:1], sGA)
            P.wait("tensor", sGA)
            P.wait("tensor", sA)
            for hc in range(8):
                pslot = PS2 if hc % 2 == 0 else PS3
                P.do("tensor", lambda e, hc=hc, p=pslot: e.transpose(
                    p.ap().bitcast(BF16)[:, 0:128],
                    dtok[:, hc * 128:(hc + 1) * 128], identt[:]))
                P.inc("tensor", sT)
                P.wait("scalar", sT)
                P.do("scalar", lambda e, hc=hc, sc=sc, p=pslot: e.copy(
                    out=dispT[:, hc * CAP + sc * 128: hc * CAP + (sc + 1) * 128],
                    in_=p.ap().bitcast(BF16)[:, 0:128]))
                P.inc("scalar", sA)
                P.wait("tensor", sA)
        for ic in range(4):
            P.wait("tensor", sA)
            P.wait("tensor", sV)
            for hc in range(8):
                P.do("tensor", lambda e, ic=ic, hc=hc: e.matmul(
                    PS0[:, 0:512], wge[:, hc * DI + ic * 128: hc * DI + (ic + 1) * 128],
                    dispT[:, hc * CAP:(hc + 1) * CAP], start=(hc == 0), stop=(hc == 7)))
            for hc in range(8):
                P.do("tensor", lambda e, ic=ic, hc=hc: e.matmul(
                    PS1[:, 0:512], wue[:, hc * DI + ic * 128: hc * DI + (ic + 1) * 128],
                    dispT[:, hc * CAP:(hc + 1) * CAP], start=(hc == 0), stop=(hc == 7)))
            P.inc("tensor", sT)
            P.wait("scalar", sT)
            P.do("scalar", lambda e: e.activation(out=tkA[:, 0:512], in_=PS0[:, 0:512], func=AF.Silu))
            P.inc("scalar", sA)
            P.wait("vector", sA)
            P.do("vector", lambda e: e.tensor_tensor(out=tkB[:, 0:512], in0=tkA[:, 0:512], in1=PS1[:, 0:512], op=ALU.mult))
            P.do("vector", lambda e, ic=ic: e.tensor_copy(out=gub[:, ic * CAP:(ic + 1) * CAP], in_=tkB[:, 0:512]))
            P.inc("vector", sV)
            P.wait("tensor", sV)
        for sc in range(4):
            P.wait("tensor", sA)
            for ic in range(4):
                for q in range(2):
                    P.do("tensor", lambda e, sc=sc, ic=ic, q=q: e.matmul(
                        PS2[:, q * 512:(q + 1) * 512],
                        gub[:, ic * CAP + sc * 128: ic * CAP + (sc + 1) * 128],
                        wde[:, ic * H + q * 512: ic * H + (q + 1) * 512],
                        start=(ic == 0), stop=(ic == 3)))
            P.inc("tensor", sT)
            P.wait("scalar", sT)
            P.wait("scalar", sGA)
            P.do("scalar", lambda e: e.copy(out=eob[:], in_=PS2[:, 0:1024]))
            P.inc("scalar", sA)
            P.wait("gpsimd", sA)
            P.dma("gpsimd", eo_loc[e_i * CAP + sc * 128: e_i * CAP + (sc + 1) * 128, :], eob[:], sD)
            P.wait("tensor", sA)
        P.wait("sync", sA)
    P.wait("gpsimd", sD)
    P.cc("AllGather", ALU.bypass, [eo_loc.opt()], [ag_eo.opt()], sCC)
    P.wait("gpsimd", sCC)
    if "eo" in tap_aps:
        P.dma("gpsimd", tap_aps["eo"], eo_loc, sD)
    if upto == "expert":
        P.wait("gpsimd", sD)
        stack.close()
        _emit(nc, P)
        return nc

    P.fullbar(sOUT)
    sbfree("dtok", "exf", "exidx", "eob", "twt", "gub", "dispT", "wde", "wue", "wge",
           "pairsT", "idx32", "isrc")
    sct = sbuf("sct", [128, 128])
    sci = sbuf("sci", [128, 128], mybir.dt.int32)
    wvT = sbuf("wvT", [128, 128])
    ctr = sbuf("ctr", [128, H])
    cgt = sbuf("cgt", [128, H], BF16)
    wj = sbuf("wj", [128, 128], BF16)
    wshg = sbuf("wshg", [128, 8 * DI], BF16)
    wshu = sbuf("wshu", [128, 8 * DI], BF16)
    wsd = sbuf("wsd", [128, 4 * H], BF16)
    t1hb = sbuf("t1hb", [128, H], BF16)
    sgb = sbuf("sgb", [128, 4 * 128], BF16)

    for c4 in range(8):
        wconv(wshg, c4 * DI, sh_g[c4 * 128:(c4 + 1) * 128, :], DI)
        wconv(wshu, c4 * DI, sh_u[c4 * 128:(c4 + 1) * 128, :], DI)
    for c4 in range(4):
        wconv(wsd, c4 * H, sh_d[c4 * 128:(c4 + 1) * 128, :], H)
    P.wait("sync", sD)
    P.dma("sync", sct[:], rec_slotc[0, :].rearrange("(p j) -> p j", p=128), sLD)
    P.dma("sync", wvT[:], rec_wv[0, :].rearrange("(p j) -> p j", p=128), sLD)
    P.wait("tensor", sLD)
    P.wait("tensor", sA)
    P.do("tensor", lambda e: e.transpose(PS3[:, 0:128], sct[:], identf[:]))
    P.inc("tensor", sT)
    P.wait("scalar", sT)
    P.do("scalar", lambda e: e.copy(out=sct[:], in_=PS3[:, 0:128]))
    P.inc("scalar", sA)
    P.wait("tensor", sA)
    P.do("tensor", lambda e: e.transpose(PS3[:, 0:128], wvT[:], identf[:]))
    P.inc("tensor", sT)
    P.wait("scalar", sT)
    P.do("scalar", lambda e: e.copy(out=wvT[:], in_=PS3[:, 0:128]))
    P.inc("scalar", sA)
    P.wait("vector", sA)
    P.do("vector", lambda e: e.tensor_copy(out=sci[:], in_=sct[:]))
    P.inc("vector", sV)

    for g in range(NT):
        for jj in range(8):
            j = g * 8 + jj
            P.wait("gpsimd", sV)
            P.wait("gpsimd", sA)
            P.wait("gpsimd", sT)
            igather("gpsimd", cgt[:], ag_eo, sci[:, j:j + 1], sGA)
            P.wait("vector", sGA)
            P.do("vector", lambda e: e.memset(wj[:], 0.0))
            P.do("vector", lambda e, j=j, jj=jj: e.tensor_scalar(
                out=wj[:, 16 * jj:16 * (jj + 1)], in0=bmaskt[:],
                scalar1=wvT[:, j:j + 1], scalar2=None, op0=ALU.mult))
            P.inc("vector", sV)
            P.wait("tensor", sV)
            P.wait("tensor", sGA)
            if jj == 0:
                P.wait("tensor", sA)
            for q in range(2):
                P.do("tensor", lambda e, q=q, jj=jj: e.matmul(
                    PS0[:, q * 512:(q + 1) * 512], wj[:],
                    cgt[:, q * 512:(q + 1) * 512], start=(jj == 0), stop=(jj == 7)))
            P.inc("tensor", sT)
        P.wait("scalar", sT)
        P.do("scalar", lambda e: e.copy(out=ctr[:], in_=PS0[:]))
        P.inc("scalar", sA)
        P.wait("sync", sA)
        P.wait("sync", sV)
        P.wait("sync", sT)
        P.dma("sync", cgt[:], t1_loc[g * 128:(g + 1) * 128, :], sLD)
        P.wait("tensor", sLD)
        for hc in range(8):
            pslot = PS2 if hc % 2 == 0 else PS3
            P.do("tensor", lambda e, hc=hc, p=pslot: e.transpose(
                p.ap().bitcast(BF16)[:, 0:128], cgt[:, hc * 128:(hc + 1) * 128], identt[:]))
            P.inc("tensor", sT)
            P.wait("scalar", sT)
            P.do("scalar", lambda e, hc=hc, p=pslot: e.copy(
                out=t1hb[:, hc * 128:(hc + 1) * 128], in_=p.ap().bitcast(BF16)[:, 0:128]))
            P.inc("scalar", sA)
            P.wait("tensor", sA)
        for ic in range(4):
            for hc in range(8):
                P.do("tensor", lambda e, ic=ic, hc=hc: e.matmul(
                    PS2[:, 0:128], wshg[:, hc * DI + ic * 128: hc * DI + (ic + 1) * 128],
                    t1hb[:, hc * 128:(hc + 1) * 128], start=(hc == 0), stop=(hc == 7)))
            for hc in range(8):
                P.do("tensor", lambda e, ic=ic, hc=hc: e.matmul(
                    PS3[:, 0:128], wshu[:, hc * DI + ic * 128: hc * DI + (ic + 1) * 128],
                    t1hb[:, hc * 128:(hc + 1) * 128], start=(hc == 0), stop=(hc == 7)))
            P.inc("tensor", sT)
            P.wait("scalar", sT)
            P.do("scalar", lambda e: e.activation(out=tkA[:, 0:128], in_=PS2[:, 0:128], func=AF.Silu))
            P.inc("scalar", sA)
            P.wait("vector", sA)
            P.do("vector", lambda e: e.tensor_tensor(out=tkA[:, 128:256], in0=tkA[:, 0:128], in1=PS3[:, 0:128], op=ALU.mult))
            P.do("vector", lambda e, ic=ic: e.tensor_copy(out=sgb[:, ic * 128:(ic + 1) * 128], in_=tkA[:, 128:256]))
            P.inc("vector", sV)
            P.wait("tensor", sV)
        for ic in range(4):
            for q in range(2):
                P.do("tensor", lambda e, ic=ic, q=q: e.matmul(
                    PS1[:, q * 512:(q + 1) * 512], sgb[:, ic * 128:(ic + 1) * 128],
                    wsd[:, ic * H + q * 512: ic * H + (q + 1) * 512],
                    start=(ic == 0), stop=(ic == 3)))
        P.inc("tensor", sT)
        P.wait("sync", sV)
        P.wait("sync", sD)
        P.dma("sync", tkA[:], x1_scr[g * 128:(g + 1) * 128, :], sLD)
        P.wait("vector", sT)
        P.do("vector", lambda e: e.tensor_tensor(out=tkC[:], in0=ctr[:], in1=PS1[:], op=ALU.add))
        ln_stats(tkC[:])
        ln_apply(tkC[:], tkB[:])
        P.wait("vector", sLD)
        P.do("vector", lambda e: e.tensor_tensor(out=tkC[:], in0=tkA[:], in1=tkB[:], op=ALU.add))
        P.inc("vector", sV)
        P.wait("gpsimd", sV)
        P.dma("gpsimd", out[g * 128:(g + 1) * 128, :], tkC[:], sFIN)
        P.wait("tensor", sV)
        P.wait("scalar", sLD)

    P.wait("gpsimd", sFIN)
    stack.close()
    _emit(nc, P)
    return nc


def _emit(nc, P):
    with nc.Block() as block:
        @block.sync
        def _(eng):
            for f in P.ops["sync"]:
                f(eng)

        @block.scalar
        def _(eng):
            for f in P.ops["scalar"]:
                f(eng)

        @block.vector
        def _(eng):
            for f in P.ops["vector"]:
                f(eng)

        @block.tensor
        def _(eng):
            for f in P.ops["tensor"]:
                f(eng)

        @block.gpsimd
        def _(eng):
            for f in P.ops["gpsimd"]:
                f(eng)


def _consts():
    ident = np.eye(128, dtype=bfdt)
    strip = np.zeros((128, 255), dtype=bfdt)
    strip[:, 127] = 1
    tri = np.triu(np.ones((128, 128), np.float32), 1).astype(bfdt)
    ones = np.ones((1, 128), np.float32)
    iotae = np.tile(np.arange(E, dtype=np.float32)[None, :], (128, 1))
    bmask = np.zeros((128, 16), np.float32)
    for r in range(128):
        bmask[r, r // 8] = 1
    return ident, strip, tri, ones, iotae, bmask.astype(bfdt)


def make_in_maps(inputs):
    f32 = np.float32
    x = np.ascontiguousarray(np.asarray(inputs["L0_attn_x"], f32).reshape(B * L, H))
    ident, strip, tri, ones, iotae, bmask = _consts()
    al = np.asarray(inputs["L0_attn_A_log"], f32)
    in_maps = []
    for core in range(NCORES):
        t0c = core * S
        halo = np.zeros((2, H), f32)
        if core % 4 != 0:
            halo[:] = x[t0c - 2:t0c]
        dblk = core % 4
        m = {
            "x_slice": np.ascontiguousarray(x[t0c:t0c + S]),
            "x_halo": halo,
            "ln_gamma": np.asarray(inputs["L0_attn_ln_gamma"], f32).reshape(1, H),
            "ln_beta": np.asarray(inputs["L0_attn_ln_beta"], f32).reshape(1, H),
            "w_up": np.ascontiguousarray(np.asarray(inputs["L0_attn_W_up"], f32)),
            "b_up": np.asarray(inputs["L0_attn_b_up"], f32).reshape(1, H),
            "conv_w": np.ascontiguousarray(np.asarray(inputs["L0_attn_conv_w"], f32).reshape(DI, 3)),
            "conv_b": np.asarray(inputs["L0_attn_conv_b"], f32).reshape(DI, 1),
            "w_delta": np.ascontiguousarray(np.asarray(inputs["L0_attn_W_delta"], f32)),
            "b_delta": np.asarray(inputs["L0_attn_b_delta"], f32).reshape(DI, 1),
            "w_B": np.ascontiguousarray(np.asarray(inputs["L0_attn_W_B"], f32)),
            "w_C": np.ascontiguousarray(np.asarray(inputs["L0_attn_W_C"], f32)),
            "a_log_blk": np.ascontiguousarray(al[dblk * 128:(dblk + 1) * 128]),
            "d_vec": np.asarray(inputs["L0_attn_D"], f32).reshape(DI, 1),
            "w_down": np.ascontiguousarray(np.asarray(inputs["L0_attn_W_down"], f32)),
            "b_down": np.asarray(inputs["L0_attn_b_down"], f32).reshape(1, H),
            "router_w": np.ascontiguousarray(np.asarray(inputs["L0_ffn_router_w"], f32)),
            "gate_w_blk": np.ascontiguousarray(
                np.asarray(inputs["L0_ffn_gate_w"], f32)[core * EPC:(core + 1) * EPC].reshape(EPC * H, DI)),
            "up_w_blk": np.ascontiguousarray(
                np.asarray(inputs["L0_ffn_up_w"], f32)[core * EPC:(core + 1) * EPC].reshape(EPC * H, DI)),
            "down_w_blk": np.ascontiguousarray(
                np.asarray(inputs["L0_ffn_down_w"], f32)[core * EPC:(core + 1) * EPC].reshape(EPC * DI, H)),
            "sh_gate": np.ascontiguousarray(np.asarray(inputs["L0_ffn_shared_gate_w"], f32)),
            "sh_up": np.ascontiguousarray(np.asarray(inputs["L0_ffn_shared_up_w"], f32)),
            "sh_down": np.ascontiguousarray(np.asarray(inputs["L0_ffn_shared_down_w"], f32)),
            "c_ident": ident, "c_strip": strip, "c_tri": tri, "c_ones": ones,
            "c_iotae": iotae, "c_bmask": bmask,
        }
        m["halo_mask"] = np.full((128, 1), 0.0 if core % 4 == 0 else 1.0, f32)
        rm = np.zeros((NCORES, 1), f32)
        rm[:core] = 1.0
        m["rank_mask"] = rm
        m["tok0"] = (core * S + np.arange(128, dtype=f32))[:, None].copy()

        bat = core // 4
        gf = []
        for rr in range(4):
            for arr in range(2):
                base = (4 * bat + rr) * 2 * DI + arr * DI + dblk * 128
                gf.append(base + np.arange(128))
        m["gidx_f32"] = np.stack(gf, axis=1).astype(np.int32)
        gb = []
        for rr in range(4):
            for tau in range(2):
                for arr in range(2):
                    base = (4 * bat + rr) * 2 * NS + arr * NS + tau * 128
                    gb.append(base + np.arange(128))
        m["gidx_bf"] = np.stack(gb, axis=1).astype(np.int32)
        gy = []
        for rr in range(4):
            rows = ((4 * bat + rr) * 128 + np.arange(128)) * 4 + (core % 4)
            gy.append(rows)
        m["gidx_ys"] = np.stack(gy, axis=1).astype(np.int32)
        m["iota128"] = np.arange(128, dtype=f32)[:, None].copy()
        m["ebase"] = np.tile((512.0 * (core * EPC + np.arange(8))).astype(f32)[None, :], (128, 1))
        in_maps.append(m)
    return in_maps


def kernel(**inputs):
    in_maps = make_in_maps(inputs)
    nc = build_nc(0)
    res = run_bass_kernel_spmd(nc, in_maps, core_ids=list(range(NCORES)))
    outs = [res.results[c]["out"] for c in range(NCORES)]
    return np.concatenate(outs, 0).reshape(B, L, H)


def run_partial(inputs, upto, taps=()):
    in_maps = make_in_maps(inputs)
    nc = build_nc(0, taps=taps, upto=upto)
    res = run_bass_kernel_spmd(nc, in_maps, core_ids=list(range(NCORES)))
    return res


# version 05
# speedup vs baseline: 1.1216x; 1.1178x over previous
import contextlib
import numpy as np
import ml_dtypes
import concourse.bass as bass
import concourse.mybir as mybir
from concourse.bass_utils import run_bass_kernel_spmd
from concourse.bass_core_rust import IndirectOffsetOnAxis

F32 = mybir.dt.float32
BF16 = mybir.dt.bfloat16
I16 = mybir.dt.int16
U16 = mybir.dt.uint16
AF = mybir.ActivationFunctionType
ALU = mybir.AluOpType
AX = mybir.AxisListType

NCORES = 8
B, L, H = 2, 8192, 1024
DI, NS = 512, 256
S = 2048
NT = S // 128
E, CAP, TOPK, G, TG = 64, 512, 8, 8, 4
MOE_SCALE = 2.5
EPC = E // NCORES
CH = 1024
NCH = L // CH
LNEPS = 1e-5

bfdt = ml_dtypes.bfloat16


class Sem:
    def __init__(self, h):
        self.h = h
        self.n = 0


class Prog:
    ENGINES = ("sync", "scalar", "vector", "tensor", "gpsimd")

    AUTO = ("scalar", "vector")

    def __init__(self):
        self.ops = {e: [] for e in self.ENGINES}
        self.last_inc = {e: None for e in self.ENGINES}
        self.chain = {}

    def do(self, eng, fn, auto=True):
        if auto and eng in self.AUTO and eng in self.chain:
            ch = self.chain[eng]
            prev = ch.n
            ch.n += 1

            def f(e, fn=fn, ch=ch, prev=prev):
                if prev > 0:
                    e.wait_ge(ch.h, prev)
                return fn(e).then_inc(ch.h, 1)
            f._wrapped = True
            self.ops[eng].append(f)
            self.last_inc[eng] = (ch, ch.n)
        else:
            self.ops[eng].append(fn)

    def dma(self, eng, out, in_, sem, slow=False):
        sem.n += 16

        def f(e, out=out, in_=in_, s=sem, slow=slow):
            if slow:
                e.dma_start(out=out, in_=in_, allow_slow_non_contiguous=True).then_inc(s.h, 16)
            else:
                e.dma_start(out=out, in_=in_).then_inc(s.h, 16)
        self.ops[eng].append(f)

    def cc(self, kind, op, ins, outs, sem):
        sem.n += 1

        def f(e, k=kind, op=op, ins=ins, outs=outs, s=sem):
            e.collective_compute(k, op, [list(range(NCORES))], ins, outs).then_inc(s.h, 1)
        self.ops["gpsimd"].append(f)

    def inc(self, eng, sem, n=1):
        ops = self.ops[eng]
        if eng in self.AUTO and eng in self.chain:
            if sem is self.chain[eng]:
                return
            ch = self.chain[eng]
            v = ch.n
            sem.n += n

            def f0(e, ch=ch, v=v, s=sem, n=n):
                if v > 0:
                    e.wait_ge(ch.h, v)
                return e.sem_inc(s.h, n)
            ops.append(f0)
            return
        sem.n += n
        if not ops:
            def fz(e, s=sem, n=n):
                return e.sem_inc(s.h, n)
            ops.append(fz)
            return
        last = ops[-1]
        if getattr(last, "_wrapped", False):
            li = self.last_inc.get(eng)
            lv = li[1] if li else 0
            lsem = li[0] if li else None

            def f1(e, s=sem, n=n, lsem=lsem, lv=lv):
                if lsem is not None and lv > 0:
                    e.wait_ge(lsem.h, lv)
                return e.sem_inc(s.h, n)
            ops.append(f1)
            return

        def f(e, last=last, s=sem, n=n):
            r = last(e)
            try:
                return r.then_inc(s.h, n)
            except Exception:
                return e.sem_inc(s.h, n)
        f._wrapped = True
        ops[-1] = f
        self.last_inc[eng] = (sem, sem.n)

    def wait(self, eng, sem, val=None):
        v = sem.n if val is None else val

        def f(e, s=sem, v=v):
            e.wait_ge(s.h, v)
        self.ops[eng].append(f)

    def fullbar(self, sem):
        for e in self.ENGINES:
            ops = self.ops[e]
            if ops and not getattr(ops[-1], "_wrapped", False):
                self.inc(e, sem)
            else:
                li = self.last_inc[e]
                if li is not None:
                    self.wait(e, li[0], li[1])
                sem.n += 1

                def f0(e2, s=sem):
                    return e2.sem_inc(s.h, 1)
                ops.append(f0)
        for e in self.ENGINES:
            self.wait(e, sem)



def _bctt(e, out, in0, in1, op):
    a, b = bass.broadcast_tensor_aps(in0, in1)
    return e.tensor_tensor(out=out, in0=a, in1=b, op=op)


def build_nc(core, taps=(), upto="all"):
    rank = core
    batch = core // 4
    dblk = core % 4
    batch_start = (core % 4 == 0)

    nc = bass.Bass("TRN2", target_bir_lowering=False, detect_race_conditions=False)

    def din(name, shape, dt=F32):
        return nc.dram_tensor(name, shape, dt, kind="ExternalInput").ap()

    def dout(name, shape, dt=F32):
        return nc.dram_tensor(name, shape, dt, kind="ExternalOutput").ap()

    def dint(name, shape, dt=F32):
        return nc.dram_tensor(name, shape, dt, kind="Internal").ap()

    def dsh(name, shape, dt=F32):
        return nc.dram_tensor(name, shape, dt, kind="Internal",
                              addr_space="Shared").ap()

    x_in = din("x_slice", [S, H])
    x_halo = din("x_halo", [2, H])
    ln_g = din("ln_gamma", [1, H])
    ln_b = din("ln_beta", [1, H])
    w_up = din("w_up", [H, H])
    b_up = din("b_up", [1, H])
    conv_w = din("conv_w", [DI, 3])
    conv_b = din("conv_b", [DI, 1])
    w_delta = din("w_delta", [DI, DI])
    b_delta = din("b_delta", [DI, 1])
    w_B = din("w_B", [DI, NS])
    w_C = din("w_C", [DI, NS])
    a_log = din("a_log_blk", [128, NS])
    d_vec = din("d_vec", [DI, 1])
    w_down = din("w_down", [DI, H])
    b_down = din("b_down", [1, H])
    router_w = din("router_w", [H, E])
    gate_w = din("gate_w_blk", [EPC * H, DI])
    up_w = din("up_w_blk", [EPC * H, DI])
    down_w = din("down_w_blk", [EPC * DI, H])
    sh_g = din("sh_gate", [H, DI])
    sh_u = din("sh_up", [H, DI])
    sh_d = din("sh_down", [DI, H])
    halom = din("halo_mask", [128, 1])
    rankmask_in = din("rank_mask", [NCORES, 1])
    tok0_in = din("tok0", [128, 1])
    I32 = mybir.dt.int32
    gidx_f32 = din("gidx_f32", [128, 8], I32)
    gidx_bf = din("gidx_bf", [128, 16], I32)
    gidx_ys = din("gidx_ys", [128, 4], I32)
    iota128 = din("iota128", [128, 1])
    ebaset_in = din("ebase", [128, 8])
    ident = din("c_ident", [128, 128], BF16)
    strip = din("c_strip", [128, 255], BF16)
    tri = din("c_tri", [128, 128], BF16)
    onesb = din("c_ones", [1, 128], F32)
    iotae = din("c_iotae", [128, E])
    bmask = din("c_bmask", [128, 16], BF16)

    out = dout("out", [S, H])

    pre_f32 = dint("pre_f32", [2 * DI, S])
    ag_f32 = dsh("ag_f32", [NCORES * 2 * DI, S])
    pre_bf = dint("pre_bf", [2 * NS, S], BF16)
    ag_bf = dsh("ag_bf", [NCORES * 2 * NS, S], BF16)
    uT_scr = dint("uT_scr", [DI, S])
    srT_scr = dint("srT_scr", [DI, S])
    ysT_loc = dint("ysT_loc", [128, L])
    ag_ys = dsh("ag_ys", [NCORES * 128, L])
    x1_scr = dint("x1_scr", [S, H])
    t1_loc = dint("t1_loc", [S, H], BF16)
    ag_t1 = dsh("ag_t1", [B * L, H], BF16)
    cnt_loc = dint("cnt_loc", [1, E])
    ag_cnt = dsh("ag_cnt", [NCORES, E])
    rec_islot = dint("rec_islot", [1, S * TOPK])
    rec_slotc = dint("rec_slotc", [1, S * TOPK])
    rec_wv = dint("rec_wv", [1, S * TOPK])
    rec_pairs = dint("rec_pairs", [S * TOPK, 2])
    T_loc = dint("T_loc", [E * CAP + 256, 2])
    T_glob = dint("T_glob", [E * CAP + 256, 2])
    T_all = dint("T_all", [NCORES * (E * CAP + 256), 2])
    eo_loc = dint("eo_loc", [EPC * CAP, H], BF16)
    ag_eo = dsh("ag_eo", [E * CAP, H], BF16)

    tap_aps = {}
    for tname, tshape, tdt in taps:
        tap_aps[tname] = dout("tap_" + tname, tshape, tdt)

    P = Prog()
    stack = contextlib.ExitStack()
    _cms = {}

    def sbuf(name, shape, dt=F32):
        cm = nc.sbuf_tensor(name, shape, dt)
        t = cm.__enter__()
        _cms[name] = cm
        return t

    def sbfree(*names):
        for n in names:
            _cms.pop(n).__exit__(None, None, None)

    def psum(name, shape, dt=F32):
        return stack.enter_context(nc.psum_tensor(name, shape, dt))

    def mksem(name):
        return Sem(stack.enter_context(nc.semaphore(name)))

    tkA = sbuf("tkA", [128, H])
    tkB = sbuf("tkB", [128, H])
    tkC = sbuf("tkC", [128, H])
    sc1 = sbuf("sc1", [128, 1])
    sc2 = sbuf("sc2", [128, 1])
    sc3 = sbuf("sc3", [128, 1])
    sc4 = sbuf("sc4", [128, 1])
    sqsc = sbuf("sqsc", [128, H])
    big0 = sbuf("big0", [128, 4 * S])
    big1 = sbuf("big1", [128, 4 * S])
    wdl = sbuf("wdl", [128, 4 * DI], BF16)
    wBc = sbuf("wBc", [128, 4 * NS], BF16)
    wCc = sbuf("wCc", [128, 4 * NS], BF16)
    convw = sbuf("convw", [128, 12])
    convbc = sbuf("convbc", [128, 4])
    bupc = sbuf("bupc", [128, 8])
    bdlc = sbuf("bdlc", [128, 4])
    bdnc = sbuf("bdnc", [128, 8])
    dvecc = sbuf("dvecc", [128, 4])
    identt = sbuf("identt", [128, 128], BF16)
    identf = sbuf("identf", [128, 128])
    stript = sbuf("stript", [128, 255], BF16)
    trit = sbuf("trit", [128, 128], BF16)
    onest = sbuf("onest", [1, 128])
    iotaet = sbuf("iotaet", [128, E])
    bmaskt = sbuf("bmaskt", [128, 16], BF16)
    halomt = sbuf("halomt", [128, 1])
    epst = sbuf("epst", [128, 1])
    onec = sbuf("onec", [128, 1])
    gixf = sbuf("gixf", [128, 8], I32)
    gixb = sbuf("gixb", [128, 16], I32)
    gixy = sbuf("gixy", [128, 4], I32)
    iot128 = sbuf("iot128", [128, 1])
    ebaset = sbuf("ebaset", [128, 8])
    halT = sbuf("halT", [128, 16], BF16)

    PS0 = psum("PS0", [128, 1024])
    PS1 = psum("PS1", [128, 1024])
    PS2 = psum("PS2", [128, 1024])
    PS3 = psum("PS3", [128, 1024])

    sLD = mksem("sLD")
    sW = mksem("sW")
    sV = mksem("sV")
    sA = mksem("sA")
    sT = mksem("sT")
    sG = mksem("sG")
    sD = mksem("sD")
    sCC = mksem("sCC")
    sOUT = mksem("sOUT")
    sGA = mksem("sGA")
    sFIN = mksem("sFIN")
    P.chain = {"vector": sV, "scalar": sA}

    def ln_stats(src_ap):
        P.inc("vector", sV)
        P.wait("scalar", sV)
        P.do("vector", lambda e, a=src_ap: e.tensor_reduce(
            out=sc3[:, 0:1], in_=a, axis=AX.X, op=ALU.add))
        P.do("scalar", lambda e, a=src_ap: e.activation(
            out=sqsc[:], in_=a, func=AF.Square))
        P.inc("scalar", sA)
        P.wait("vector", sA)
        P.do("vector", lambda e: e.tensor_reduce(
            out=sc4[:, 0:1], in_=sqsc[:], axis=AX.X, op=ALU.add))
        P.do("vector", lambda e: e.tensor_scalar(
            out=sc3[:, 0:1], in0=sc3[:, 0:1], scalar1=1.0 / H, scalar2=None, op0=ALU.mult))
        P.do("vector", lambda e: e.tensor_scalar(
            out=sc4[:, 0:1], in0=sc4[:, 0:1], scalar1=1.0 / H, scalar2=None, op0=ALU.mult))
        P.do("vector", lambda e: e.tensor_tensor(
            out=sc1[:, 0:1], in0=sc3[:, 0:1], in1=sc3[:, 0:1], op=ALU.mult))
        P.do("vector", lambda e: e.tensor_tensor(
            out=sc4[:, 0:1], in0=sc4[:, 0:1], in1=sc1[:, 0:1], op=ALU.subtract))
        P.inc("vector", sV)
        P.wait("scalar", sV)
        P.do("scalar", lambda e: e.activation(
            out=sc1[:, 0:1], in_=sc4[:, 0:1], func=AF.Sqrt, bias=epst[:, 0:1]))
        P.inc("scalar", sA)
        P.wait("vector", sA)
        P.do("vector", lambda e: e.reciprocal(out=sc2[:, 0:1], in_=sc1[:, 0:1]))
        P.inc("vector", sV)
        P.wait("vector", sV)

    def ln_apply(src_ap, dst_ap):
        P.do("vector", lambda e, a=src_ap, d=dst_ap: e.tensor_scalar(
            out=d, in0=a, scalar1=sc3[:, 0:1], scalar2=sc2[:, 0:1],
            op0=ALU.subtract, op1=ALU.mult))

    P.dma("sync", identt[:], ident, sW)
    P.dma("sync", stript[:], strip, sW)
    P.dma("sync", trit[:], tri, sW)
    P.dma("sync", onest[:], onesb, sW)
    P.dma("sync", iotaet[:], iotae, sW)
    P.dma("sync", bmaskt[:], bmask, sW)
    P.dma("sync", halomt[:], halom, sW)
    P.dma("sync", gixf[:], gidx_f32, sW)
    P.dma("sync", gixb[:], gidx_bf, sW)
    P.dma("sync", gixy[:], gidx_ys, sW)
    P.dma("sync", iot128[:], iota128, sW)
    P.dma("sync", ebaset[:], ebaset_in, sW)
    P.dma("sync", convw[:].rearrange("p (c k) -> p c k", k=3), conv_w.rearrange("(c p) k -> p c k", p=128), sW, slow=True)
    P.dma("sync", convbc[:].rearrange("p (c o) -> p c o", o=1), conv_b.rearrange("(c p) o -> p c o", p=128), sW, slow=True)
    P.dma("sync", bupc[:].rearrange("p (o c) -> p o c", o=1), b_up.rearrange("o (c p) -> p o c", p=128), sW, slow=True)
    P.dma("sync", bdlc[:].rearrange("p (c o) -> p c o", o=1), b_delta.rearrange("(c p) o -> p c o", p=128), sW, slow=True)
    P.dma("sync", bdnc[:].rearrange("p (o c) -> p o c", o=1), b_down.rearrange("o (c p) -> p o c", p=128), sW, slow=True)
    P.dma("sync", dvecc[:].rearrange("p (c o) -> p c o", o=1), d_vec.rearrange("(c p) o -> p c o", p=128), sW, slow=True)
    def wconv(dst_tile, dst_off, src_ap, cols):
        P.wait("sync", sA)
        P.wait("sync", sV)
        P.dma("sync", tkA[:, 0:cols], src_ap, sLD)
        P.wait("scalar", sLD)
        P.wait("scalar", sT)
        P.do("scalar", lambda e, d=dst_tile, o=dst_off, c=cols: e.copy(
            out=d[:, o:o + c], in_=tkA[:, 0:c]))
        P.inc("scalar", sA)

    for c4 in range(4):
        wconv(wdl, c4 * DI, w_delta[c4 * 128:(c4 + 1) * 128, :], DI)
        wconv(wBc, c4 * NS, w_B[c4 * 128:(c4 + 1) * 128, :], NS)
        wconv(wCc, c4 * NS, w_C[c4 * 128:(c4 + 1) * 128, :], NS)

    wupT = sbuf("wupT", [128, 8 * H], BF16)
    xaT = sbuf("xaT", [128, 8 * S], BF16)
    gamt = sbuf("gamt", [128, H])
    bett = sbuf("bett", [128, H])
    P.dma("sync", gamt[0:1, :], ln_g, sW)
    P.dma("sync", bett[0:1, :], ln_b, sW)
    P.wait("scalar", sW)
    P.do("scalar", lambda e: e.copy(out=identf[:], in_=identt[:]))
    P.inc("scalar", sA)
    P.do("vector", lambda e: e.memset(epst[:], LNEPS))
    P.do("vector", lambda e: e.memset(onec[:], 1.0))
    P.inc("vector", sV)
    for c4 in range(8):
        wconv(wupT, c4 * H, w_up[c4 * 128:(c4 + 1) * 128, :], H)
    P.wait("tensor", sW)
    for half in range(2):
        P.do("tensor", lambda e, h=half: e.matmul(
            PS0[:, h * 512:(h + 1) * 512], onest[0:1, :],
            gamt[0:1, h * 512:(h + 1) * 512], start=True, stop=True))
        P.do("tensor", lambda e, h=half: e.matmul(
            PS1[:, h * 512:(h + 1) * 512], onest[0:1, :],
            bett[0:1, h * 512:(h + 1) * 512], start=True, stop=True))
    P.inc("tensor", sT)
    P.wait("scalar", sT)
    P.do("scalar", lambda e: e.copy(out=gamt[:], in_=PS0[:]))
    P.do("scalar", lambda e: e.copy(out=bett[:], in_=PS1[:]))
    P.inc("scalar", sA)
    P.wait("tensor", sA)
    P.wait("vector", sA)

    def tileproc(g, src_rows, halo=False):
        P.wait("sync", sV)
        P.wait("sync", sA)
        P.dma("sync", tkA[0:2, :] if halo else tkA[:], src_rows, sLD)
        P.wait("vector", sLD)
        P.wait("scalar", sLD)
        ln_stats(tkA[:])
        ln_apply(tkA[:], tkB[:])
        ln_stats(tkB[:])
        ln_apply(tkB[:], tkC[:])
        P.do("vector", lambda e: e.tensor_tensor(out=tkC[:], in0=tkC[:], in1=gamt[:], op=ALU.mult))
        P.do("vector", lambda e: e.tensor_tensor(out=tkC[:], in0=tkC[:], in1=bett[:], op=ALU.add))
        P.inc("vector", sV)
        if g == 0 and not halo and "t0tile" in tap_aps:
            P.wait("gpsimd", sV)
            P.dma("gpsimd", tap_aps["t0tile"], tkB[:], sD)
            P.dma("gpsimd", tap_aps["xatile"], tkC[:], sD)
            P.dma("gpsimd", tap_aps["gamtile"], gamt[:], sD)
            P.wait("tensor", sD)
        P.wait("tensor", sV)
        for hc in range(8):
            pslot = PS2 if hc % 2 == 0 else PS3
            P.do("tensor", lambda e, hc=hc, p=pslot: e.transpose(
                p[:, 0:128], tkC[:, hc * 128:(hc + 1) * 128], identf[:]))
            P.inc("tensor", sT)
            P.wait("scalar", sT)
            if halo:
                P.do("scalar", lambda e, hc=hc, p=pslot: e.copy(
                    out=halT[:, hc * 2:hc * 2 + 2], in_=p[:, 0:2]))
            else:
                P.do("scalar", lambda e, hc=hc, p=pslot, g=g: e.copy(
                    out=xaT[:, hc * S + g * 128: hc * S + (g + 1) * 128], in_=p[:, 0:128]))
            P.inc("scalar", sA)
            P.wait("tensor", sA)

    for g in range(NT):
        tileproc(g, x_in[g * 128:(g + 1) * 128, :])
    P.wait("vector", sA)
    P.do("vector", lambda e: e.memset(tkA[:], 0.0))
    P.inc("vector", sV)
    tileproc(0, x_halo[0:2, :], halo=True)
    P.wait("vector", sA)
    P.do("vector", lambda e: e.tensor_scalar(
        out=halT[:], in0=halT[:], scalar1=halomt[:, 0:1], scalar2=None, op0=ALU.mult))
    P.inc("vector", sV)

    P.fullbar(sOUT)
    sbfree("bett", "gamt")
    leftT = sbuf("leftT", [128, 4 * (S + 8)])
    big3 = sbuf("big3", [128, 4 * S])
    P.do("vector", lambda e: e.memset(leftT[:, 0:2 * (S + 8)], 0.0))
    P.do("vector", lambda e: e.memset(leftT[:, 2 * (S + 8):], 0.0))
    P.inc("vector", sV)
    P.wait("tensor", sV)
    P.wait("tensor", sA)
    for oc in range(8):
        for th in range(2):
            pslot = PS0 if th == 0 else PS1
            for kc in range(8):
                for q in range(2):
                    P.do("tensor", lambda e, oc=oc, th=th, kc=kc, q=q, p=pslot: e.matmul(
                        p[:, q * 512:(q + 1) * 512],
                        wupT[:, kc * H + oc * 128: kc * H + (oc + 1) * 128],
                        xaT[:, kc * S + th * 1024 + q * 512: kc * S + th * 1024 + (q + 1) * 512],
                        start=(kc == 0), stop=(kc == 7)))
            P.inc("tensor", sT)
            P.wait("scalar", sT)
            if oc < 4:
                P.do("scalar", lambda e, oc=oc, th=th, p=pslot: e.activation(
                    out=leftT[:, oc * (S + 8) + 4 + th * 1024: oc * (S + 8) + 4 + (th + 1) * 1024],
                    in_=p[:], func=AF.Identity, bias=bupc[:, oc:oc + 1]))
            else:
                P.do("scalar", lambda e, oc=oc, th=th, p=pslot: e.activation(
                    out=big3[:, (oc - 4) * S + th * 1024: (oc - 4) * S + (th + 1) * 1024],
                    in_=p[:], func=AF.Silu, bias=bupc[:, oc:oc + 1]))
            P.inc("scalar", sA)
            P.wait("tensor", sA)
    if True:
        for oc in range(4):
            for kc in range(8):
                P.do("tensor", lambda e, oc=oc, kc=kc: e.matmul(
                    PS0[:, 0:2],
                    wupT[:, kc * H + oc * 128: kc * H + (oc + 1) * 128],
                    halT[:, kc * 2:(kc + 1) * 2],
                    start=(kc == 0), stop=(kc == 7)))
            P.inc("tensor", sT)
            P.wait("scalar", sT)
            P.do("scalar", lambda e, oc=oc: e.activation(
                out=leftT[:, oc * (S + 8) + 2: oc * (S + 8) + 4],
                in_=PS0[:, 0:2], func=AF.Identity, bias=bupc[:, oc:oc + 1]))
            P.inc("scalar", sA)
            P.wait("tensor", sA)
            P.wait("vector", sA)
            P.do("vector", lambda e, oc=oc: e.tensor_scalar(
                out=leftT[:, oc * (S + 8) + 2: oc * (S + 8) + 4],
                in0=leftT[:, oc * (S + 8) + 2: oc * (S + 8) + 4],
                scalar1=halomt[:, 0:1], scalar2=None, op0=ALU.mult))
            P.inc("vector", sV)
            P.wait("scalar", sV)
    P.wait("gpsimd", sA)
    P.dma("gpsimd", srT_scr.rearrange("(c p) t -> p c t", p=128), big3[:].rearrange("p (c t) -> p c t", t=S), sD)
    if "xaT" in tap_aps:
        P.dma("gpsimd", tap_aps["xaT"], xaT[:], sD)
    if "leftT" in tap_aps:
        P.dma("gpsimd", tap_aps["leftT"], leftT[:], sD)

    P.wait("vector", sA)
    for dc in range(4):
        b0 = dc * (S + 8)
        sl = slice(dc * S, (dc + 1) * S)
        P.do("vector", lambda e, b=b0, sl=sl, dc=dc: e.tensor_scalar(
            out=big0[:, sl], in0=leftT[:, b + 2:b + 2 + S],
            scalar1=convw[:, dc * 3 + 0: dc * 3 + 1], scalar2=None, op0=ALU.mult))
        P.do("vector", lambda e, b=b0, sl=sl, dc=dc: e.scalar_tensor_tensor(
            out=big0[:, sl], in0=leftT[:, b + 3:b + 3 + S],
            scalar=convw[:, dc * 3 + 1: dc * 3 + 2], in1=big0[:, sl],
            op0=ALU.mult, op1=ALU.add))
        P.do("vector", lambda e, b=b0, sl=sl, dc=dc: e.scalar_tensor_tensor(
            out=big0[:, sl], in0=leftT[:, b + 4:b + 4 + S],
            scalar=convw[:, dc * 3 + 2: dc * 3 + 3], in1=big0[:, sl],
            op0=ALU.mult, op1=ALU.add))
    P.inc("vector", sV)
    P.wait("scalar", sV)
    for dc in range(4):
        sl = slice(dc * S, (dc + 1) * S)
        P.do("scalar", lambda e, sl=sl, dc=dc: e.activation(
            out=big0[:, sl], in_=big0[:, sl], func=AF.Silu, bias=convbc[:, dc:dc + 1]))
    P.inc("scalar", sA)
    P.wait("gpsimd", sA)
    P.dma("gpsimd", uT_scr.rearrange("(c p) t -> p c t", p=128), big0[:].rearrange("p (c t) -> p c t", t=S), sD)

    P.fullbar(sOUT)
    sbfree("big3", "leftT", "xaT", "wupT")
    uTb = sbuf("uTb", [128, 4 * S], BF16)
    big2 = sbuf("big2", [128, 4 * S])
    bmstage = sbuf("bmstage", [128, 4 * S], BF16)
    P.wait("vector", sA)
    P.do("vector", lambda e: e.tensor_copy(out=uTb[:], in_=big0[:]))
    P.inc("vector", sV)
    P.wait("tensor", sV)
    for dc in range(4):
        for th in range(2):
            pslot = PS0 if th == 0 else PS1
            for kc in range(4):
                for q in range(2):
                    P.do("tensor", lambda e, dc=dc, th=th, kc=kc, q=q, p=pslot: e.matmul(
                        p[:, q * 512:(q + 1) * 512],
                        wdl[:, kc * DI + dc * 128: kc * DI + (dc + 1) * 128],
                        uTb[:, kc * S + th * 1024 + q * 512: kc * S + th * 1024 + (q + 1) * 512],
                        start=(kc == 0), stop=(kc == 3)))
            P.inc("tensor", sT)
            P.wait("scalar", sT)
            P.do("scalar", lambda e, dc=dc, th=th, p=pslot: e.activation(
                out=big1[:, dc * S + th * 1024: dc * S + (th + 1) * 1024],
                in_=p[:], func=AF.Identity, bias=bdlc[:, dc:dc + 1]))
            P.inc("scalar", sA)
            P.wait("tensor", sA)
    P.wait("vector", sA)
    for dc in range(4):
        sl = slice(dc * S, (dc + 1) * S)
        P.do("vector", lambda e, sl=sl: e.tensor_scalar(
            out=big2[:, sl], in0=big1[:, sl], scalar1=-1.0, scalar2=None, op0=ALU.mult))
        P.do("vector", lambda e, sl=sl: e.tensor_tensor(
            out=big2[:, sl], in0=big1[:, sl], in1=big2[:, sl], op=ALU.min))
        P.inc("vector", sV)
        P.wait("scalar", sV)
        P.do("scalar", lambda e, sl=sl: e.activation(
            out=big2[:, sl], in_=big2[:, sl], func=AF.Exp))
        P.do("scalar", lambda e, sl=sl: e.activation(
            out=big2[:, sl], in_=big2[:, sl], func=AF.Ln, bias=onec[:, 0:1]))
        P.inc("scalar", sA)
        P.wait("vector", sA)
        P.do("vector", lambda e, sl=sl: e.tensor_relu(out=big1[:, sl], in_=big1[:, sl]))
        P.do("vector", lambda e, sl=sl: e.tensor_tensor(
            out=big1[:, sl], in0=big1[:, sl], in1=big2[:, sl], op=ALU.add))
        P.do("vector", lambda e, sl=sl: e.tensor_tensor(
            out=big2[:, sl], in0=big1[:, sl], in1=big0[:, sl], op=ALU.mult))
    P.inc("vector", sV)
    P.wait("gpsimd", sV)
    P.dma("gpsimd", pre_f32[0:DI, :].rearrange("(c p) t -> p c t", p=128), big1[:].rearrange("p (c t) -> p c t", t=S), sD)
    P.dma("gpsimd", pre_f32[DI:2 * DI, :].rearrange("(c p) t -> p c t", p=128), big2[:].rearrange("p (c t) -> p c t", t=S), sD)
    P.wait("tensor", sV)
    for which in range(2):
        wtile = wBc if which == 0 else wCc
        for nch in range(2):
            for th in range(2):
                pslot = PS0 if th == 0 else PS1
                for kc in range(4):
                    for q in range(2):
                        P.do("tensor", lambda e, w=wtile, ncb=nch, th=th, kc=kc, q=q, p=pslot: e.matmul(
                            p[:, q * 512:(q + 1) * 512],
                            w[:, kc * NS + ncb * 128: kc * NS + (ncb + 1) * 128],
                            uTb[:, kc * S + th * 1024 + q * 512: kc * S + th * 1024 + (q + 1) * 512],
                            start=(kc == 0), stop=(kc == 3)))
                P.inc("tensor", sT)
                P.wait("scalar", sT)
                P.do("scalar", lambda e, w=which, ncb=nch, th=th, p=pslot: e.copy(
                    out=bmstage[:, (w * 2 + ncb) * S + th * 1024: (w * 2 + ncb) * S + (th + 1) * 1024],
                    in_=p[:]))
                P.inc("scalar", sA)
                P.wait("tensor", sA)
    P.wait("gpsimd", sA)
    P.dma("gpsimd", pre_bf.rearrange("(c p) t -> p c t", p=128), bmstage[:].rearrange("p (c t) -> p c t", t=S), sD)
    P.wait("gpsimd", sD)
    P.cc("AllGather", ALU.bypass, [pre_f32.opt()], [ag_f32.opt()], sCC)
    P.wait("gpsimd", sCC)
    P.cc("AllGather", ALU.bypass, [pre_bf.opt()], [ag_bf.opt()], sCC)
    P.wait("gpsimd", sCC)
    P.inc("gpsimd", sG)

    if "preT" in tap_aps:
        P.dma("gpsimd", tap_aps["preT"], pre_f32, sD)
    if "uT" in tap_aps:
        P.dma("gpsimd", tap_aps["uT"], uT_scr, sD)

    if upto == "pre":
        P.wait("gpsimd", sD)
        stack.close()
        _emit(nc, P)
        return nc

    P.fullbar(sOUT)
    sbfree("bmstage", "big2", "uTb")
    dltT = big0
    duT = big1
    bmT = sbuf("bmT", [128, 2 * L], BF16)
    cmT = sbuf("cmT", [128, 2 * L], BF16)
    a_nT = sbuf("a_nT", [128, 2 * 128])
    hbuf = sbuf("hbuf", [128, 2 * 1024])
    hcar = sbuf("hcar", [128, 2 * 128])
    dAt = sbuf("dAt", [128, 4 * 1024])
    trm = sbuf("trm", [128, 2 * 1024])
    ymt = sbuf("ymt", [128, 2 * 1024], BF16)
    ysst = sbuf("ysst", [128, 1024])
    onesf = sbuf("onesf", [128, 128])
    wbc = sbuf("wbc", [128, 2 * 128])

    def igather(eng, dst, table, idxcol, sem):
        sem.n += 16

        def gf(e, dst=dst, table=table, idxcol=idxcol, s=sem):
            return e.indirect_dma_start(
                out=dst, out_offset=None, in_=table,
                in_offset=IndirectOffsetOnAxis(ap=idxcol, axis=0),
            ).then_inc(s.h, 16)
        P.do(eng, gf)

    gi = 0
    for rr in range(4):
        for arr, dst in ((0, dltT[:, rr * S:(rr + 1) * S]), (1, duT[:, rr * S:(rr + 1) * S])):
            igather("gpsimd", dst, ag_f32, gixf[:, gi:gi + 1], sGA)
            gi += 1
    gi = 0
    for rr in range(4):
        for tau in range(2):
            for arr, dst in ((0, bmT[:, tau * L + rr * S: tau * L + (rr + 1) * S]),
                             (1, cmT[:, tau * L + rr * S: tau * L + (rr + 1) * S])):
                igather("gpsimd", dst, ag_bf, gixb[:, gi:gi + 1], sGA)
                gi += 1
    P.do("vector", lambda e: e.memset(onesf[:], 1.0))
    P.inc("vector", sV)
    P.wait("scalar", sV)
    P.dma("sync", tkA[:, 0:NS], a_log, sLD)
    P.wait("scalar", sLD)
    P.do("scalar", lambda e: e.activation(out=tkA[:, 0:NS], in_=tkA[:, 0:NS], func=AF.Exp))
    P.inc("scalar", sA)
    P.wait("vector", sA)
    P.do("vector", lambda e: e.tensor_scalar(
        out=tkA[:, 0:NS], in0=tkA[:, 0:NS], scalar1=-1.0, scalar2=None, op0=ALU.mult))
    P.do("vector", lambda e: e.memset(hcar[:], 0.0))
    P.inc("vector", sV)
    P.wait("tensor", sV)
    for tau in range(2):
        P.do("tensor", lambda e, tau=tau: e.transpose(
            PS2[:, 0:128], tkA[:, tau * 128:(tau + 1) * 128], identf[:]))
        P.inc("tensor", sT)
        P.wait("scalar", sT)
        P.do("scalar", lambda e, tau=tau: e.copy(
            out=a_nT[:, tau * 128:(tau + 1) * 128], in_=PS2[:, 0:128]))
        P.inc("scalar", sA)
        P.wait("tensor", sA)
    P.wait("vector", sA)
    P.wait("vector", sLD)
    P.wait("vector", sGA)
    P.wait("tensor", sLD)
    P.wait("tensor", sGA)
    P.wait("scalar", sLD)
    P.wait("scalar", sGA)

    redmark = [0]
    for c in range(NCH):
        for i in range(128):
            first = (c == 0 and i == 0)
            par = i % 2
            P.do("scalar", lambda e, i=i, par=par: e.activation(
                out=wbc[:, par * 128:(par + 1) * 128], in_=onesf[:],
                func=AF.Copy, scale=identf[:, i:i + 1]))
            P.inc("scalar", sA)
            if not first:
                P.wait("tensor", sV)
            P.wait("tensor", sA)
            for q in range(2):
                P.do("tensor", lambda e, i=i, c=c, q=q, par=par: e.matmul(
                    PS0[:, q * 512:(q + 1) * 512], wbc[:, par * 128:(par + 1) * 128],
                    dltT[:, c * CH + q * 512: c * CH + (q + 1) * 512],
                    start=True, stop=True))
            for q in range(2):
                P.do("tensor", lambda e, i=i, c=c, q=q, par=par: e.matmul(
                    PS1[:, q * 512:(q + 1) * 512], wbc[:, par * 128:(par + 1) * 128],
                    duT[:, c * CH + q * 512: c * CH + (q + 1) * 512],
                    start=True, stop=True))
            P.inc("tensor", sT)
            P.wait("scalar", sT)
            for tau in range(2):
                P.do("scalar", lambda e, tau=tau, i=i, par=par: e.activation(
                    out=dAt[:, (par * 2 + tau) * 1024:(par * 2 + tau + 1) * 1024], in_=PS0[:],
                    func=AF.Exp, scale=a_nT[:, tau * 128 + i: tau * 128 + i + 1]))
            P.inc("scalar", sA)
            P.wait("vector", sT)
            for tau in range(2):
                P.do("vector", lambda e, tau=tau, c=c: e.tensor_tensor(
                    out=trm[:, tau * 1024:(tau + 1) * 1024],
                    in0=bmT[:, tau * L + c * CH: tau * L + (c + 1) * CH],
                    in1=PS1[:], op=ALU.mult))
            P.inc("vector", sV)
            P.wait("vector", sA)
            if not first:
                P.wait("vector", sT, redmark[0])
            for tau in range(2):
                P.do("vector", lambda e, tau=tau, i=i, par=par: e.tensor_tensor_scan(
                    out=hbuf[:, tau * 1024:(tau + 1) * 1024],
                    data0=dAt[:, (par * 2 + tau) * 1024:(par * 2 + tau + 1) * 1024],
                    data1=trm[:, tau * 1024:(tau + 1) * 1024],
                    initial=hcar[:, tau * 128 + i: tau * 128 + i + 1],
                    op0=ALU.mult, op1=ALU.add))
                P.do("vector", lambda e, tau=tau, i=i: e.tensor_copy(
                    out=hcar[:, tau * 128 + i: tau * 128 + i + 1],
                    in_=hbuf[:, tau * 1024 + 1023: tau * 1024 + 1024]))
                P.do("vector", lambda e, tau=tau, c=c: e.tensor_tensor(
                    out=ymt[:, tau * 1024:(tau + 1) * 1024],
                    in0=hbuf[:, tau * 1024:(tau + 1) * 1024],
                    in1=cmT[:, tau * L + c * CH: tau * L + (c + 1) * CH], op=ALU.mult))
                P.inc("vector", sV)
                P.wait("tensor", sV)
                for q in range(2):
                    P.do("tensor", lambda e, i=i, tau=tau, q=q: e.matmul(
                        PS2[:, q * 512:(q + 1) * 512],
                        stript[:, 127 - i:255 - i],
                        ymt[:, tau * 1024 + q * 512: tau * 1024 + (q + 1) * 512],
                        start=(i == 0 and tau == 0), stop=(i == 127 and tau == 1)))
            P.inc("tensor", sT)
            redmark[0] = sT.n
        P.wait("scalar", sT)
        P.do("scalar", lambda e: e.copy(out=ysst[:], in_=PS2[:]))
        P.inc("scalar", sA)
        P.wait("gpsimd", sA)
        P.dma("gpsimd", ysT_loc[:, c * CH:(c + 1) * CH], ysst[:], sD)
        P.wait("gpsimd", sD)
        P.inc("gpsimd", sG)
        P.wait("tensor", sA)

    P.cc("AllGather", ALU.bypass, [ysT_loc.opt()], [ag_ys.opt()], sCC)
    P.wait("gpsimd", sCC)
    if "ysT" in tap_aps:
        P.dma("gpsimd", tap_aps["ysT"], ysT_loc, sD)

    if upto == "scan":
        P.wait("gpsimd", sD)
        stack.close()
        _emit(nc, P)
        return nc

    P.fullbar(sOUT)
    sbfree("wbc", "onesf", "ysst", "ymt", "trm", "dAt", "hcar", "hbuf",
           "a_nT", "cmT", "bmT")
    big2s = sbuf("big2s", [128, 4 * S])
    wdn = sbuf("wdn", [128, 4 * H], BF16)
    bdnT = sbuf("bdnT", [128, H])
    gtb = sbuf("gtb", [128, 4 * S], BF16)
    t1Tg = sbuf("t1Tg", [128, H])
    rwf = sbuf("rwf", [128, 8 * E])
    posb = sbuf("posb", [128, 128])
    ixb = sbuf("ixb", [128, 128])
    wvb = sbuf("wvb", [128, 128])
    ohb = sbuf("ohb", [128, E], BF16)
    crow = sbuf("crow", [1, E])
    brow = sbuf("brow", [128, E])
    agcS = sbuf("agcS", [8, E])
    rkmS = sbuf("rkmS", [8, 1])
    ixt = sbuf("ixt", [128, 8], U16)
    t1bt = sbuf("t1bt", [128, H], BF16)
    ptile = sbuf("ptile", [128, 16])
    tokcol = sbuf("tokcol", [128, 1])
    tok0S = sbuf("tok0S", [128, 1])

    for rr in range(4):
        igather("gpsimd", big0[:, rr * S:(rr + 1) * S],
                ag_ys.rearrange("a (c t) -> (a c) t", t=S), gixy[:, rr:rr + 1], sGA)
    P.dma("sync", big1[:].rearrange("p (c t) -> p c t", t=S),
          uT_scr.rearrange("(c p) t -> p c t", p=128), sLD)
    P.dma("sync", big2s[:].rearrange("p (c t) -> p c t", t=S),
          srT_scr.rearrange("(c p) t -> p c t", p=128), sLD)
    P.dma("sync", rwf[:].rearrange("p (c e) -> p c e", e=E),
          router_w.rearrange("(c p) e -> p c e", p=128), sLD)
    P.dma("sync", rkmS[:], rankmask_in, sLD)
    P.dma("sync", tok0S[:], tok0_in, sLD)
    P.dma("sync", bdnT[0:1, :], b_down, sLD)
    for c4 in range(4):
        wconv(wdn, c4 * H, w_down[c4 * 128:(c4 + 1) * 128, :], H)
    P.wait("tensor", sLD)
    P.wait("tensor", sA)
    for half in range(2):
        P.do("tensor", lambda e, h=half: e.matmul(
            PS0[:, h * 512:(h + 1) * 512], onest[0:1, :],
            bdnT[0:1, h * 512:(h + 1) * 512], start=True, stop=True))
    P.inc("tensor", sT)
    P.wait("scalar", sT)
    P.do("scalar", lambda e: e.copy(out=bdnT[:], in_=PS0[:]))
    P.inc("scalar", sA)
    P.wait("tensor", sA)

    P.wait("vector", sLD)
    P.do("vector", lambda e: e.memset(crow[:], 0.0))
    P.inc("vector", sV)
    P.wait("vector", sGA)
    P.wait("vector", sLD)
    for dc in range(4):
        sl = slice(dc * S, (dc + 1) * S)
        P.do("vector", lambda e, sl=sl, dc=dc: e.scalar_tensor_tensor(
            out=big1[:, sl], in0=big1[:, sl], scalar=dvecc[:, dc:dc + 1],
            in1=big0[:, sl], op0=ALU.mult, op1=ALU.add))
        P.do("vector", lambda e, sl=sl: e.tensor_tensor(
            out=big0[:, sl], in0=big1[:, sl], in1=big2s[:, sl], op=ALU.mult))
        P.do("vector", lambda e, sl=sl: e.tensor_copy(out=gtb[:, sl], in_=big0[:, sl]))
    P.inc("vector", sV)

    if "gatedT" in tap_aps:
        P.wait("gpsimd", sV)
        P.dma("gpsimd", tap_aps["gatedT"], big0[:], sD)
        P.dma("gpsimd", tap_aps["ysl"], big1[:], sD)
    P.wait("tensor", sV)
    for g in range(NT):
        P.wait("sync", sV)
        P.wait("sync", sA)
        P.dma("sync", tkA[:], x_in[g * 128:(g + 1) * 128, :], sLD)
        if g > 0:
            P.wait("tensor", sV)
            P.wait("tensor", sA)
        for kc in range(4):
            for q in range(2):
                P.do("tensor", lambda e, g=g, kc=kc, q=q: e.matmul(
                    PS0[:, q * 512:(q + 1) * 512],
                    gtb[:, kc * S + g * 128: kc * S + (g + 1) * 128],
                    wdn[:, kc * H + q * 512: kc * H + (q + 1) * 512],
                    start=(kc == 0), stop=(kc == 3)))
        P.inc("tensor", sT)
        P.wait("vector", sT)
        P.do("vector", lambda e: e.scalar_tensor_tensor(
            out=tkC[:], in0=PS0[:], scalar=1.0, in1=bdnT[:], op0=ALU.mult, op1=ALU.add))
        P.wait("vector", sLD)
        P.wait("scalar", sLD)
        ln_stats(tkA[:])
        ln_apply(tkA[:], tkB[:])
        P.do("vector", lambda e: e.tensor_tensor(out=tkC[:], in0=tkC[:], in1=tkB[:], op=ALU.add))
        ln_stats(tkC[:])
        ln_apply(tkC[:], tkB[:])
        P.do("vector", lambda e: e.tensor_tensor(out=tkA[:], in0=tkA[:], in1=tkB[:], op=ALU.add))
        P.inc("vector", sV)
        P.wait("gpsimd", sV)
        P.dma("gpsimd", x1_scr[g * 128:(g + 1) * 128, :], tkA[:], sD)
        ln_stats(tkA[:])
        ln_apply(tkA[:], tkB[:])
        P.inc("vector", sV)
        P.wait("scalar", sV)
        P.do("scalar", lambda e: e.copy(out=t1bt[:], in_=tkB[:]))
        P.inc("scalar", sA)
        P.wait("gpsimd", sA)
        P.dma("gpsimd", t1_loc[g * 128:(g + 1) * 128, :], t1bt[:], sD)
        P.wait("tensor", sV)
        for hc in range(8):
            pslot = PS2 if hc % 2 == 0 else PS3
            P.do("tensor", lambda e, hc=hc, p=pslot: e.transpose(
                p[:, 0:128], tkB[:, hc * 128:(hc + 1) * 128], identf[:]))
            P.inc("tensor", sT)
            P.wait("scalar", sT)
            P.do("scalar", lambda e, hc=hc, p=pslot: e.copy(
                out=t1Tg[:, hc * 128:(hc + 1) * 128], in_=p[:, 0:128]))
            P.inc("scalar", sA)
            P.wait("tensor", sA)
        for hc in range(8):
            P.do("tensor", lambda e, hc=hc: e.matmul(
                PS1[:, 0:64], t1Tg[:, hc * 128:(hc + 1) * 128],
                rwf[:, hc * E:(hc + 1) * E], start=(hc == 0), stop=(hc == 7)))
        P.inc("tensor", sT)
        C = tkC
        P.wait("scalar", sT)
        P.do("scalar", lambda e: e.activation(out=C[:, 0:64], in_=PS1[:, 0:64], func=AF.Sigmoid))
        P.inc("scalar", sA)
        P.wait("vector", sA)
        P.do("vector", lambda e: e.tensor_reduce(
            out=C[:, 64:72], in_=C[:, 0:64].rearrange("p (g e) -> p g e", e=8),
            axis=AX.X, op=ALU.max))
        P.do("vector", lambda e: e.max(out=C[:, 72:80], in_=C[:, 64:72]))
        P.do("vector", lambda e: e.tensor_scalar(
            out=C[:, 80:88], in0=C[:, 64:72], scalar1=C[:, 75:76], scalar2=None, op0=ALU.is_ge))
        P.do("vector", lambda e: _bctt(e,
            C[:, 88:152].rearrange("p (g e) -> p g e", e=8),
            C[:, 0:64].rearrange("p (g e) -> p g e", e=8),
            C[:, 80:88].rearrange("p (g o) -> p g o", o=1),
            ALU.mult))
        P.do("vector", lambda e: e.max(out=C[:, 152:160], in_=C[:, 88:152]))
        P.do("vector", lambda e: e.max_index(out=ixt[:], in_max=C[:, 152:160], in_values=C[:, 88:152]))
        P.do("vector", lambda e: e.tensor_copy(out=C[:, 170:178], in_=ixt[:]))
        P.do("vector", lambda e: e.tensor_reduce(out=C[:, 160:161], in_=C[:, 152:160], axis=AX.X, op=ALU.add))
        P.do("vector", lambda e: e.tensor_scalar(
            out=C[:, 160:161], in0=C[:, 160:161], scalar1=1e-9, scalar2=None, op0=ALU.max))
        P.do("vector", lambda e: e.reciprocal(out=C[:, 161:162], in_=C[:, 160:161]))
        P.do("vector", lambda e: e.tensor_scalar(
            out=C[:, 162:170], in0=C[:, 152:160], scalar1=C[:, 161:162], scalar2=MOE_SCALE,
            op0=ALU.mult, op1=ALU.mult))
        B_ = tkB
        P.do("vector", lambda e: _bctt(e,
            B_[:, 0:512].rearrange("p (x k) -> p x k", k=8),
            C[:, 170:178].rearrange("p (o k) -> p o k", o=1),
            iotaet[:].rearrange("p (x o) -> p x o", o=1),
            ALU.is_equal))
        P.do("vector", lambda e: e.tensor_reduce(
            out=B_[:, 512:576], in_=B_[:, 0:512].rearrange("p (x k) -> p x k", k=8),
            axis=AX.X, op=ALU.add))
        P.do("vector", lambda e: e.tensor_copy(out=ohb[:], in_=B_[:, 512:576]))
        P.do("vector", lambda e, g=g: e.tensor_copy(out=wvb[:, g * 8:(g + 1) * 8], in_=C[:, 162:170]))
        P.do("vector", lambda e, g=g: e.tensor_copy(out=ixb[:, g * 8:(g + 1) * 8], in_=C[:, 170:178]))
        P.inc("vector", sV)
        P.wait("tensor", sV)
        P.do("tensor", lambda e: e.matmul(PS0[:, 0:64], trit[:], ohb[:], start=True, stop=False))
        P.do("tensor", lambda e: e.matmul(PS0[:, 0:64], onest[0:1, :], crow[0:1, :], start=False, stop=True))
        P.do("tensor", lambda e: e.matmul(PS0[0:1, 64:128], stript[:, 127:128], ohb[:], start=True, stop=True))
        P.inc("tensor", sT)
        P.wait("vector", sT)
        P.do("vector", lambda e: _bctt(e,
            B_[:, 0:512].rearrange("p (k x) -> p k x", x=64),
            C[:, 170:178].rearrange("p (k o) -> p k o", o=1),
            iotaet[:].rearrange("p (o x) -> p o x", o=1),
            ALU.is_equal))
        P.do("vector", lambda e: e.tensor_copy(out=B_[:, 576:640], in_=PS0[:, 0:64]))
        P.do("vector", lambda e: _bctt(e,
            C[:, 256:768].rearrange("p (k x) -> p k x", x=64),
            B_[:, 0:512].rearrange("p (k x) -> p k x", x=64),
            B_[:, 576:640].rearrange("p (o x) -> p o x", o=1),
            ALU.mult))
        P.do("vector", lambda e, g=g: e.tensor_reduce(
            out=posb[:, g * 8:(g + 1) * 8],
            in_=C[:, 256:768].rearrange("p (k x) -> p k x", x=64),
            axis=AX.X, op=ALU.add))
        P.do("vector", lambda e: e.tensor_tensor(
            out=crow[0:1, :], in0=crow[0:1, :], in1=PS0[0:1, 64:128], op=ALU.add))
        P.inc("vector", sV)
        P.wait("tensor", sV)

    P.wait("gpsimd", sV)
    P.wait("gpsimd", sD)
    P.cc("AllGather", ALU.bypass, [t1_loc.opt()], [ag_t1.opt()], sCC)
    P.dma("gpsimd", cnt_loc, crow[0:1, :], sD)
    P.wait("gpsimd", sD)
    P.cc("AllGather", ALU.bypass, [cnt_loc.opt()], [ag_cnt.opt()], sCC)
    P.wait("sync", sCC)
    P.dma("sync", agcS[:], ag_cnt, sLD)
    P.wait("tensor", sLD)
    P.wait("tensor", sV)
    P.wait("tensor", sA)
    P.do("tensor", lambda e: e.matmul(PS0[0:1, 0:64], rkmS[:], agcS[:], start=True, stop=True))
    P.inc("tensor", sT)
    P.wait("scalar", sT)
    P.do("scalar", lambda e: e.copy(out=crow[0:1, :], in_=PS0[0:1, 0:64]))
    P.inc("scalar", sA)
    P.wait("tensor", sA)
    P.do("tensor", lambda e: e.matmul(PS1[:, 0:64], onest[0:1, :], crow[0:1, :], start=True, stop=True))
    P.inc("tensor", sT)
    P.wait("scalar", sT)
    P.do("scalar", lambda e: e.copy(out=brow[:, 0:64], in_=PS1[:, 0:64]))
    P.inc("scalar", sA)

    P.wait("vector", sA)
    for g in range(NT):
        B_ = tkB
        C = tkC
        gs8 = slice(g * 8, (g + 1) * 8)
        P.do("vector", lambda e, gs8=gs8: _bctt(e,
            B_[:, 0:512].rearrange("p (k x) -> p k x", x=64),
            ixb[:, gs8].rearrange("p (k o) -> p k o", o=1),
            iotaet[:].rearrange("p (o x) -> p o x", o=1),
            ALU.is_equal))
        P.do("vector", lambda e: _bctt(e,
            C[:, 0:512].rearrange("p (k x) -> p k x", x=64),
            B_[:, 0:512].rearrange("p (k x) -> p k x", x=64),
            brow[:, 0:64].rearrange("p (o x) -> p o x", o=1),
            ALU.mult))
        P.do("vector", lambda e: e.tensor_reduce(
            out=C[:, 512:520], in_=C[:, 0:512].rearrange("p (k x) -> p k x", x=64),
            axis=AX.X, op=ALU.add))
        P.do("vector", lambda e, gs8=gs8: e.tensor_tensor(
            out=C[:, 520:528], in0=posb[:, gs8], in1=C[:, 512:520], op=ALU.add))
        P.do("vector", lambda e: e.tensor_scalar(
            out=C[:, 528:536], in0=C[:, 520:528], scalar1=float(CAP), scalar2=None, op0=ALU.is_lt))
        P.do("vector", lambda e, gs8=gs8: e.tensor_tensor(
            out=C[:, 536:544], in0=wvb[:, gs8], in1=C[:, 528:536], op=ALU.mult))
        P.do("vector", lambda e: e.tensor_scalar(
            out=C[:, 544:552], in0=C[:, 520:528], scalar1=float(CAP - 1), scalar2=None, op0=ALU.min))
        P.do("vector", lambda e, gs8=gs8: e.scalar_tensor_tensor(
            out=C[:, 552:560], in0=ixb[:, gs8], scalar=float(CAP), in1=C[:, 544:552],
            op0=ALU.mult, op1=ALU.add))
        P.do("vector", lambda e: e.tensor_scalar(
            out=C[:, 560:568], in0=C[:, 552:560], scalar1=1.0, scalar2=None, op0=ALU.add))
        P.do("vector", lambda e: e.tensor_tensor(
            out=C[:, 560:568], in0=C[:, 560:568], in1=C[:, 528:536], op=ALU.mult))
        P.do("vector", lambda e: e.tensor_scalar(
            out=C[:, 560:568], in0=C[:, 560:568], scalar1=-1.0, scalar2=None, op0=ALU.add))
        P.do("vector", lambda e, g=g: e.tensor_scalar(
            out=tokcol[:, 0:1], in0=tok0S[:, 0:1], scalar1=float(g * 128 + 1), scalar2=None, op0=ALU.add))
        P.do("vector", lambda e: e.tensor_scalar(
            out=C[:, 568:576], in0=C[:, 552:560], scalar1=0.0, scalar2=tokcol[:, 0:1],
            op0=ALU.mult, op1=ALU.add))
        P.do("vector", lambda e: e.tensor_copy(
            out=ptile[:].rearrange("p (k c) -> p k c", c=2)[:, :, 0:1],
            in_=C[:, 568:576].rearrange("p (k o) -> p k o", o=1)))
        P.do("vector", lambda e: e.tensor_copy(
            out=ptile[:].rearrange("p (k c) -> p k c", c=2)[:, :, 1:2],
            in_=C[:, 536:544].rearrange("p (k o) -> p k o", o=1)))
        P.inc("vector", sV)
        P.wait("gpsimd", sV)
        gsl = slice(g * 1024, (g + 1) * 1024)
        P.dma("gpsimd", rec_islot[0, gsl].rearrange("(t k) -> t k", k=8), C[:, 560:568], sD)
        P.dma("gpsimd", rec_slotc[0, gsl].rearrange("(t k) -> t k", k=8), C[:, 552:560], sD)
        P.dma("gpsimd", rec_wv[0, gsl].rearrange("(t k) -> t k", k=8), C[:, 536:544], sD)
        P.dma("gpsimd", rec_pairs[gsl, :].rearrange("(t k) c -> t (k c)", k=128), ptile[:], sD)
        P.wait("vector", sD)

    if "recs" in tap_aps:
        P.wait("gpsimd", sD)
        P.dma("gpsimd", tap_aps["recs"], rec_islot, sD)
        P.dma("gpsimd", tap_aps["recw"], rec_wv, sD)
        P.dma("gpsimd", tap_aps["recc"], rec_slotc, sD)
    if "x1" in tap_aps:
        P.wait("gpsimd", sD)
        P.dma("gpsimd", tap_aps["x1"], x1_scr, sD)
    if "t1" in tap_aps:
        P.wait("gpsimd", sD)
        P.dma("gpsimd", tap_aps["t1"], t1_loc, sD)

    if upto == "router":
        P.wait("gpsimd", sD)
        stack.close()
        _emit(nc, P)
        return nc

    P.fullbar(sOUT)
    sbfree("tok0S", "tokcol", "ptile", "t1bt",
           "ixt", "rkmS", "agcS", "brow", "crow", "ohb", "wvb", "ixb",
           "posb", "rwf", "t1Tg", "gtb", "bdnT", "wdn", "big2s")
    isrc = sbuf("isrc", [128, 128])
    idx32 = sbuf("idx32", [128, 128], mybir.dt.int32)
    pairsT = sbuf("pairsT", [128, 256])
    P.wait("vector", sA)
    P.do("vector", lambda e: e.memset(tkC[:, 0:520], 0.0))
    P.inc("vector", sV)
    P.wait("gpsimd", sV)
    P.wait("gpsimd", sD)
    P.dma("gpsimd", T_loc.rearrange("(p a) c -> p (a c)", p=128), tkC[:, 0:516], sD)
    P.wait("sync", sD)
    P.dma("sync", isrc[:], rec_islot[0, :].rearrange("(p j) -> p j", p=128), sLD)
    P.dma("sync", pairsT[:], rec_pairs[:, :].rearrange("(p j) c -> p (j c)", p=128), sLD)
    P.wait("vector", sLD)
    P.do("vector", lambda e: e.tensor_scalar(
        out=tkA[:, 0:128], in0=isrc[:], scalar1=0.0, scalar2=None, op0=ALU.is_lt))
    P.do("vector", lambda e: e.scalar_tensor_tensor(
        out=tkA[:, 0:128], in0=tkA[:, 0:128], scalar=float(E * CAP + 100 + 1), in1=isrc[:],
        op0=ALU.mult, op1=ALU.add))
    P.do("vector", lambda e: e.tensor_copy(out=idx32[:], in_=tkA[:, 0:128]))
    P.inc("vector", sV)
    P.wait("gpsimd", sV)
    P.wait("gpsimd", sD)
    for j in range(128):
        sGA.n += 16

        def fsc(e, j=j):
            return e.indirect_dma_start(
                out=T_loc, out_offset=IndirectOffsetOnAxis(ap=idx32[:, j:j + 1], axis=0),
                in_=pairsT[:, 2 * j:2 * j + 2], in_offset=None,
                compute_op=ALU.add,
            ).then_inc(sGA.h, 16)
        P.do("gpsimd", fsc)
    P.wait("gpsimd", sGA)
    P.cc("AllReduce", ALU.add, [T_loc.opt()], [T_glob.opt()], sCC)
    P.wait("gpsimd", sCC)
    if "Tglob" in tap_aps:
        P.dma("gpsimd", tap_aps["Tglob"], T_glob, sD)

    if upto == "table":
        P.wait("gpsimd", sD)
        stack.close()
        _emit(nc, P)
        return nc

    wge = sbuf("wge", [128, 8 * DI], BF16)
    wue = sbuf("wue", [128, 8 * DI], BF16)
    wde = sbuf("wde", [128, 4 * H], BF16)
    dispT = sbuf("dispT", [128, 8 * CAP], BF16)
    gub = sbuf("gub", [128, 4 * CAP], BF16)
    twt = sbuf("twt", [128, 8])
    eob = sbuf("eob", [128, H], BF16)
    exidx = sbuf("exidx", [128, 1], mybir.dt.int32)
    exf = sbuf("exf", [128, 1])
    dtok = sbuf("dtok", [128, H], BF16)

    for e_i in range(EPC):
        for c4 in range(8):
            wconv(wge, c4 * DI, gate_w[e_i * H + c4 * 128: e_i * H + (c4 + 1) * 128, :], DI)
            wconv(wue, c4 * DI, up_w[e_i * H + c4 * 128: e_i * H + (c4 + 1) * 128, :], DI)
        for c4 in range(4):
            wconv(wde, c4 * H, down_w[e_i * DI + c4 * 128: e_i * DI + (c4 + 1) * 128, :], H)
        for sc in range(4):
            P.wait("vector", sGA)
            P.wait("vector", sD)
            P.do("vector", lambda e, e_i=e_i, sc=sc: e.tensor_scalar(
                out=exf[:, 0:1], in0=iot128[:, 0:1], scalar1=float(128 * sc), scalar2=None, op0=ALU.add))
            P.do("vector", lambda e, e_i=e_i: e.tensor_tensor(
                out=exf[:, 0:1], in0=exf[:, 0:1], in1=ebaset[:, e_i:e_i + 1], op=ALU.add))
            P.do("vector", lambda e: e.tensor_copy(out=exidx[:], in_=exf[:, 0:1]))
            P.inc("vector", sV)
            P.wait("gpsimd", sV)
            igather("gpsimd", twt[:, 2 * sc:2 * sc + 2], T_glob, exidx[:, 0:1], sGA)
            P.wait("vector", sGA)
            P.do("vector", lambda e, sc=sc: e.tensor_scalar(
                out=exf[:, 0:1], in0=twt[:, 2 * sc:2 * sc + 1], scalar1=-1.0, scalar2=0.0,
                op0=ALU.add, op1=ALU.max))
            P.do("vector", lambda e: e.tensor_copy(out=exidx[:], in_=exf[:, 0:1]))
            P.inc("vector", sV)
            P.wait("gpsimd", sV)
            P.wait("gpsimd", sT)
            igather("gpsimd", dtok[:], ag_t1, exidx[:, 0:1], sGA)
            P.wait("tensor", sGA)
            P.wait("tensor", sA)
            for hc in range(8):
                pslot = PS2 if hc % 2 == 0 else PS3
                P.do("tensor", lambda e, hc=hc, p=pslot: e.transpose(
                    p.ap().bitcast(BF16)[:, 0:128],
                    dtok[:, hc * 128:(hc + 1) * 128], identt[:]))
                P.inc("tensor", sT)
                P.wait("scalar", sT)
                P.do("scalar", lambda e, hc=hc, sc=sc, p=pslot: e.copy(
                    out=dispT[:, hc * CAP + sc * 128: hc * CAP + (sc + 1) * 128],
                    in_=p.ap().bitcast(BF16)[:, 0:128]))
                P.inc("scalar", sA)
                P.wait("tensor", sA)
        for ic in range(4):
            P.wait("tensor", sA)
            P.wait("tensor", sV)
            for hc in range(8):
                P.do("tensor", lambda e, ic=ic, hc=hc: e.matmul(
                    PS0[:, 0:512], wge[:, hc * DI + ic * 128: hc * DI + (ic + 1) * 128],
                    dispT[:, hc * CAP:(hc + 1) * CAP], start=(hc == 0), stop=(hc == 7)))
            for hc in range(8):
                P.do("tensor", lambda e, ic=ic, hc=hc: e.matmul(
                    PS1[:, 0:512], wue[:, hc * DI + ic * 128: hc * DI + (ic + 1) * 128],
                    dispT[:, hc * CAP:(hc + 1) * CAP], start=(hc == 0), stop=(hc == 7)))
            P.inc("tensor", sT)
            P.wait("scalar", sT)
            P.do("scalar", lambda e: e.activation(out=tkA[:, 0:512], in_=PS0[:, 0:512], func=AF.Silu))
            P.inc("scalar", sA)
            P.wait("vector", sA)
            P.do("vector", lambda e: e.tensor_tensor(out=tkB[:, 0:512], in0=tkA[:, 0:512], in1=PS1[:, 0:512], op=ALU.mult))
            P.do("vector", lambda e, ic=ic: e.tensor_copy(out=gub[:, ic * CAP:(ic + 1) * CAP], in_=tkB[:, 0:512]))
            P.inc("vector", sV)
            P.wait("tensor", sV)
        for sc in range(4):
            P.wait("tensor", sA)
            for ic in range(4):
                for q in range(2):
                    P.do("tensor", lambda e, sc=sc, ic=ic, q=q: e.matmul(
                        PS2[:, q * 512:(q + 1) * 512],
                        gub[:, ic * CAP + sc * 128: ic * CAP + (sc + 1) * 128],
                        wde[:, ic * H + q * 512: ic * H + (q + 1) * 512],
                        start=(ic == 0), stop=(ic == 3)))
            P.inc("tensor", sT)
            P.wait("scalar", sT)
            P.wait("scalar", sGA)
            P.do("scalar", lambda e: e.copy(out=eob[:], in_=PS2[:, 0:1024]))
            P.inc("scalar", sA)
            P.wait("gpsimd", sA)
            P.dma("gpsimd", eo_loc[e_i * CAP + sc * 128: e_i * CAP + (sc + 1) * 128, :], eob[:], sD)
            P.wait("tensor", sA)
        P.wait("sync", sA)
    P.wait("gpsimd", sD)
    P.cc("AllGather", ALU.bypass, [eo_loc.opt()], [ag_eo.opt()], sCC)
    P.wait("gpsimd", sCC)
    if "eo" in tap_aps:
        P.dma("gpsimd", tap_aps["eo"], eo_loc, sD)
    if upto == "expert":
        P.wait("gpsimd", sD)
        stack.close()
        _emit(nc, P)
        return nc

    P.fullbar(sOUT)
    sbfree("dtok", "exf", "exidx", "eob", "twt", "gub", "dispT", "wde", "wue", "wge",
           "pairsT", "idx32", "isrc")
    sct = sbuf("sct", [128, 128])
    sci = sbuf("sci", [128, 128], mybir.dt.int32)
    wvT = sbuf("wvT", [128, 128])
    ctr = sbuf("ctr", [128, H])
    cgt = sbuf("cgt", [128, H], BF16)
    wj = sbuf("wj", [128, 128], BF16)
    wshg = sbuf("wshg", [128, 8 * DI], BF16)
    wshu = sbuf("wshu", [128, 8 * DI], BF16)
    wsd = sbuf("wsd", [128, 4 * H], BF16)
    t1hb = sbuf("t1hb", [128, H], BF16)
    sgb = sbuf("sgb", [128, 4 * 128], BF16)

    for c4 in range(8):
        wconv(wshg, c4 * DI, sh_g[c4 * 128:(c4 + 1) * 128, :], DI)
        wconv(wshu, c4 * DI, sh_u[c4 * 128:(c4 + 1) * 128, :], DI)
    for c4 in range(4):
        wconv(wsd, c4 * H, sh_d[c4 * 128:(c4 + 1) * 128, :], H)
    P.wait("sync", sD)
    P.dma("sync", sct[:], rec_slotc[0, :].rearrange("(p j) -> p j", p=128), sLD)
    P.dma("sync", wvT[:], rec_wv[0, :].rearrange("(p j) -> p j", p=128), sLD)
    P.wait("tensor", sLD)
    P.wait("tensor", sA)
    P.do("tensor", lambda e: e.transpose(PS3[:, 0:128], sct[:], identf[:]))
    P.inc("tensor", sT)
    P.wait("scalar", sT)
    P.do("scalar", lambda e: e.copy(out=sct[:], in_=PS3[:, 0:128]))
    P.inc("scalar", sA)
    P.wait("tensor", sA)
    P.do("tensor", lambda e: e.transpose(PS3[:, 0:128], wvT[:], identf[:]))
    P.inc("tensor", sT)
    P.wait("scalar", sT)
    P.do("scalar", lambda e: e.copy(out=wvT[:], in_=PS3[:, 0:128]))
    P.inc("scalar", sA)
    P.wait("vector", sA)
    P.do("vector", lambda e: e.tensor_copy(out=sci[:], in_=sct[:]))
    P.inc("vector", sV)

    for g in range(NT):
        for jj in range(8):
            j = g * 8 + jj
            P.wait("gpsimd", sV)
            P.wait("gpsimd", sA)
            P.wait("gpsimd", sT)
            igather("gpsimd", cgt[:], ag_eo, sci[:, j:j + 1], sGA)
            P.wait("vector", sGA)
            P.do("vector", lambda e: e.memset(wj[:], 0.0))
            P.do("vector", lambda e, j=j, jj=jj: e.tensor_scalar(
                out=wj[:, 16 * jj:16 * (jj + 1)], in0=bmaskt[:],
                scalar1=wvT[:, j:j + 1], scalar2=None, op0=ALU.mult))
            P.inc("vector", sV)
            P.wait("tensor", sV)
            P.wait("tensor", sGA)
            if jj == 0:
                P.wait("tensor", sA)
            for q in range(2):
                P.do("tensor", lambda e, q=q, jj=jj: e.matmul(
                    PS0[:, q * 512:(q + 1) * 512], wj[:],
                    cgt[:, q * 512:(q + 1) * 512], start=(jj == 0), stop=(jj == 7)))
            P.inc("tensor", sT)
        P.wait("scalar", sT)
        P.do("scalar", lambda e: e.copy(out=ctr[:], in_=PS0[:]))
        P.inc("scalar", sA)
        P.wait("sync", sA)
        P.wait("sync", sV)
        P.wait("sync", sT)
        P.dma("sync", cgt[:], t1_loc[g * 128:(g + 1) * 128, :], sLD)
        P.wait("tensor", sLD)
        for hc in range(8):
            pslot = PS2 if hc % 2 == 0 else PS3
            P.do("tensor", lambda e, hc=hc, p=pslot: e.transpose(
                p.ap().bitcast(BF16)[:, 0:128], cgt[:, hc * 128:(hc + 1) * 128], identt[:]))
            P.inc("tensor", sT)
            P.wait("scalar", sT)
            P.do("scalar", lambda e, hc=hc, p=pslot: e.copy(
                out=t1hb[:, hc * 128:(hc + 1) * 128], in_=p.ap().bitcast(BF16)[:, 0:128]))
            P.inc("scalar", sA)
            P.wait("tensor", sA)
        for ic in range(4):
            for hc in range(8):
                P.do("tensor", lambda e, ic=ic, hc=hc: e.matmul(
                    PS2[:, 0:128], wshg[:, hc * DI + ic * 128: hc * DI + (ic + 1) * 128],
                    t1hb[:, hc * 128:(hc + 1) * 128], start=(hc == 0), stop=(hc == 7)))
            for hc in range(8):
                P.do("tensor", lambda e, ic=ic, hc=hc: e.matmul(
                    PS3[:, 0:128], wshu[:, hc * DI + ic * 128: hc * DI + (ic + 1) * 128],
                    t1hb[:, hc * 128:(hc + 1) * 128], start=(hc == 0), stop=(hc == 7)))
            P.inc("tensor", sT)
            P.wait("scalar", sT)
            P.do("scalar", lambda e: e.activation(out=tkA[:, 0:128], in_=PS2[:, 0:128], func=AF.Silu))
            P.inc("scalar", sA)
            P.wait("vector", sA)
            P.do("vector", lambda e: e.tensor_tensor(out=tkA[:, 128:256], in0=tkA[:, 0:128], in1=PS3[:, 0:128], op=ALU.mult))
            P.do("vector", lambda e, ic=ic: e.tensor_copy(out=sgb[:, ic * 128:(ic + 1) * 128], in_=tkA[:, 128:256]))
            P.inc("vector", sV)
            P.wait("tensor", sV)
        for ic in range(4):
            for q in range(2):
                P.do("tensor", lambda e, ic=ic, q=q: e.matmul(
                    PS1[:, q * 512:(q + 1) * 512], sgb[:, ic * 128:(ic + 1) * 128],
                    wsd[:, ic * H + q * 512: ic * H + (q + 1) * 512],
                    start=(ic == 0), stop=(ic == 3)))
        P.inc("tensor", sT)
        P.wait("sync", sV)
        P.wait("sync", sD)
        P.dma("sync", tkA[:], x1_scr[g * 128:(g + 1) * 128, :], sLD)
        P.wait("vector", sT)
        P.do("vector", lambda e: e.tensor_tensor(out=tkC[:], in0=ctr[:], in1=PS1[:], op=ALU.add))
        ln_stats(tkC[:])
        ln_apply(tkC[:], tkB[:])
        P.wait("vector", sLD)
        P.do("vector", lambda e: e.tensor_tensor(out=tkC[:], in0=tkA[:], in1=tkB[:], op=ALU.add))
        P.inc("vector", sV)
        P.wait("gpsimd", sV)
        P.dma("gpsimd", out[g * 128:(g + 1) * 128, :], tkC[:], sFIN)
        P.wait("tensor", sV)
        P.wait("scalar", sLD)

    P.wait("gpsimd", sFIN)
    stack.close()
    _emit(nc, P)
    return nc


def _emit(nc, P):
    with nc.Block() as block:
        @block.sync
        def _(eng):
            for f in P.ops["sync"]:
                f(eng)

        @block.scalar
        def _(eng):
            for f in P.ops["scalar"]:
                f(eng)

        @block.vector
        def _(eng):
            for f in P.ops["vector"]:
                f(eng)

        @block.tensor
        def _(eng):
            for f in P.ops["tensor"]:
                f(eng)

        @block.gpsimd
        def _(eng):
            for f in P.ops["gpsimd"]:
                f(eng)


def _consts():
    ident = np.eye(128, dtype=bfdt)
    strip = np.zeros((128, 255), dtype=bfdt)
    strip[:, 127] = 1
    tri = np.triu(np.ones((128, 128), np.float32), 1).astype(bfdt)
    ones = np.ones((1, 128), np.float32)
    iotae = np.tile(np.arange(E, dtype=np.float32)[None, :], (128, 1))
    bmask = np.zeros((128, 16), np.float32)
    for r in range(128):
        bmask[r, r // 8] = 1
    return ident, strip, tri, ones, iotae, bmask.astype(bfdt)


def make_in_maps(inputs):
    f32 = np.float32
    x = np.ascontiguousarray(np.asarray(inputs["L0_attn_x"], f32).reshape(B * L, H))
    ident, strip, tri, ones, iotae, bmask = _consts()
    al = np.asarray(inputs["L0_attn_A_log"], f32)
    in_maps = []
    for core in range(NCORES):
        t0c = core * S
        halo = np.zeros((2, H), f32)
        if core % 4 != 0:
            halo[:] = x[t0c - 2:t0c]
        dblk = core % 4
        m = {
            "x_slice": np.ascontiguousarray(x[t0c:t0c + S]),
            "x_halo": halo,
            "ln_gamma": np.asarray(inputs["L0_attn_ln_gamma"], f32).reshape(1, H),
            "ln_beta": np.asarray(inputs["L0_attn_ln_beta"], f32).reshape(1, H),
            "w_up": np.ascontiguousarray(np.asarray(inputs["L0_attn_W_up"], f32)),
            "b_up": np.asarray(inputs["L0_attn_b_up"], f32).reshape(1, H),
            "conv_w": np.ascontiguousarray(np.asarray(inputs["L0_attn_conv_w"], f32).reshape(DI, 3)),
            "conv_b": np.asarray(inputs["L0_attn_conv_b"], f32).reshape(DI, 1),
            "w_delta": np.ascontiguousarray(np.asarray(inputs["L0_attn_W_delta"], f32)),
            "b_delta": np.asarray(inputs["L0_attn_b_delta"], f32).reshape(DI, 1),
            "w_B": np.ascontiguousarray(np.asarray(inputs["L0_attn_W_B"], f32)),
            "w_C": np.ascontiguousarray(np.asarray(inputs["L0_attn_W_C"], f32)),
            "a_log_blk": np.ascontiguousarray(al[dblk * 128:(dblk + 1) * 128]),
            "d_vec": np.asarray(inputs["L0_attn_D"], f32).reshape(DI, 1),
            "w_down": np.ascontiguousarray(np.asarray(inputs["L0_attn_W_down"], f32)),
            "b_down": np.asarray(inputs["L0_attn_b_down"], f32).reshape(1, H),
            "router_w": np.ascontiguousarray(np.asarray(inputs["L0_ffn_router_w"], f32)),
            "gate_w_blk": np.ascontiguousarray(
                np.asarray(inputs["L0_ffn_gate_w"], f32)[core * EPC:(core + 1) * EPC].reshape(EPC * H, DI)),
            "up_w_blk": np.ascontiguousarray(
                np.asarray(inputs["L0_ffn_up_w"], f32)[core * EPC:(core + 1) * EPC].reshape(EPC * H, DI)),
            "down_w_blk": np.ascontiguousarray(
                np.asarray(inputs["L0_ffn_down_w"], f32)[core * EPC:(core + 1) * EPC].reshape(EPC * DI, H)),
            "sh_gate": np.ascontiguousarray(np.asarray(inputs["L0_ffn_shared_gate_w"], f32)),
            "sh_up": np.ascontiguousarray(np.asarray(inputs["L0_ffn_shared_up_w"], f32)),
            "sh_down": np.ascontiguousarray(np.asarray(inputs["L0_ffn_shared_down_w"], f32)),
            "c_ident": ident, "c_strip": strip, "c_tri": tri, "c_ones": ones,
            "c_iotae": iotae, "c_bmask": bmask,
        }
        m["halo_mask"] = np.full((128, 1), 0.0 if core % 4 == 0 else 1.0, f32)
        rm = np.zeros((NCORES, 1), f32)
        rm[:core] = 1.0
        m["rank_mask"] = rm
        m["tok0"] = (core * S + np.arange(128, dtype=f32))[:, None].copy()

        bat = core // 4
        gf = []
        for rr in range(4):
            for arr in range(2):
                base = (4 * bat + rr) * 2 * DI + arr * DI + dblk * 128
                gf.append(base + np.arange(128))
        m["gidx_f32"] = np.stack(gf, axis=1).astype(np.int32)
        gb = []
        for rr in range(4):
            for tau in range(2):
                for arr in range(2):
                    base = (4 * bat + rr) * 2 * NS + arr * NS + tau * 128
                    gb.append(base + np.arange(128))
        m["gidx_bf"] = np.stack(gb, axis=1).astype(np.int32)
        gy = []
        for rr in range(4):
            rows = ((4 * bat + rr) * 128 + np.arange(128)) * 4 + (core % 4)
            gy.append(rows)
        m["gidx_ys"] = np.stack(gy, axis=1).astype(np.int32)
        m["iota128"] = np.arange(128, dtype=f32)[:, None].copy()
        m["ebase"] = np.tile((512.0 * (core * EPC + np.arange(8))).astype(f32)[None, :], (128, 1))
        in_maps.append(m)
    return in_maps


def kernel(**inputs):
    in_maps = make_in_maps(inputs)
    nc = build_nc(0)
    res = run_bass_kernel_spmd(nc, in_maps, core_ids=list(range(NCORES)))
    outs = [res.results[c]["out"] for c in range(NCORES)]
    return np.concatenate(outs, 0).reshape(B, L, H)


def run_partial(inputs, upto, taps=()):
    in_maps = make_in_maps(inputs)
    nc = build_nc(0, taps=taps, upto=upto)
    res = run_bass_kernel_spmd(nc, in_maps, core_ids=list(range(NCORES)))
    return res


# version 06 (final)
# speedup vs baseline: 1.1349x; 1.0119x over previous
import contextlib
import numpy as np
import ml_dtypes
import concourse.bass as bass
import concourse.mybir as mybir
from concourse.bass_utils import run_bass_kernel_spmd
from concourse.bass_core_rust import IndirectOffsetOnAxis

F32 = mybir.dt.float32
BF16 = mybir.dt.bfloat16
I16 = mybir.dt.int16
U16 = mybir.dt.uint16
AF = mybir.ActivationFunctionType
ALU = mybir.AluOpType
AX = mybir.AxisListType

NCORES = 8
B, L, H = 2, 8192, 1024
DI, NS = 512, 256
S = 2048
NT = S // 128
E, CAP, TOPK, G, TG = 64, 512, 8, 8, 4
MOE_SCALE = 2.5
EPC = E // NCORES
CH = 1024
NCH = L // CH
LNEPS = 1e-5

bfdt = ml_dtypes.bfloat16


class Sem:
    def __init__(self, h):
        self.h = h
        self.n = 0


class Prog:
    ENGINES = ("sync", "scalar", "vector", "tensor", "gpsimd")

    AUTO = ("scalar", "vector")

    def __init__(self):
        self.ops = {e: [] for e in self.ENGINES}
        self.last_inc = {e: None for e in self.ENGINES}
        self.chain = {}

    def do(self, eng, fn, auto=True):
        if auto and eng in self.AUTO and eng in self.chain:
            ch = self.chain[eng]
            prev = ch.n
            ch.n += 1

            def f(e, fn=fn, ch=ch, prev=prev):
                if prev > 0:
                    e.wait_ge(ch.h, prev)
                return fn(e).then_inc(ch.h, 1)
            f._wrapped = True
            self.ops[eng].append(f)
            self.last_inc[eng] = (ch, ch.n)
        else:
            self.ops[eng].append(fn)

    def dma(self, eng, out, in_, sem, slow=False):
        sem.n += 16

        def f(e, out=out, in_=in_, s=sem, slow=slow):
            if slow:
                e.dma_start(out=out, in_=in_, allow_slow_non_contiguous=True).then_inc(s.h, 16)
            else:
                e.dma_start(out=out, in_=in_).then_inc(s.h, 16)
        self.ops[eng].append(f)

    def cc(self, kind, op, ins, outs, sem):
        sem.n += 1

        def f(e, k=kind, op=op, ins=ins, outs=outs, s=sem):
            e.collective_compute(k, op, [list(range(NCORES))], ins, outs).then_inc(s.h, 1)
        self.ops["gpsimd"].append(f)

    def inc(self, eng, sem, n=1):
        ops = self.ops[eng]
        if eng in self.AUTO and eng in self.chain:
            if sem is self.chain[eng]:
                return
            ch = self.chain[eng]
            v = ch.n
            sem.n += n

            def f0(e, ch=ch, v=v, s=sem, n=n):
                if v > 0:
                    e.wait_ge(ch.h, v)
                return e.sem_inc(s.h, n)
            ops.append(f0)
            return
        sem.n += n
        if not ops:
            def fz(e, s=sem, n=n):
                return e.sem_inc(s.h, n)
            ops.append(fz)
            return
        last = ops[-1]
        if getattr(last, "_wrapped", False):
            li = self.last_inc.get(eng)
            lv = li[1] if li else 0
            lsem = li[0] if li else None

            def f1(e, s=sem, n=n, lsem=lsem, lv=lv):
                if lsem is not None and lv > 0:
                    e.wait_ge(lsem.h, lv)
                return e.sem_inc(s.h, n)
            ops.append(f1)
            return

        def f(e, last=last, s=sem, n=n):
            r = last(e)
            try:
                return r.then_inc(s.h, n)
            except Exception:
                return e.sem_inc(s.h, n)
        f._wrapped = True
        ops[-1] = f
        self.last_inc[eng] = (sem, sem.n)

    def wait(self, eng, sem, val=None):
        v = sem.n if val is None else val

        def f(e, s=sem, v=v):
            e.wait_ge(s.h, v)
        self.ops[eng].append(f)

    def fullbar(self, sem):
        for e in self.ENGINES:
            ops = self.ops[e]
            if ops and not getattr(ops[-1], "_wrapped", False):
                self.inc(e, sem)
            else:
                li = self.last_inc[e]
                if li is not None:
                    self.wait(e, li[0], li[1])
                sem.n += 1

                def f0(e2, s=sem):
                    return e2.sem_inc(s.h, 1)
                ops.append(f0)
        for e in self.ENGINES:
            self.wait(e, sem)



def _bctt(e, out, in0, in1, op):
    a, b = bass.broadcast_tensor_aps(in0, in1)
    return e.tensor_tensor(out=out, in0=a, in1=b, op=op)


def build_nc(core, taps=(), upto="all"):
    rank = core
    batch = core // 4
    dblk = core % 4
    batch_start = (core % 4 == 0)

    nc = bass.Bass("TRN2", target_bir_lowering=False, detect_race_conditions=False)

    def din(name, shape, dt=F32):
        return nc.dram_tensor(name, shape, dt, kind="ExternalInput").ap()

    def dout(name, shape, dt=F32):
        return nc.dram_tensor(name, shape, dt, kind="ExternalOutput").ap()

    def dint(name, shape, dt=F32):
        return nc.dram_tensor(name, shape, dt, kind="Internal").ap()

    def dsh(name, shape, dt=F32):
        return nc.dram_tensor(name, shape, dt, kind="Internal",
                              addr_space="Shared").ap()

    x_in = din("x_slice", [S, H])
    x_halo = din("x_halo", [2, H])
    ln_g = din("ln_gamma", [1, H])
    ln_b = din("ln_beta", [1, H])
    w_up = din("w_up", [H, H])
    b_up = din("b_up", [1, H])
    conv_w = din("conv_w", [DI, 3])
    conv_b = din("conv_b", [DI, 1])
    w_delta = din("w_delta", [DI, DI])
    b_delta = din("b_delta", [DI, 1])
    w_B = din("w_B", [DI, NS])
    w_C = din("w_C", [DI, NS])
    a_log = din("a_log_blk", [128, NS])
    d_vec = din("d_vec", [DI, 1])
    w_down = din("w_down", [DI, H])
    b_down = din("b_down", [1, H])
    router_w = din("router_w", [H, E])
    gate_w = din("gate_w_blk", [EPC * H, DI])
    up_w = din("up_w_blk", [EPC * H, DI])
    down_w = din("down_w_blk", [EPC * DI, H])
    sh_g = din("sh_gate", [H, DI])
    sh_u = din("sh_up", [H, DI])
    sh_d = din("sh_down", [DI, H])
    halom = din("halo_mask", [128, 1])
    rankmask_in = din("rank_mask", [NCORES, 1])
    tok0_in = din("tok0", [128, 1])
    I32 = mybir.dt.int32
    gidx_f32 = din("gidx_f32", [128, 8], I32)
    gidx_bf = din("gidx_bf", [128, 16], I32)
    gidx_ys = din("gidx_ys", [128, 4], I32)
    iota128 = din("iota128", [128, 1])
    ebaset_in = din("ebase", [128, 8])
    ident = din("c_ident", [128, 128], BF16)
    strip = din("c_strip", [128, 255], BF16)
    tri = din("c_tri", [128, 128], BF16)
    onesb = din("c_ones", [1, 128], F32)
    iotae = din("c_iotae", [128, E])
    bmask = din("c_bmask", [128, 16], BF16)

    out = dout("out", [S, H])

    pre_f32 = dint("pre_f32", [2 * DI, S])
    ag_f32 = dsh("ag_f32", [NCORES * 2 * DI, S])
    pre_bf = dint("pre_bf", [2 * NS, S], BF16)
    ag_bf = dsh("ag_bf", [NCORES * 2 * NS, S], BF16)
    uT_scr = dint("uT_scr", [DI, S])
    srT_scr = dint("srT_scr", [DI, S])
    ysT_loc = dint("ysT_loc", [128, L])
    ag_ys = dsh("ag_ys", [NCORES * 128, L])
    x1_scr = dint("x1_scr", [S, H])
    t1_loc = dint("t1_loc", [S, H], BF16)
    ag_t1 = dsh("ag_t1", [B * L, H], BF16)
    cnt_loc = dint("cnt_loc", [1, E])
    ag_cnt = dsh("ag_cnt", [NCORES, E])
    rec_islot = dint("rec_islot", [1, S * TOPK])
    rec_slotc = dint("rec_slotc", [1, S * TOPK])
    rec_wv = dint("rec_wv", [1, S * TOPK])
    rec_pairs = dint("rec_pairs", [S * TOPK, 2])
    T_loc = dint("T_loc", [E * CAP + 256, 2])
    T_glob = dsh("T_glob", [E * CAP + 256, 2])
    T_all = dint("T_all", [NCORES * (E * CAP + 256), 2])
    eo_loc = dint("eo_loc", [EPC * CAP, H], BF16)
    ag_eo = dsh("ag_eo", [E * CAP, H], BF16)

    tap_aps = {}
    for tname, tshape, tdt in taps:
        tap_aps[tname] = dout("tap_" + tname, tshape, tdt)

    P = Prog()
    stack = contextlib.ExitStack()
    _cms = {}

    def sbuf(name, shape, dt=F32):
        cm = nc.sbuf_tensor(name, shape, dt)
        t = cm.__enter__()
        _cms[name] = cm
        return t

    def sbfree(*names):
        for n in names:
            _cms.pop(n).__exit__(None, None, None)

    def psum(name, shape, dt=F32):
        return stack.enter_context(nc.psum_tensor(name, shape, dt))

    def mksem(name):
        return Sem(stack.enter_context(nc.semaphore(name)))

    tkA = sbuf("tkA", [128, H])
    tkB = sbuf("tkB", [128, H])
    tkC = sbuf("tkC", [128, H])
    sc1 = sbuf("sc1", [128, 1])
    sc2 = sbuf("sc2", [128, 1])
    sc3 = sbuf("sc3", [128, 1])
    sc4 = sbuf("sc4", [128, 1])
    sqsc = sbuf("sqsc", [128, H])
    big0 = sbuf("big0", [128, 4 * S])
    big1 = sbuf("big1", [128, 4 * S])
    wdl = sbuf("wdl", [128, 4 * DI], BF16)
    wBc = sbuf("wBc", [128, 4 * NS], BF16)
    wCc = sbuf("wCc", [128, 4 * NS], BF16)
    convw = sbuf("convw", [128, 12])
    convbc = sbuf("convbc", [128, 4])
    bupc = sbuf("bupc", [128, 8])
    bdlc = sbuf("bdlc", [128, 4])
    bdnc = sbuf("bdnc", [128, 8])
    dvecc = sbuf("dvecc", [128, 4])
    identt = sbuf("identt", [128, 128], BF16)
    identf = sbuf("identf", [128, 128])
    stript = sbuf("stript", [128, 255], BF16)
    trit = sbuf("trit", [128, 128], BF16)
    onest = sbuf("onest", [1, 128])
    iotaet = sbuf("iotaet", [128, E])
    bmaskt = sbuf("bmaskt", [128, 16], BF16)
    halomt = sbuf("halomt", [128, 1])
    epst = sbuf("epst", [128, 1])
    onec = sbuf("onec", [128, 1])
    gixf = sbuf("gixf", [128, 8], I32)
    gixb = sbuf("gixb", [128, 16], I32)
    gixy = sbuf("gixy", [128, 4], I32)
    iot128 = sbuf("iot128", [128, 1])
    ebaset = sbuf("ebaset", [128, 8])
    halT = sbuf("halT", [128, 16], BF16)

    PS0 = psum("PS0", [128, 1024])
    PS1 = psum("PS1", [128, 1024])
    PS2 = psum("PS2", [128, 1024])
    PS3 = psum("PS3", [128, 1024])

    sLD = mksem("sLD")
    sW = mksem("sW")
    sV = mksem("sV")
    sA = mksem("sA")
    sT = mksem("sT")
    sG = mksem("sG")
    sD = mksem("sD")
    sCC = mksem("sCC")
    sOUT = mksem("sOUT")
    sGA = mksem("sGA")
    sFIN = mksem("sFIN")
    P.chain = {"vector": sV, "scalar": sA}

    def ln_stats(src_ap):
        P.inc("vector", sV)
        P.wait("scalar", sV)
        P.do("vector", lambda e, a=src_ap: e.tensor_reduce(
            out=sc3[:, 0:1], in_=a, axis=AX.X, op=ALU.add))
        P.do("scalar", lambda e, a=src_ap: e.activation(
            out=sqsc[:], in_=a, func=AF.Square))
        P.inc("scalar", sA)
        P.wait("vector", sA)
        P.do("vector", lambda e: e.tensor_reduce(
            out=sc4[:, 0:1], in_=sqsc[:], axis=AX.X, op=ALU.add))
        P.do("vector", lambda e: e.tensor_scalar(
            out=sc3[:, 0:1], in0=sc3[:, 0:1], scalar1=1.0 / H, scalar2=None, op0=ALU.mult))
        P.do("vector", lambda e: e.tensor_scalar(
            out=sc4[:, 0:1], in0=sc4[:, 0:1], scalar1=1.0 / H, scalar2=None, op0=ALU.mult))
        P.do("vector", lambda e: e.tensor_tensor(
            out=sc1[:, 0:1], in0=sc3[:, 0:1], in1=sc3[:, 0:1], op=ALU.mult))
        P.do("vector", lambda e: e.tensor_tensor(
            out=sc4[:, 0:1], in0=sc4[:, 0:1], in1=sc1[:, 0:1], op=ALU.subtract))
        P.inc("vector", sV)
        P.wait("scalar", sV)
        P.do("scalar", lambda e: e.activation(
            out=sc1[:, 0:1], in_=sc4[:, 0:1], func=AF.Sqrt, bias=epst[:, 0:1]))
        P.inc("scalar", sA)
        P.wait("vector", sA)
        P.do("vector", lambda e: e.reciprocal(out=sc2[:, 0:1], in_=sc1[:, 0:1]))
        P.inc("vector", sV)
        P.wait("vector", sV)

    def ln_apply(src_ap, dst_ap):
        P.do("vector", lambda e, a=src_ap, d=dst_ap: e.tensor_scalar(
            out=d, in0=a, scalar1=sc3[:, 0:1], scalar2=sc2[:, 0:1],
            op0=ALU.subtract, op1=ALU.mult))

    P.dma("sync", identt[:], ident, sW)
    P.dma("sync", stript[:], strip, sW)
    P.dma("sync", trit[:], tri, sW)
    P.dma("sync", onest[:], onesb, sW)
    P.dma("sync", iotaet[:], iotae, sW)
    P.dma("sync", bmaskt[:], bmask, sW)
    P.dma("sync", halomt[:], halom, sW)
    P.dma("sync", gixf[:], gidx_f32, sW)
    P.dma("sync", gixb[:], gidx_bf, sW)
    P.dma("sync", gixy[:], gidx_ys, sW)
    P.dma("sync", iot128[:], iota128, sW)
    P.dma("sync", ebaset[:], ebaset_in, sW)
    P.dma("sync", convw[:].rearrange("p (c k) -> p c k", k=3), conv_w.rearrange("(c p) k -> p c k", p=128), sW, slow=True)
    P.dma("sync", convbc[:].rearrange("p (c o) -> p c o", o=1), conv_b.rearrange("(c p) o -> p c o", p=128), sW, slow=True)
    P.dma("sync", bupc[:].rearrange("p (o c) -> p o c", o=1), b_up.rearrange("o (c p) -> p o c", p=128), sW, slow=True)
    P.dma("sync", bdlc[:].rearrange("p (c o) -> p c o", o=1), b_delta.rearrange("(c p) o -> p c o", p=128), sW, slow=True)
    P.dma("sync", bdnc[:].rearrange("p (o c) -> p o c", o=1), b_down.rearrange("o (c p) -> p o c", p=128), sW, slow=True)
    P.dma("sync", dvecc[:].rearrange("p (c o) -> p c o", o=1), d_vec.rearrange("(c p) o -> p c o", p=128), sW, slow=True)
    def wconv(dst_tile, dst_off, src_ap, cols):
        P.wait("sync", sA)
        P.wait("sync", sV)
        P.dma("sync", tkA[:, 0:cols], src_ap, sLD)
        P.wait("scalar", sLD)
        P.wait("scalar", sT)
        P.do("scalar", lambda e, d=dst_tile, o=dst_off, c=cols: e.copy(
            out=d[:, o:o + c], in_=tkA[:, 0:c]))
        P.inc("scalar", sA)

    for c4 in range(4):
        wconv(wdl, c4 * DI, w_delta[c4 * 128:(c4 + 1) * 128, :], DI)
        wconv(wBc, c4 * NS, w_B[c4 * 128:(c4 + 1) * 128, :], NS)
        wconv(wCc, c4 * NS, w_C[c4 * 128:(c4 + 1) * 128, :], NS)

    wupT = sbuf("wupT", [128, 8 * H], BF16)
    xaT = sbuf("xaT", [128, 8 * S], BF16)
    gamt = sbuf("gamt", [128, H])
    bett = sbuf("bett", [128, H])
    P.dma("sync", gamt[0:1, :], ln_g, sW)
    P.dma("sync", bett[0:1, :], ln_b, sW)
    P.wait("scalar", sW)
    P.do("scalar", lambda e: e.copy(out=identf[:], in_=identt[:]))
    P.inc("scalar", sA)
    P.do("vector", lambda e: e.memset(epst[:], LNEPS))
    P.do("vector", lambda e: e.memset(onec[:], 1.0))
    P.inc("vector", sV)
    for c4 in range(8):
        wconv(wupT, c4 * H, w_up[c4 * 128:(c4 + 1) * 128, :], H)
    P.wait("tensor", sW)
    for half in range(2):
        P.do("tensor", lambda e, h=half: e.matmul(
            PS0[:, h * 512:(h + 1) * 512], onest[0:1, :],
            gamt[0:1, h * 512:(h + 1) * 512], start=True, stop=True))
        P.do("tensor", lambda e, h=half: e.matmul(
            PS1[:, h * 512:(h + 1) * 512], onest[0:1, :],
            bett[0:1, h * 512:(h + 1) * 512], start=True, stop=True))
    P.inc("tensor", sT)
    P.wait("scalar", sT)
    P.do("scalar", lambda e: e.copy(out=gamt[:], in_=PS0[:]))
    P.do("scalar", lambda e: e.copy(out=bett[:], in_=PS1[:]))
    P.inc("scalar", sA)
    P.wait("tensor", sA)
    P.wait("vector", sA)

    def tileproc(g, src_rows, halo=False):
        P.wait("sync", sV)
        P.wait("sync", sA)
        P.dma("sync", tkA[0:2, :] if halo else tkA[:], src_rows, sLD)
        P.wait("vector", sLD)
        P.wait("scalar", sLD)
        ln_stats(tkA[:])
        ln_apply(tkA[:], tkB[:])
        ln_stats(tkB[:])
        ln_apply(tkB[:], tkC[:])
        P.do("vector", lambda e: e.tensor_tensor(out=tkC[:], in0=tkC[:], in1=gamt[:], op=ALU.mult))
        P.do("vector", lambda e: e.tensor_tensor(out=tkC[:], in0=tkC[:], in1=bett[:], op=ALU.add))
        P.inc("vector", sV)
        if g == 0 and not halo and "t0tile" in tap_aps:
            P.wait("gpsimd", sV)
            P.dma("gpsimd", tap_aps["t0tile"], tkB[:], sD)
            P.dma("gpsimd", tap_aps["xatile"], tkC[:], sD)
            P.dma("gpsimd", tap_aps["gamtile"], gamt[:], sD)
            P.wait("tensor", sD)
        P.wait("tensor", sV)
        for hc in range(8):
            pslot = PS2 if hc % 2 == 0 else PS3
            P.do("tensor", lambda e, hc=hc, p=pslot: e.transpose(
                p[:, 0:128], tkC[:, hc * 128:(hc + 1) * 128], identf[:]))
            P.inc("tensor", sT)
            P.wait("scalar", sT)
            if halo:
                P.do("scalar", lambda e, hc=hc, p=pslot: e.copy(
                    out=halT[:, hc * 2:hc * 2 + 2], in_=p[:, 0:2]))
            else:
                P.do("scalar", lambda e, hc=hc, p=pslot, g=g: e.copy(
                    out=xaT[:, hc * S + g * 128: hc * S + (g + 1) * 128], in_=p[:, 0:128]))
            P.inc("scalar", sA)
            P.wait("tensor", sA)

    for g in range(NT):
        tileproc(g, x_in[g * 128:(g + 1) * 128, :])
    P.wait("vector", sA)
    P.do("vector", lambda e: e.memset(tkA[:], 0.0))
    P.inc("vector", sV)
    tileproc(0, x_halo[0:2, :], halo=True)
    P.wait("vector", sA)
    P.do("vector", lambda e: e.tensor_scalar(
        out=halT[:], in0=halT[:], scalar1=halomt[:, 0:1], scalar2=None, op0=ALU.mult))
    P.inc("vector", sV)

    P.fullbar(sOUT)
    sbfree("bett", "gamt")
    leftT = sbuf("leftT", [128, 4 * (S + 8)])
    big3 = sbuf("big3", [128, 4 * S])
    P.do("vector", lambda e: e.memset(leftT[:, 0:2 * (S + 8)], 0.0))
    P.do("vector", lambda e: e.memset(leftT[:, 2 * (S + 8):], 0.0))
    P.inc("vector", sV)
    P.wait("tensor", sV)
    P.wait("tensor", sA)
    for oc in range(8):
        for th in range(2):
            pslot = PS0 if th == 0 else PS1
            for kc in range(8):
                for q in range(2):
                    P.do("tensor", lambda e, oc=oc, th=th, kc=kc, q=q, p=pslot: e.matmul(
                        p[:, q * 512:(q + 1) * 512],
                        wupT[:, kc * H + oc * 128: kc * H + (oc + 1) * 128],
                        xaT[:, kc * S + th * 1024 + q * 512: kc * S + th * 1024 + (q + 1) * 512],
                        start=(kc == 0), stop=(kc == 7)))
            P.inc("tensor", sT)
            P.wait("scalar", sT)
            if oc < 4:
                P.do("scalar", lambda e, oc=oc, th=th, p=pslot: e.activation(
                    out=leftT[:, oc * (S + 8) + 4 + th * 1024: oc * (S + 8) + 4 + (th + 1) * 1024],
                    in_=p[:], func=AF.Identity, bias=bupc[:, oc:oc + 1]))
            else:
                P.do("scalar", lambda e, oc=oc, th=th, p=pslot: e.activation(
                    out=big3[:, (oc - 4) * S + th * 1024: (oc - 4) * S + (th + 1) * 1024],
                    in_=p[:], func=AF.Silu, bias=bupc[:, oc:oc + 1]))
            P.inc("scalar", sA)
            P.wait("tensor", sA)
    if True:
        for oc in range(4):
            for kc in range(8):
                P.do("tensor", lambda e, oc=oc, kc=kc: e.matmul(
                    PS0[:, 0:2],
                    wupT[:, kc * H + oc * 128: kc * H + (oc + 1) * 128],
                    halT[:, kc * 2:(kc + 1) * 2],
                    start=(kc == 0), stop=(kc == 7)))
            P.inc("tensor", sT)
            P.wait("scalar", sT)
            P.do("scalar", lambda e, oc=oc: e.activation(
                out=leftT[:, oc * (S + 8) + 2: oc * (S + 8) + 4],
                in_=PS0[:, 0:2], func=AF.Identity, bias=bupc[:, oc:oc + 1]))
            P.inc("scalar", sA)
            P.wait("tensor", sA)
            P.wait("vector", sA)
            P.do("vector", lambda e, oc=oc: e.tensor_scalar(
                out=leftT[:, oc * (S + 8) + 2: oc * (S + 8) + 4],
                in0=leftT[:, oc * (S + 8) + 2: oc * (S + 8) + 4],
                scalar1=halomt[:, 0:1], scalar2=None, op0=ALU.mult))
            P.inc("vector", sV)
            P.wait("scalar", sV)
    P.wait("gpsimd", sA)
    P.dma("gpsimd", srT_scr.rearrange("(c p) t -> p c t", p=128), big3[:].rearrange("p (c t) -> p c t", t=S), sD)
    if "xaT" in tap_aps:
        P.dma("gpsimd", tap_aps["xaT"], xaT[:], sD)
    if "leftT" in tap_aps:
        P.dma("gpsimd", tap_aps["leftT"], leftT[:], sD)

    P.wait("vector", sA)
    for dc in range(4):
        b0 = dc * (S + 8)
        sl = slice(dc * S, (dc + 1) * S)
        P.do("vector", lambda e, b=b0, sl=sl, dc=dc: e.tensor_scalar(
            out=big0[:, sl], in0=leftT[:, b + 2:b + 2 + S],
            scalar1=convw[:, dc * 3 + 0: dc * 3 + 1], scalar2=None, op0=ALU.mult))
        P.do("vector", lambda e, b=b0, sl=sl, dc=dc: e.scalar_tensor_tensor(
            out=big0[:, sl], in0=leftT[:, b + 3:b + 3 + S],
            scalar=convw[:, dc * 3 + 1: dc * 3 + 2], in1=big0[:, sl],
            op0=ALU.mult, op1=ALU.add))
        P.do("vector", lambda e, b=b0, sl=sl, dc=dc: e.scalar_tensor_tensor(
            out=big0[:, sl], in0=leftT[:, b + 4:b + 4 + S],
            scalar=convw[:, dc * 3 + 2: dc * 3 + 3], in1=big0[:, sl],
            op0=ALU.mult, op1=ALU.add))
    P.inc("vector", sV)
    P.wait("scalar", sV)
    for dc in range(4):
        sl = slice(dc * S, (dc + 1) * S)
        P.do("scalar", lambda e, sl=sl, dc=dc: e.activation(
            out=big0[:, sl], in_=big0[:, sl], func=AF.Silu, bias=convbc[:, dc:dc + 1]))
    P.inc("scalar", sA)
    P.wait("gpsimd", sA)
    P.dma("gpsimd", uT_scr.rearrange("(c p) t -> p c t", p=128), big0[:].rearrange("p (c t) -> p c t", t=S), sD)

    P.fullbar(sOUT)
    sbfree("big3", "leftT", "xaT", "wupT")
    uTb = sbuf("uTb", [128, 4 * S], BF16)
    big2 = sbuf("big2", [128, 4 * S])
    bmstage = sbuf("bmstage", [128, 4 * S], BF16)
    P.wait("vector", sA)
    P.do("vector", lambda e: e.tensor_copy(out=uTb[:], in_=big0[:]))
    P.inc("vector", sV)
    P.wait("tensor", sV)
    for dc in range(4):
        for th in range(2):
            pslot = PS0 if th == 0 else PS1
            for kc in range(4):
                for q in range(2):
                    P.do("tensor", lambda e, dc=dc, th=th, kc=kc, q=q, p=pslot: e.matmul(
                        p[:, q * 512:(q + 1) * 512],
                        wdl[:, kc * DI + dc * 128: kc * DI + (dc + 1) * 128],
                        uTb[:, kc * S + th * 1024 + q * 512: kc * S + th * 1024 + (q + 1) * 512],
                        start=(kc == 0), stop=(kc == 3)))
            P.inc("tensor", sT)
            P.wait("scalar", sT)
            P.do("scalar", lambda e, dc=dc, th=th, p=pslot: e.activation(
                out=big1[:, dc * S + th * 1024: dc * S + (th + 1) * 1024],
                in_=p[:], func=AF.Identity, bias=bdlc[:, dc:dc + 1]))
            P.inc("scalar", sA)
            P.wait("tensor", sA)
    P.wait("vector", sA)
    for dc in range(4):
        sl = slice(dc * S, (dc + 1) * S)
        P.do("vector", lambda e, sl=sl: e.tensor_scalar(
            out=big2[:, sl], in0=big1[:, sl], scalar1=-1.0, scalar2=None, op0=ALU.mult))
        P.do("vector", lambda e, sl=sl: e.tensor_tensor(
            out=big2[:, sl], in0=big1[:, sl], in1=big2[:, sl], op=ALU.min))
        P.inc("vector", sV)
        P.wait("scalar", sV)
        P.do("scalar", lambda e, sl=sl: e.activation(
            out=big2[:, sl], in_=big2[:, sl], func=AF.Exp))
        P.do("scalar", lambda e, sl=sl: e.activation(
            out=big2[:, sl], in_=big2[:, sl], func=AF.Ln, bias=onec[:, 0:1]))
        P.inc("scalar", sA)
        P.wait("vector", sA)
        P.do("vector", lambda e, sl=sl: e.tensor_relu(out=big1[:, sl], in_=big1[:, sl]))
        P.do("vector", lambda e, sl=sl: e.tensor_tensor(
            out=big1[:, sl], in0=big1[:, sl], in1=big2[:, sl], op=ALU.add))
        P.do("vector", lambda e, sl=sl: e.tensor_tensor(
            out=big2[:, sl], in0=big1[:, sl], in1=big0[:, sl], op=ALU.mult))
    P.inc("vector", sV)
    P.wait("gpsimd", sV)
    P.dma("gpsimd", pre_f32[0:DI, :].rearrange("(c p) t -> p c t", p=128), big1[:].rearrange("p (c t) -> p c t", t=S), sD)
    P.dma("gpsimd", pre_f32[DI:2 * DI, :].rearrange("(c p) t -> p c t", p=128), big2[:].rearrange("p (c t) -> p c t", t=S), sD)
    P.wait("tensor", sV)
    for which in range(2):
        wtile = wBc if which == 0 else wCc
        for nch in range(2):
            for th in range(2):
                pslot = PS0 if th == 0 else PS1
                for kc in range(4):
                    for q in range(2):
                        P.do("tensor", lambda e, w=wtile, ncb=nch, th=th, kc=kc, q=q, p=pslot: e.matmul(
                            p[:, q * 512:(q + 1) * 512],
                            w[:, kc * NS + ncb * 128: kc * NS + (ncb + 1) * 128],
                            uTb[:, kc * S + th * 1024 + q * 512: kc * S + th * 1024 + (q + 1) * 512],
                            start=(kc == 0), stop=(kc == 3)))
                P.inc("tensor", sT)
                P.wait("scalar", sT)
                P.do("scalar", lambda e, w=which, ncb=nch, th=th, p=pslot: e.copy(
                    out=bmstage[:, (w * 2 + ncb) * S + th * 1024: (w * 2 + ncb) * S + (th + 1) * 1024],
                    in_=p[:]))
                P.inc("scalar", sA)
                P.wait("tensor", sA)
    P.wait("gpsimd", sA)
    P.dma("gpsimd", pre_bf.rearrange("(c p) t -> p c t", p=128), bmstage[:].rearrange("p (c t) -> p c t", t=S), sD)
    P.wait("gpsimd", sD)
    P.cc("AllGather", ALU.bypass, [pre_f32.opt()], [ag_f32.opt()], sCC)
    P.wait("gpsimd", sCC)
    P.cc("AllGather", ALU.bypass, [pre_bf.opt()], [ag_bf.opt()], sCC)
    P.wait("gpsimd", sCC)
    P.inc("gpsimd", sG)

    if "preT" in tap_aps:
        P.dma("gpsimd", tap_aps["preT"], pre_f32, sD)
    if "uT" in tap_aps:
        P.dma("gpsimd", tap_aps["uT"], uT_scr, sD)

    if upto == "pre":
        P.wait("gpsimd", sD)
        stack.close()
        _emit(nc, P)
        return nc

    P.fullbar(sOUT)
    sbfree("bmstage", "big2", "uTb")
    dltT = big0
    duT = big1
    bmT = sbuf("bmT", [128, 2 * L], BF16)
    cmT = sbuf("cmT", [128, 2 * L], BF16)
    a_nT = sbuf("a_nT", [128, 2 * 128])
    hbuf = sbuf("hbuf", [128, 2 * 1024])
    hcar = sbuf("hcar", [128, 2 * 128])
    dAt = sbuf("dAt", [128, 4 * 1024])
    trm = sbuf("trm", [128, 2 * 1024])
    ymt = sbuf("ymt", [128, 2 * 1024], BF16)
    ysst = sbuf("ysst", [128, 1024])
    onesf = sbuf("onesf", [128, 128])
    wbc = sbuf("wbc", [128, 2 * 128])

    def igather(eng, dst, table, idxcol, sem):
        sem.n += 16

        def gf(e, dst=dst, table=table, idxcol=idxcol, s=sem):
            return e.indirect_dma_start(
                out=dst, out_offset=None, in_=table,
                in_offset=IndirectOffsetOnAxis(ap=idxcol, axis=0),
            ).then_inc(s.h, 16)
        P.do(eng, gf)

    gi = 0
    for rr in range(4):
        for arr, dst in ((0, dltT[:, rr * S:(rr + 1) * S]), (1, duT[:, rr * S:(rr + 1) * S])):
            igather("gpsimd", dst, ag_f32, gixf[:, gi:gi + 1], sGA)
            gi += 1
    gi = 0
    for rr in range(4):
        for tau in range(2):
            for arr, dst in ((0, bmT[:, tau * L + rr * S: tau * L + (rr + 1) * S]),
                             (1, cmT[:, tau * L + rr * S: tau * L + (rr + 1) * S])):
                igather("gpsimd", dst, ag_bf, gixb[:, gi:gi + 1], sGA)
                gi += 1
    P.do("vector", lambda e: e.memset(onesf[:], 1.0))
    P.inc("vector", sV)
    P.wait("scalar", sV)
    P.dma("sync", tkA[:, 0:NS], a_log, sLD)
    P.wait("scalar", sLD)
    P.do("scalar", lambda e: e.activation(out=tkA[:, 0:NS], in_=tkA[:, 0:NS], func=AF.Exp))
    P.inc("scalar", sA)
    P.wait("vector", sA)
    P.do("vector", lambda e: e.tensor_scalar(
        out=tkA[:, 0:NS], in0=tkA[:, 0:NS], scalar1=-1.0, scalar2=None, op0=ALU.mult))
    P.do("vector", lambda e: e.memset(hcar[:], 0.0))
    P.inc("vector", sV)
    P.wait("tensor", sV)
    for tau in range(2):
        P.do("tensor", lambda e, tau=tau: e.transpose(
            PS2[:, 0:128], tkA[:, tau * 128:(tau + 1) * 128], identf[:]))
        P.inc("tensor", sT)
        P.wait("scalar", sT)
        P.do("scalar", lambda e, tau=tau: e.copy(
            out=a_nT[:, tau * 128:(tau + 1) * 128], in_=PS2[:, 0:128]))
        P.inc("scalar", sA)
        P.wait("tensor", sA)
    P.wait("vector", sA)
    P.wait("vector", sLD)
    P.wait("vector", sGA)
    P.wait("tensor", sLD)
    P.wait("tensor", sGA)
    P.wait("scalar", sLD)
    P.wait("scalar", sGA)

    redmark = [0]
    for c in range(NCH):
        for i in range(128):
            first = (c == 0 and i == 0)
            par = i % 2
            P.do("scalar", lambda e, i=i, par=par: e.activation(
                out=wbc[:, par * 128:(par + 1) * 128], in_=onesf[:],
                func=AF.Copy, scale=identf[:, i:i + 1]))
            P.inc("scalar", sA)
            if not first:
                P.wait("tensor", sV)
            P.wait("tensor", sA)
            for q in range(2):
                P.do("tensor", lambda e, i=i, c=c, q=q, par=par: e.matmul(
                    PS0[:, q * 512:(q + 1) * 512], wbc[:, par * 128:(par + 1) * 128],
                    dltT[:, c * CH + q * 512: c * CH + (q + 1) * 512],
                    start=True, stop=True))
            for q in range(2):
                P.do("tensor", lambda e, i=i, c=c, q=q, par=par: e.matmul(
                    PS1[:, q * 512:(q + 1) * 512], wbc[:, par * 128:(par + 1) * 128],
                    duT[:, c * CH + q * 512: c * CH + (q + 1) * 512],
                    start=True, stop=True))
            P.inc("tensor", sT)
            P.wait("scalar", sT)
            for tau in range(2):
                P.do("scalar", lambda e, tau=tau, i=i, par=par: e.activation(
                    out=dAt[:, (par * 2 + tau) * 1024:(par * 2 + tau + 1) * 1024], in_=PS0[:],
                    func=AF.Exp, scale=a_nT[:, tau * 128 + i: tau * 128 + i + 1]))
            P.inc("scalar", sA)
            P.wait("vector", sT)
            for tau in range(2):
                P.do("vector", lambda e, tau=tau, c=c: e.tensor_tensor(
                    out=trm[:, tau * 1024:(tau + 1) * 1024],
                    in0=bmT[:, tau * L + c * CH: tau * L + (c + 1) * CH],
                    in1=PS1[:], op=ALU.mult))
            P.inc("vector", sV)
            P.wait("vector", sA)
            if not first:
                P.wait("vector", sT, redmark[0])
            for tau in range(2):
                P.do("vector", lambda e, tau=tau, i=i, par=par: e.tensor_tensor_scan(
                    out=hbuf[:, tau * 1024:(tau + 1) * 1024],
                    data0=dAt[:, (par * 2 + tau) * 1024:(par * 2 + tau + 1) * 1024],
                    data1=trm[:, tau * 1024:(tau + 1) * 1024],
                    initial=hcar[:, tau * 128 + i: tau * 128 + i + 1],
                    op0=ALU.mult, op1=ALU.add))
                P.do("vector", lambda e, tau=tau, i=i: e.tensor_copy(
                    out=hcar[:, tau * 128 + i: tau * 128 + i + 1],
                    in_=hbuf[:, tau * 1024 + 1023: tau * 1024 + 1024]))
                P.do("vector", lambda e, tau=tau, c=c: e.tensor_tensor(
                    out=ymt[:, tau * 1024:(tau + 1) * 1024],
                    in0=hbuf[:, tau * 1024:(tau + 1) * 1024],
                    in1=cmT[:, tau * L + c * CH: tau * L + (c + 1) * CH], op=ALU.mult))
                P.inc("vector", sV)
                P.wait("tensor", sV)
                for q in range(2):
                    P.do("tensor", lambda e, i=i, tau=tau, q=q: e.matmul(
                        PS2[:, q * 512:(q + 1) * 512],
                        stript[:, 127 - i:255 - i],
                        ymt[:, tau * 1024 + q * 512: tau * 1024 + (q + 1) * 512],
                        start=(i == 0 and tau == 0), stop=(i == 127 and tau == 1)))
            P.inc("tensor", sT)
            redmark[0] = sT.n
        P.wait("scalar", sT)
        P.do("scalar", lambda e: e.copy(out=ysst[:], in_=PS2[:]))
        P.inc("scalar", sA)
        P.wait("gpsimd", sA)
        P.dma("gpsimd", ysT_loc[:, c * CH:(c + 1) * CH], ysst[:], sD)
        P.wait("gpsimd", sD)
        P.inc("gpsimd", sG)
        P.wait("tensor", sA)

    P.cc("AllGather", ALU.bypass, [ysT_loc.opt()], [ag_ys.opt()], sCC)
    P.wait("gpsimd", sCC)
    if "ysT" in tap_aps:
        P.dma("gpsimd", tap_aps["ysT"], ysT_loc, sD)

    if upto == "scan":
        P.wait("gpsimd", sD)
        stack.close()
        _emit(nc, P)
        return nc

    P.fullbar(sOUT)
    sbfree("wbc", "onesf", "ysst", "ymt", "trm", "dAt", "hcar", "hbuf",
           "a_nT", "cmT", "bmT")
    big2s = sbuf("big2s", [128, 4 * S])
    wdn = sbuf("wdn", [128, 4 * H], BF16)
    bdnT = sbuf("bdnT", [128, H])
    gtb = sbuf("gtb", [128, 4 * S], BF16)
    t1Tg = sbuf("t1Tg", [128, H])
    rwf = sbuf("rwf", [128, 8 * E])
    posb = sbuf("posb", [128, 128])
    ixb = sbuf("ixb", [128, 128])
    wvb = sbuf("wvb", [128, 128])
    ohb = sbuf("ohb", [128, E], BF16)
    crow = sbuf("crow", [1, E])
    brow = sbuf("brow", [128, E])
    agcS = sbuf("agcS", [8, E])
    rkmS = sbuf("rkmS", [8, 1])
    ixt = sbuf("ixt", [128, 8], U16)
    t1bt = sbuf("t1bt", [128, H], BF16)
    ptile = sbuf("ptile", [128, 16])
    tokcol = sbuf("tokcol", [128, 1])
    tok0S = sbuf("tok0S", [128, 1])

    for rr in range(4):
        igather("gpsimd", big0[:, rr * S:(rr + 1) * S],
                ag_ys.rearrange("a (c t) -> (a c) t", t=S), gixy[:, rr:rr + 1], sGA)
    P.dma("sync", big1[:].rearrange("p (c t) -> p c t", t=S),
          uT_scr.rearrange("(c p) t -> p c t", p=128), sLD)
    P.dma("sync", big2s[:].rearrange("p (c t) -> p c t", t=S),
          srT_scr.rearrange("(c p) t -> p c t", p=128), sLD)
    P.dma("sync", rwf[:].rearrange("p (c e) -> p c e", e=E),
          router_w.rearrange("(c p) e -> p c e", p=128), sLD)
    P.dma("sync", rkmS[:], rankmask_in, sLD)
    P.dma("sync", tok0S[:], tok0_in, sLD)
    P.dma("sync", bdnT[0:1, :], b_down, sLD)
    for c4 in range(4):
        wconv(wdn, c4 * H, w_down[c4 * 128:(c4 + 1) * 128, :], H)
    P.wait("tensor", sLD)
    P.wait("tensor", sA)
    for half in range(2):
        P.do("tensor", lambda e, h=half: e.matmul(
            PS0[:, h * 512:(h + 1) * 512], onest[0:1, :],
            bdnT[0:1, h * 512:(h + 1) * 512], start=True, stop=True))
    P.inc("tensor", sT)
    P.wait("scalar", sT)
    P.do("scalar", lambda e: e.copy(out=bdnT[:], in_=PS0[:]))
    P.inc("scalar", sA)
    P.wait("tensor", sA)

    P.wait("vector", sLD)
    P.do("vector", lambda e: e.memset(crow[:], 0.0))
    P.inc("vector", sV)
    P.wait("vector", sGA)
    P.wait("vector", sLD)
    for dc in range(4):
        sl = slice(dc * S, (dc + 1) * S)
        P.do("vector", lambda e, sl=sl, dc=dc: e.scalar_tensor_tensor(
            out=big1[:, sl], in0=big1[:, sl], scalar=dvecc[:, dc:dc + 1],
            in1=big0[:, sl], op0=ALU.mult, op1=ALU.add))
        P.do("vector", lambda e, sl=sl: e.tensor_tensor(
            out=big0[:, sl], in0=big1[:, sl], in1=big2s[:, sl], op=ALU.mult))
        P.do("vector", lambda e, sl=sl: e.tensor_copy(out=gtb[:, sl], in_=big0[:, sl]))
    P.inc("vector", sV)

    if "gatedT" in tap_aps:
        P.wait("gpsimd", sV)
        P.dma("gpsimd", tap_aps["gatedT"], big0[:], sD)
        P.dma("gpsimd", tap_aps["ysl"], big1[:], sD)
    P.wait("tensor", sV)
    for g in range(NT):
        P.wait("sync", sV)
        P.wait("sync", sA)
        P.dma("sync", tkA[:], x_in[g * 128:(g + 1) * 128, :], sLD)
        if g > 0:
            P.wait("tensor", sV)
            P.wait("tensor", sA)
        for kc in range(4):
            for q in range(2):
                P.do("tensor", lambda e, g=g, kc=kc, q=q: e.matmul(
                    PS0[:, q * 512:(q + 1) * 512],
                    gtb[:, kc * S + g * 128: kc * S + (g + 1) * 128],
                    wdn[:, kc * H + q * 512: kc * H + (q + 1) * 512],
                    start=(kc == 0), stop=(kc == 3)))
        P.inc("tensor", sT)
        P.wait("vector", sT)
        P.do("vector", lambda e: e.scalar_tensor_tensor(
            out=tkC[:], in0=PS0[:], scalar=1.0, in1=bdnT[:], op0=ALU.mult, op1=ALU.add))
        P.wait("vector", sLD)
        P.wait("scalar", sLD)
        ln_stats(tkA[:])
        ln_apply(tkA[:], tkB[:])
        P.do("vector", lambda e: e.tensor_tensor(out=tkC[:], in0=tkC[:], in1=tkB[:], op=ALU.add))
        ln_stats(tkC[:])
        ln_apply(tkC[:], tkB[:])
        P.do("vector", lambda e: e.tensor_tensor(out=tkA[:], in0=tkA[:], in1=tkB[:], op=ALU.add))
        P.inc("vector", sV)
        P.wait("gpsimd", sV)
        P.dma("gpsimd", x1_scr[g * 128:(g + 1) * 128, :], tkA[:], sD)
        ln_stats(tkA[:])
        ln_apply(tkA[:], tkB[:])
        P.inc("vector", sV)
        P.wait("scalar", sV)
        P.do("scalar", lambda e: e.copy(out=t1bt[:], in_=tkB[:]))
        P.inc("scalar", sA)
        P.wait("gpsimd", sA)
        P.dma("gpsimd", t1_loc[g * 128:(g + 1) * 128, :], t1bt[:], sD)
        P.wait("tensor", sV)
        for hc in range(8):
            pslot = PS2 if hc % 2 == 0 else PS3
            P.do("tensor", lambda e, hc=hc, p=pslot: e.transpose(
                p[:, 0:128], tkB[:, hc * 128:(hc + 1) * 128], identf[:]))
            P.inc("tensor", sT)
            P.wait("scalar", sT)
            P.do("scalar", lambda e, hc=hc, p=pslot: e.copy(
                out=t1Tg[:, hc * 128:(hc + 1) * 128], in_=p[:, 0:128]))
            P.inc("scalar", sA)
            P.wait("tensor", sA)
        for hc in range(8):
            P.do("tensor", lambda e, hc=hc: e.matmul(
                PS1[:, 0:64], t1Tg[:, hc * 128:(hc + 1) * 128],
                rwf[:, hc * E:(hc + 1) * E], start=(hc == 0), stop=(hc == 7)))
        P.inc("tensor", sT)
        C = tkC
        P.wait("scalar", sT)
        P.do("scalar", lambda e: e.activation(out=C[:, 0:64], in_=PS1[:, 0:64], func=AF.Sigmoid))
        P.inc("scalar", sA)
        P.wait("vector", sA)
        P.do("vector", lambda e: e.tensor_reduce(
            out=C[:, 64:72], in_=C[:, 0:64].rearrange("p (g e) -> p g e", e=8),
            axis=AX.X, op=ALU.max))
        P.do("vector", lambda e: e.max(out=C[:, 72:80], in_=C[:, 64:72]))
        P.do("vector", lambda e: e.tensor_scalar(
            out=C[:, 80:88], in0=C[:, 64:72], scalar1=C[:, 75:76], scalar2=None, op0=ALU.is_ge))
        P.do("vector", lambda e: _bctt(e,
            C[:, 88:152].rearrange("p (g e) -> p g e", e=8),
            C[:, 0:64].rearrange("p (g e) -> p g e", e=8),
            C[:, 80:88].rearrange("p (g o) -> p g o", o=1),
            ALU.mult))
        P.do("vector", lambda e: e.max(out=C[:, 152:160], in_=C[:, 88:152]))
        P.do("vector", lambda e: e.max_index(out=ixt[:], in_max=C[:, 152:160], in_values=C[:, 88:152]))
        P.do("vector", lambda e: e.tensor_copy(out=C[:, 170:178], in_=ixt[:]))
        P.do("vector", lambda e: e.tensor_reduce(out=C[:, 160:161], in_=C[:, 152:160], axis=AX.X, op=ALU.add))
        P.do("vector", lambda e: e.tensor_scalar(
            out=C[:, 160:161], in0=C[:, 160:161], scalar1=1e-9, scalar2=None, op0=ALU.max))
        P.do("vector", lambda e: e.reciprocal(out=C[:, 161:162], in_=C[:, 160:161]))
        P.do("vector", lambda e: e.tensor_scalar(
            out=C[:, 162:170], in0=C[:, 152:160], scalar1=C[:, 161:162], scalar2=MOE_SCALE,
            op0=ALU.mult, op1=ALU.mult))
        B_ = tkB
        P.do("vector", lambda e: _bctt(e,
            B_[:, 0:512].rearrange("p (x k) -> p x k", k=8),
            C[:, 170:178].rearrange("p (o k) -> p o k", o=1),
            iotaet[:].rearrange("p (x o) -> p x o", o=1),
            ALU.is_equal))
        P.do("vector", lambda e: e.tensor_reduce(
            out=B_[:, 512:576], in_=B_[:, 0:512].rearrange("p (x k) -> p x k", k=8),
            axis=AX.X, op=ALU.add))
        P.do("vector", lambda e: e.tensor_copy(out=ohb[:], in_=B_[:, 512:576]))
        P.do("vector", lambda e, g=g: e.tensor_copy(out=wvb[:, g * 8:(g + 1) * 8], in_=C[:, 162:170]))
        P.do("vector", lambda e, g=g: e.tensor_copy(out=ixb[:, g * 8:(g + 1) * 8], in_=C[:, 170:178]))
        P.inc("vector", sV)
        P.wait("tensor", sV)
        P.do("tensor", lambda e: e.matmul(PS0[:, 0:64], trit[:], ohb[:], start=True, stop=False))
        P.do("tensor", lambda e: e.matmul(PS0[:, 0:64], onest[0:1, :], crow[0:1, :], start=False, stop=True))
        P.do("tensor", lambda e: e.matmul(PS0[0:1, 64:128], stript[:, 127:128], ohb[:], start=True, stop=True))
        P.inc("tensor", sT)
        P.wait("vector", sT)
        P.do("vector", lambda e: _bctt(e,
            B_[:, 0:512].rearrange("p (k x) -> p k x", x=64),
            C[:, 170:178].rearrange("p (k o) -> p k o", o=1),
            iotaet[:].rearrange("p (o x) -> p o x", o=1),
            ALU.is_equal))
        P.do("vector", lambda e: e.tensor_copy(out=B_[:, 576:640], in_=PS0[:, 0:64]))
        P.do("vector", lambda e: _bctt(e,
            C[:, 256:768].rearrange("p (k x) -> p k x", x=64),
            B_[:, 0:512].rearrange("p (k x) -> p k x", x=64),
            B_[:, 576:640].rearrange("p (o x) -> p o x", o=1),
            ALU.mult))
        P.do("vector", lambda e, g=g: e.tensor_reduce(
            out=posb[:, g * 8:(g + 1) * 8],
            in_=C[:, 256:768].rearrange("p (k x) -> p k x", x=64),
            axis=AX.X, op=ALU.add))
        P.do("vector", lambda e: e.tensor_tensor(
            out=crow[0:1, :], in0=crow[0:1, :], in1=PS0[0:1, 64:128], op=ALU.add))
        P.inc("vector", sV)
        P.wait("tensor", sV)

    P.wait("gpsimd", sV)
    P.wait("gpsimd", sD)
    P.cc("AllGather", ALU.bypass, [t1_loc.opt()], [ag_t1.opt()], sCC)
    P.dma("gpsimd", cnt_loc, crow[0:1, :], sD)
    P.wait("gpsimd", sD)
    P.cc("AllGather", ALU.bypass, [cnt_loc.opt()], [ag_cnt.opt()], sCC)
    P.wait("sync", sCC)
    P.dma("sync", agcS[:], ag_cnt, sLD)
    P.wait("tensor", sLD)
    P.wait("tensor", sV)
    P.wait("tensor", sA)
    P.do("tensor", lambda e: e.matmul(PS0[0:1, 0:64], rkmS[:], agcS[:], start=True, stop=True))
    P.inc("tensor", sT)
    P.wait("scalar", sT)
    P.do("scalar", lambda e: e.copy(out=crow[0:1, :], in_=PS0[0:1, 0:64]))
    P.inc("scalar", sA)
    P.wait("tensor", sA)
    P.do("tensor", lambda e: e.matmul(PS1[:, 0:64], onest[0:1, :], crow[0:1, :], start=True, stop=True))
    P.inc("tensor", sT)
    P.wait("scalar", sT)
    P.do("scalar", lambda e: e.copy(out=brow[:, 0:64], in_=PS1[:, 0:64]))
    P.inc("scalar", sA)

    P.wait("vector", sA)
    for g in range(NT):
        B_ = tkB
        C = tkC
        gs8 = slice(g * 8, (g + 1) * 8)
        P.do("vector", lambda e, gs8=gs8: _bctt(e,
            B_[:, 0:512].rearrange("p (k x) -> p k x", x=64),
            ixb[:, gs8].rearrange("p (k o) -> p k o", o=1),
            iotaet[:].rearrange("p (o x) -> p o x", o=1),
            ALU.is_equal))
        P.do("vector", lambda e: _bctt(e,
            C[:, 0:512].rearrange("p (k x) -> p k x", x=64),
            B_[:, 0:512].rearrange("p (k x) -> p k x", x=64),
            brow[:, 0:64].rearrange("p (o x) -> p o x", o=1),
            ALU.mult))
        P.do("vector", lambda e: e.tensor_reduce(
            out=C[:, 512:520], in_=C[:, 0:512].rearrange("p (k x) -> p k x", x=64),
            axis=AX.X, op=ALU.add))
        P.do("vector", lambda e, gs8=gs8: e.tensor_tensor(
            out=C[:, 520:528], in0=posb[:, gs8], in1=C[:, 512:520], op=ALU.add))
        P.do("vector", lambda e: e.tensor_scalar(
            out=C[:, 528:536], in0=C[:, 520:528], scalar1=float(CAP), scalar2=None, op0=ALU.is_lt))
        P.do("vector", lambda e, gs8=gs8: e.tensor_tensor(
            out=C[:, 536:544], in0=wvb[:, gs8], in1=C[:, 528:536], op=ALU.mult))
        P.do("vector", lambda e: e.tensor_scalar(
            out=C[:, 544:552], in0=C[:, 520:528], scalar1=float(CAP - 1), scalar2=None, op0=ALU.min))
        P.do("vector", lambda e, gs8=gs8: e.scalar_tensor_tensor(
            out=C[:, 552:560], in0=ixb[:, gs8], scalar=float(CAP), in1=C[:, 544:552],
            op0=ALU.mult, op1=ALU.add))
        P.do("vector", lambda e: e.tensor_scalar(
            out=C[:, 560:568], in0=C[:, 552:560], scalar1=1.0, scalar2=None, op0=ALU.add))
        P.do("vector", lambda e: e.tensor_tensor(
            out=C[:, 560:568], in0=C[:, 560:568], in1=C[:, 528:536], op=ALU.mult))
        P.do("vector", lambda e: e.tensor_scalar(
            out=C[:, 560:568], in0=C[:, 560:568], scalar1=-1.0, scalar2=None, op0=ALU.add))
        P.do("vector", lambda e, g=g: e.tensor_scalar(
            out=tokcol[:, 0:1], in0=tok0S[:, 0:1], scalar1=float(g * 128 + 1), scalar2=None, op0=ALU.add))
        P.do("vector", lambda e: e.tensor_scalar(
            out=C[:, 568:576], in0=C[:, 552:560], scalar1=0.0, scalar2=tokcol[:, 0:1],
            op0=ALU.mult, op1=ALU.add))
        P.do("vector", lambda e: e.tensor_copy(
            out=ptile[:].rearrange("p (k c) -> p k c", c=2)[:, :, 0:1],
            in_=C[:, 568:576].rearrange("p (k o) -> p k o", o=1)))
        P.do("vector", lambda e: e.tensor_copy(
            out=ptile[:].rearrange("p (k c) -> p k c", c=2)[:, :, 1:2],
            in_=C[:, 536:544].rearrange("p (k o) -> p k o", o=1)))
        P.inc("vector", sV)
        P.wait("gpsimd", sV)
        gsl = slice(g * 1024, (g + 1) * 1024)
        P.dma("gpsimd", rec_islot[0, gsl].rearrange("(t k) -> t k", k=8), C[:, 560:568], sD)
        P.dma("gpsimd", rec_slotc[0, gsl].rearrange("(t k) -> t k", k=8), C[:, 552:560], sD)
        P.dma("gpsimd", rec_wv[0, gsl].rearrange("(t k) -> t k", k=8), C[:, 536:544], sD)
        P.dma("gpsimd", rec_pairs[gsl, :].rearrange("(t k) c -> t (k c)", k=128), ptile[:], sD)
        P.wait("vector", sD)

    if "recs" in tap_aps:
        P.wait("gpsimd", sD)
        P.dma("gpsimd", tap_aps["recs"], rec_islot, sD)
        P.dma("gpsimd", tap_aps["recw"], rec_wv, sD)
        P.dma("gpsimd", tap_aps["recc"], rec_slotc, sD)
    if "x1" in tap_aps:
        P.wait("gpsimd", sD)
        P.dma("gpsimd", tap_aps["x1"], x1_scr, sD)
    if "t1" in tap_aps:
        P.wait("gpsimd", sD)
        P.dma("gpsimd", tap_aps["t1"], t1_loc, sD)

    if upto == "router":
        P.wait("gpsimd", sD)
        stack.close()
        _emit(nc, P)
        return nc

    P.fullbar(sOUT)
    sbfree("tok0S", "tokcol", "ptile", "t1bt",
           "ixt", "rkmS", "agcS", "brow", "crow", "ohb", "wvb", "ixb",
           "posb", "rwf", "t1Tg", "gtb", "bdnT", "wdn", "big2s")
    isrc = sbuf("isrc", [128, 128])
    idx32 = sbuf("idx32", [128, 128], mybir.dt.int32)
    pairsT = sbuf("pairsT", [128, 256])
    P.wait("vector", sA)
    P.do("vector", lambda e: e.memset(tkC[:, 0:520], 0.0))
    P.inc("vector", sV)
    P.wait("gpsimd", sV)
    P.wait("gpsimd", sD)
    P.dma("gpsimd", T_loc.rearrange("(p a) c -> p (a c)", p=128), tkC[:, 0:516], sD)
    P.wait("sync", sD)
    P.dma("sync", isrc[:], rec_islot[0, :].rearrange("(p j) -> p j", p=128), sLD)
    P.dma("sync", pairsT[:], rec_pairs[:, :].rearrange("(p j) c -> p (j c)", p=128), sLD)
    P.wait("vector", sLD)
    P.do("vector", lambda e: e.tensor_scalar(
        out=tkA[:, 0:128], in0=isrc[:], scalar1=0.0, scalar2=None, op0=ALU.is_lt))
    P.do("vector", lambda e: e.scalar_tensor_tensor(
        out=tkA[:, 0:128], in0=tkA[:, 0:128], scalar=float(E * CAP + 100 + 1), in1=isrc[:],
        op0=ALU.mult, op1=ALU.add))
    P.do("vector", lambda e: e.tensor_copy(out=idx32[:], in_=tkA[:, 0:128]))
    P.inc("vector", sV)
    P.wait("gpsimd", sV)
    P.wait("gpsimd", sD)
    for j in range(128):
        sGA.n += 16

        def fsc(e, j=j):
            return e.indirect_dma_start(
                out=T_loc, out_offset=IndirectOffsetOnAxis(ap=idx32[:, j:j + 1], axis=0),
                in_=pairsT[:, 2 * j:2 * j + 2], in_offset=None,
                compute_op=ALU.add,
            ).then_inc(sGA.h, 16)
        P.do("gpsimd", fsc)
    P.wait("gpsimd", sGA)
    P.cc("AllReduce", ALU.add, [T_loc.opt()], [T_glob.opt()], sCC)
    P.wait("gpsimd", sCC)
    if "Tglob" in tap_aps:
        P.dma("gpsimd", tap_aps["Tglob"], T_glob, sD)

    if upto == "table":
        P.wait("gpsimd", sD)
        stack.close()
        _emit(nc, P)
        return nc

    wge = sbuf("wge", [128, 8 * DI], BF16)
    wue = sbuf("wue", [128, 8 * DI], BF16)
    wde = sbuf("wde", [128, 4 * H], BF16)
    dispT = sbuf("dispT", [128, 8 * CAP], BF16)
    gub = sbuf("gub", [128, 4 * CAP], BF16)
    twt = sbuf("twt", [128, 8])
    eob = sbuf("eob", [128, H], BF16)
    exidx = sbuf("exidx", [128, 1], mybir.dt.int32)
    exf = sbuf("exf", [128, 1])
    dtok = sbuf("dtok", [128, H], BF16)

    for e_i in range(EPC):
        for c4 in range(8):
            wconv(wge, c4 * DI, gate_w[e_i * H + c4 * 128: e_i * H + (c4 + 1) * 128, :], DI)
            wconv(wue, c4 * DI, up_w[e_i * H + c4 * 128: e_i * H + (c4 + 1) * 128, :], DI)
        for c4 in range(4):
            wconv(wde, c4 * H, down_w[e_i * DI + c4 * 128: e_i * DI + (c4 + 1) * 128, :], H)
        for sc in range(4):
            P.wait("vector", sGA)
            P.wait("vector", sD)
            P.do("vector", lambda e, e_i=e_i, sc=sc: e.tensor_scalar(
                out=exf[:, 0:1], in0=iot128[:, 0:1], scalar1=float(128 * sc), scalar2=None, op0=ALU.add))
            P.do("vector", lambda e, e_i=e_i: e.tensor_tensor(
                out=exf[:, 0:1], in0=exf[:, 0:1], in1=ebaset[:, e_i:e_i + 1], op=ALU.add))
            P.do("vector", lambda e: e.tensor_copy(out=exidx[:], in_=exf[:, 0:1]))
            P.inc("vector", sV)
            P.wait("gpsimd", sV)
            igather("gpsimd", twt[:, 2 * sc:2 * sc + 2], T_glob, exidx[:, 0:1], sGA)
            P.wait("vector", sGA)
            P.do("vector", lambda e, sc=sc: e.tensor_scalar(
                out=exf[:, 0:1], in0=twt[:, 2 * sc:2 * sc + 1], scalar1=-1.0, scalar2=0.0,
                op0=ALU.add, op1=ALU.max))
            P.do("vector", lambda e: e.tensor_copy(out=exidx[:], in_=exf[:, 0:1]))
            P.inc("vector", sV)
            P.wait("gpsimd", sV)
            P.wait("gpsimd", sT)
            igather("gpsimd", dtok[:], ag_t1, exidx[:, 0:1], sGA)
            P.wait("tensor", sGA)
            P.wait("tensor", sA)
            for hc in range(8):
                pslot = PS2 if hc % 2 == 0 else PS3
                P.do("tensor", lambda e, hc=hc, p=pslot: e.transpose(
                    p.ap().bitcast(BF16)[:, 0:128],
                    dtok[:, hc * 128:(hc + 1) * 128], identt[:]))
                P.inc("tensor", sT)
                P.wait("scalar", sT)
                P.do("scalar", lambda e, hc=hc, sc=sc, p=pslot: e.copy(
                    out=dispT[:, hc * CAP + sc * 128: hc * CAP + (sc + 1) * 128],
                    in_=p.ap().bitcast(BF16)[:, 0:128]))
                P.inc("scalar", sA)
                P.wait("tensor", sA)
        for ic in range(4):
            P.wait("tensor", sA)
            P.wait("tensor", sV)
            for hc in range(8):
                P.do("tensor", lambda e, ic=ic, hc=hc: e.matmul(
                    PS0[:, 0:512], wge[:, hc * DI + ic * 128: hc * DI + (ic + 1) * 128],
                    dispT[:, hc * CAP:(hc + 1) * CAP], start=(hc == 0), stop=(hc == 7)))
            for hc in range(8):
                P.do("tensor", lambda e, ic=ic, hc=hc: e.matmul(
                    PS1[:, 0:512], wue[:, hc * DI + ic * 128: hc * DI + (ic + 1) * 128],
                    dispT[:, hc * CAP:(hc + 1) * CAP], start=(hc == 0), stop=(hc == 7)))
            P.inc("tensor", sT)
            P.wait("scalar", sT)
            P.do("scalar", lambda e: e.activation(out=tkA[:, 0:512], in_=PS0[:, 0:512], func=AF.Silu))
            P.inc("scalar", sA)
            P.wait("vector", sA)
            P.do("vector", lambda e: e.tensor_tensor(out=tkB[:, 0:512], in0=tkA[:, 0:512], in1=PS1[:, 0:512], op=ALU.mult))
            P.do("vector", lambda e, ic=ic: e.tensor_copy(out=gub[:, ic * CAP:(ic + 1) * CAP], in_=tkB[:, 0:512]))
            P.inc("vector", sV)
            P.wait("tensor", sV)
        for sc in range(4):
            P.wait("tensor", sA)
            for ic in range(4):
                for q in range(2):
                    P.do("tensor", lambda e, sc=sc, ic=ic, q=q: e.matmul(
                        PS2[:, q * 512:(q + 1) * 512],
                        gub[:, ic * CAP + sc * 128: ic * CAP + (sc + 1) * 128],
                        wde[:, ic * H + q * 512: ic * H + (q + 1) * 512],
                        start=(ic == 0), stop=(ic == 3)))
            P.inc("tensor", sT)
            P.wait("scalar", sT)
            P.wait("scalar", sGA)
            P.do("scalar", lambda e: e.copy(out=eob[:], in_=PS2[:, 0:1024]))
            P.inc("scalar", sA)
            P.wait("gpsimd", sA)
            P.dma("gpsimd", eo_loc[e_i * CAP + sc * 128: e_i * CAP + (sc + 1) * 128, :], eob[:], sD)
            P.wait("tensor", sA)
        P.wait("sync", sA)
    P.wait("gpsimd", sD)
    P.cc("AllGather", ALU.bypass, [eo_loc.opt()], [ag_eo.opt()], sCC)
    P.wait("gpsimd", sCC)
    if "eo" in tap_aps:
        P.dma("gpsimd", tap_aps["eo"], eo_loc, sD)
    if upto == "expert":
        P.wait("gpsimd", sD)
        stack.close()
        _emit(nc, P)
        return nc

    P.fullbar(sOUT)
    sbfree("dtok", "exf", "exidx", "eob", "twt", "gub", "dispT", "wde", "wue", "wge",
           "pairsT", "idx32", "isrc")
    sct = sbuf("sct", [128, 128])
    sci = sbuf("sci", [128, 128], mybir.dt.int32)
    wvT = sbuf("wvT", [128, 128])
    ctr = sbuf("ctr", [128, H])
    cgt = sbuf("cgt", [128, H], BF16)
    wj = sbuf("wj", [128, 128], BF16)
    wshg = sbuf("wshg", [128, 8 * DI], BF16)
    wshu = sbuf("wshu", [128, 8 * DI], BF16)
    wsd = sbuf("wsd", [128, 4 * H], BF16)
    t1hb = sbuf("t1hb", [128, H], BF16)
    sgb = sbuf("sgb", [128, 4 * 128], BF16)

    for c4 in range(8):
        wconv(wshg, c4 * DI, sh_g[c4 * 128:(c4 + 1) * 128, :], DI)
        wconv(wshu, c4 * DI, sh_u[c4 * 128:(c4 + 1) * 128, :], DI)
    for c4 in range(4):
        wconv(wsd, c4 * H, sh_d[c4 * 128:(c4 + 1) * 128, :], H)
    P.wait("sync", sD)
    P.dma("sync", sct[:], rec_slotc[0, :].rearrange("(p j) -> p j", p=128), sLD)
    P.dma("sync", wvT[:], rec_wv[0, :].rearrange("(p j) -> p j", p=128), sLD)
    P.wait("tensor", sLD)
    P.wait("tensor", sA)
    P.do("tensor", lambda e: e.transpose(PS3[:, 0:128], sct[:], identf[:]))
    P.inc("tensor", sT)
    P.wait("scalar", sT)
    P.do("scalar", lambda e: e.copy(out=sct[:], in_=PS3[:, 0:128]))
    P.inc("scalar", sA)
    P.wait("tensor", sA)
    P.do("tensor", lambda e: e.transpose(PS3[:, 0:128], wvT[:], identf[:]))
    P.inc("tensor", sT)
    P.wait("scalar", sT)
    P.do("scalar", lambda e: e.copy(out=wvT[:], in_=PS3[:, 0:128]))
    P.inc("scalar", sA)
    P.wait("vector", sA)
    P.do("vector", lambda e: e.tensor_copy(out=sci[:], in_=sct[:]))
    P.inc("vector", sV)

    for g in range(NT):
        for jj in range(8):
            j = g * 8 + jj
            P.wait("gpsimd", sV)
            P.wait("gpsimd", sA)
            P.wait("gpsimd", sT)
            igather("gpsimd", cgt[:], ag_eo, sci[:, j:j + 1], sGA)
            P.wait("vector", sGA)
            P.do("vector", lambda e: e.memset(wj[:], 0.0))
            P.do("vector", lambda e, j=j, jj=jj: e.tensor_scalar(
                out=wj[:, 16 * jj:16 * (jj + 1)], in0=bmaskt[:],
                scalar1=wvT[:, j:j + 1], scalar2=None, op0=ALU.mult))
            P.inc("vector", sV)
            P.wait("tensor", sV)
            P.wait("tensor", sGA)
            if jj == 0:
                P.wait("tensor", sA)
            for q in range(2):
                P.do("tensor", lambda e, q=q, jj=jj: e.matmul(
                    PS0[:, q * 512:(q + 1) * 512], wj[:],
                    cgt[:, q * 512:(q + 1) * 512], start=(jj == 0), stop=(jj == 7)))
            P.inc("tensor", sT)
        P.wait("scalar", sT)
        P.do("scalar", lambda e: e.copy(out=ctr[:], in_=PS0[:]))
        P.inc("scalar", sA)
        P.wait("sync", sA)
        P.wait("sync", sV)
        P.wait("sync", sT)
        P.dma("sync", cgt[:], t1_loc[g * 128:(g + 1) * 128, :], sLD)
        P.wait("tensor", sLD)
        for hc in range(8):
            pslot = PS2 if hc % 2 == 0 else PS3
            P.do("tensor", lambda e, hc=hc, p=pslot: e.transpose(
                p.ap().bitcast(BF16)[:, 0:128], cgt[:, hc * 128:(hc + 1) * 128], identt[:]))
            P.inc("tensor", sT)
            P.wait("scalar", sT)
            P.do("scalar", lambda e, hc=hc, p=pslot: e.copy(
                out=t1hb[:, hc * 128:(hc + 1) * 128], in_=p.ap().bitcast(BF16)[:, 0:128]))
            P.inc("scalar", sA)
            P.wait("tensor", sA)
        for ic in range(4):
            for hc in range(8):
                P.do("tensor", lambda e, ic=ic, hc=hc: e.matmul(
                    PS2[:, 0:128], wshg[:, hc * DI + ic * 128: hc * DI + (ic + 1) * 128],
                    t1hb[:, hc * 128:(hc + 1) * 128], start=(hc == 0), stop=(hc == 7)))
            for hc in range(8):
                P.do("tensor", lambda e, ic=ic, hc=hc: e.matmul(
                    PS3[:, 0:128], wshu[:, hc * DI + ic * 128: hc * DI + (ic + 1) * 128],
                    t1hb[:, hc * 128:(hc + 1) * 128], start=(hc == 0), stop=(hc == 7)))
            P.inc("tensor", sT)
            P.wait("scalar", sT)
            P.do("scalar", lambda e: e.activation(out=tkA[:, 0:128], in_=PS2[:, 0:128], func=AF.Silu))
            P.inc("scalar", sA)
            P.wait("vector", sA)
            P.do("vector", lambda e: e.tensor_tensor(out=tkA[:, 128:256], in0=tkA[:, 0:128], in1=PS3[:, 0:128], op=ALU.mult))
            P.do("vector", lambda e, ic=ic: e.tensor_copy(out=sgb[:, ic * 128:(ic + 1) * 128], in_=tkA[:, 128:256]))
            P.inc("vector", sV)
            P.wait("tensor", sV)
        for ic in range(4):
            for q in range(2):
                P.do("tensor", lambda e, ic=ic, q=q: e.matmul(
                    PS1[:, q * 512:(q + 1) * 512], sgb[:, ic * 128:(ic + 1) * 128],
                    wsd[:, ic * H + q * 512: ic * H + (q + 1) * 512],
                    start=(ic == 0), stop=(ic == 3)))
        P.inc("tensor", sT)
        P.wait("sync", sV)
        P.wait("sync", sD)
        P.dma("sync", tkA[:], x1_scr[g * 128:(g + 1) * 128, :], sLD)
        P.wait("vector", sT)
        P.do("vector", lambda e: e.tensor_tensor(out=tkC[:], in0=ctr[:], in1=PS1[:], op=ALU.add))
        ln_stats(tkC[:])
        ln_apply(tkC[:], tkB[:])
        P.wait("vector", sLD)
        P.do("vector", lambda e: e.tensor_tensor(out=tkC[:], in0=tkA[:], in1=tkB[:], op=ALU.add))
        P.inc("vector", sV)
        P.wait("gpsimd", sV)
        P.dma("gpsimd", out[g * 128:(g + 1) * 128, :], tkC[:], sFIN)
        P.wait("tensor", sV)
        P.wait("scalar", sLD)

    P.wait("gpsimd", sFIN)
    stack.close()
    _emit(nc, P)
    return nc


def _emit(nc, P):
    with nc.Block() as block:
        @block.sync
        def _(eng):
            for f in P.ops["sync"]:
                f(eng)

        @block.scalar
        def _(eng):
            for f in P.ops["scalar"]:
                f(eng)

        @block.vector
        def _(eng):
            for f in P.ops["vector"]:
                f(eng)

        @block.tensor
        def _(eng):
            for f in P.ops["tensor"]:
                f(eng)

        @block.gpsimd
        def _(eng):
            for f in P.ops["gpsimd"]:
                f(eng)


def _consts():
    ident = np.eye(128, dtype=bfdt)
    strip = np.zeros((128, 255), dtype=bfdt)
    strip[:, 127] = 1
    tri = np.triu(np.ones((128, 128), np.float32), 1).astype(bfdt)
    ones = np.ones((1, 128), np.float32)
    iotae = np.tile(np.arange(E, dtype=np.float32)[None, :], (128, 1))
    bmask = np.zeros((128, 16), np.float32)
    for r in range(128):
        bmask[r, r // 8] = 1
    return ident, strip, tri, ones, iotae, bmask.astype(bfdt)


def make_in_maps(inputs):
    f32 = np.float32
    x = np.ascontiguousarray(np.asarray(inputs["L0_attn_x"], f32).reshape(B * L, H))
    ident, strip, tri, ones, iotae, bmask = _consts()
    al = np.asarray(inputs["L0_attn_A_log"], f32)
    in_maps = []
    for core in range(NCORES):
        t0c = core * S
        halo = np.zeros((2, H), f32)
        if core % 4 != 0:
            halo[:] = x[t0c - 2:t0c]
        dblk = core % 4
        m = {
            "x_slice": np.ascontiguousarray(x[t0c:t0c + S]),
            "x_halo": halo,
            "ln_gamma": np.asarray(inputs["L0_attn_ln_gamma"], f32).reshape(1, H),
            "ln_beta": np.asarray(inputs["L0_attn_ln_beta"], f32).reshape(1, H),
            "w_up": np.ascontiguousarray(np.asarray(inputs["L0_attn_W_up"], f32)),
            "b_up": np.asarray(inputs["L0_attn_b_up"], f32).reshape(1, H),
            "conv_w": np.ascontiguousarray(np.asarray(inputs["L0_attn_conv_w"], f32).reshape(DI, 3)),
            "conv_b": np.asarray(inputs["L0_attn_conv_b"], f32).reshape(DI, 1),
            "w_delta": np.ascontiguousarray(np.asarray(inputs["L0_attn_W_delta"], f32)),
            "b_delta": np.asarray(inputs["L0_attn_b_delta"], f32).reshape(DI, 1),
            "w_B": np.ascontiguousarray(np.asarray(inputs["L0_attn_W_B"], f32)),
            "w_C": np.ascontiguousarray(np.asarray(inputs["L0_attn_W_C"], f32)),
            "a_log_blk": np.ascontiguousarray(al[dblk * 128:(dblk + 1) * 128]),
            "d_vec": np.asarray(inputs["L0_attn_D"], f32).reshape(DI, 1),
            "w_down": np.ascontiguousarray(np.asarray(inputs["L0_attn_W_down"], f32)),
            "b_down": np.asarray(inputs["L0_attn_b_down"], f32).reshape(1, H),
            "router_w": np.ascontiguousarray(np.asarray(inputs["L0_ffn_router_w"], f32)),
            "gate_w_blk": np.ascontiguousarray(
                np.asarray(inputs["L0_ffn_gate_w"], f32)[core * EPC:(core + 1) * EPC].reshape(EPC * H, DI)),
            "up_w_blk": np.ascontiguousarray(
                np.asarray(inputs["L0_ffn_up_w"], f32)[core * EPC:(core + 1) * EPC].reshape(EPC * H, DI)),
            "down_w_blk": np.ascontiguousarray(
                np.asarray(inputs["L0_ffn_down_w"], f32)[core * EPC:(core + 1) * EPC].reshape(EPC * DI, H)),
            "sh_gate": np.ascontiguousarray(np.asarray(inputs["L0_ffn_shared_gate_w"], f32)),
            "sh_up": np.ascontiguousarray(np.asarray(inputs["L0_ffn_shared_up_w"], f32)),
            "sh_down": np.ascontiguousarray(np.asarray(inputs["L0_ffn_shared_down_w"], f32)),
            "c_ident": ident, "c_strip": strip, "c_tri": tri, "c_ones": ones,
            "c_iotae": iotae, "c_bmask": bmask,
        }
        m["halo_mask"] = np.full((128, 1), 0.0 if core % 4 == 0 else 1.0, f32)
        rm = np.zeros((NCORES, 1), f32)
        rm[:core] = 1.0
        m["rank_mask"] = rm
        m["tok0"] = (core * S + np.arange(128, dtype=f32))[:, None].copy()

        bat = core // 4
        gf = []
        for rr in range(4):
            for arr in range(2):
                base = (4 * bat + rr) * 2 * DI + arr * DI + dblk * 128
                gf.append(base + np.arange(128))
        m["gidx_f32"] = np.stack(gf, axis=1).astype(np.int32)
        gb = []
        for rr in range(4):
            for tau in range(2):
                for arr in range(2):
                    base = (4 * bat + rr) * 2 * NS + arr * NS + tau * 128
                    gb.append(base + np.arange(128))
        m["gidx_bf"] = np.stack(gb, axis=1).astype(np.int32)
        gy = []
        for rr in range(4):
            rows = ((4 * bat + rr) * 128 + np.arange(128)) * 4 + (core % 4)
            gy.append(rows)
        m["gidx_ys"] = np.stack(gy, axis=1).astype(np.int32)
        m["iota128"] = np.arange(128, dtype=f32)[:, None].copy()
        m["ebase"] = np.tile((512.0 * (core * EPC + np.arange(8))).astype(f32)[None, :], (128, 1))
        in_maps.append(m)
    return in_maps


def kernel(**inputs):
    in_maps = make_in_maps(inputs)
    nc = build_nc(0)
    res = run_bass_kernel_spmd(nc, in_maps, core_ids=list(range(NCORES)))
    outs = [res.results[c]["out"] for c in range(NCORES)]
    return np.concatenate(outs, 0).reshape(B, L, H)


def run_partial(inputs, upto, taps=()):
    in_maps = make_in_maps(inputs)
    nc = build_nc(0, taps=taps, upto=upto)
    res = run_bass_kernel_spmd(nc, in_maps, core_ids=list(range(NCORES)))
    return res
